# version 07
# baseline (speedup 1.0000x reference)
import numpy as np
import ml_dtypes
from contextlib import ExitStack

import concourse.bass as bass
import concourse.mybir as mybir
from concourse.bass_utils import run_bass_kernel_spmd

F32 = mybir.dt.float32
BF16 = mybir.dt.bfloat16
AF = mybir.ActivationFunctionType
ALU = mybir.AluOpType
AX = mybir.AxisListType

D = 2048
T = 2048
NCORES = 8


ENGINES = ("pe", "dve", "act", "pool", "sp")
DMA_QUEUES = ("sp", "act", "pool")


class Buf:
    __slots__ = ("name", "last_w", "readers")

    def __init__(self, name="b"):
        self.name = name
        self.last_w = None
        self.readers = []


class Step:
    __slots__ = ("eng", "fn", "deps", "need_inc", "inc_cum", "is_dma", "sem",
                 "dma_cum", "pos", "tag", "epoch", "join_totals")

    def __init__(self, eng, fn, is_dma=False, sem=None, tag=""):
        self.eng = eng
        self.fn = fn
        self.deps = []
        self.need_inc = False
        self.inc_cum = 0
        self.is_dma = is_dma
        self.sem = sem
        self.dma_cum = 0
        self.pos = 0
        self.tag = tag
        self.epoch = 0
        self.join_totals = None


class Prog:
    def __init__(self):
        self.steps = {e: [] for e in ENGINES}
        self.dma_sems = {}
        self.dma_sem_names = []
        self.init_dep = None
        self._touched = set()
        self.epoch = 0
        self.epoch_snapshots = []
        self._stream_sems = {}
        self._sem_pool = []
        self._last_dma = {}

    def _stream_sem(self, key):
        if key in self._stream_sems:
            return self._stream_sems[key]
        if self._sem_pool:
            nm = self._sem_pool.pop()
        else:
            nm = self.new_dma_sem(f"a{len(self.dma_sem_names)}")
        self._stream_sems[key] = nm
        return nm

    def join_barrier(self, eng="pool"):
        bufs = tuple(self._touched)
        s = self.step(eng, (lambda e: e.engine_nop()), writes=bufs,
                      tag="joinbar")
        s.join_totals = {n: c for n, c in self.dma_sems.items() if c > 0}
        self._touched = set()
        self.init_dep = s
        self.epoch_snapshots.append(dict(self.dma_sems))
        self._sem_pool.extend(self._stream_sems.values())
        self._stream_sems = {}
        self.epoch += 1
        return s

    def new_dma_sem(self, name):
        assert name not in self.dma_sems
        self.dma_sems[name] = 0
        self.dma_sem_names.append(name)
        return name

    def _mkdeps(self, s, reads, writes, deps):
        dep = set(deps)
        for b in reads:
            if b.last_w is not None:
                dep.add(b.last_w)
        for b in writes:
            if b.last_w is not None:
                dep.add(b.last_w)
            dep.update(b.readers)
        for b in writes:
            if b.last_w is None and self.init_dep is not None:
                dep.add(self.init_dep)
        for b in reads:
            b.readers.append(s)
            self._touched.add(b)
        for b in writes:
            b.last_w = s
            b.readers = []
            self._touched.add(b)
        out = []
        for d in dep:
            if d is s:
                continue
            if (not d.is_dma) and d.eng == "pe" and s.eng == "pe":
                continue
            out.append(d)
        s.deps = out

    def step(self, eng, fn, reads=(), writes=(), deps=(), tag=""):
        s = Step(eng, fn, tag=tag)
        s.epoch = self.epoch
        s.pos = len(self.steps[eng])
        self.steps[eng].append(s)
        self._mkdeps(s, reads, writes, deps)
        return s

    def dma(self, queue, fn, sem, reads=(), writes=(), deps=(), tag="",
            stream=None):
        assert queue in DMA_QUEUES
        if stream is None:
            if writes:
                stream = writes[0].name.split(".")[0]
            elif reads:
                stream = reads[0].name.split(".")[0]
            else:
                stream = tag
        semname = self._stream_sem((sem, stream))
        s = Step(queue, fn, is_dma=True, sem=semname, tag=tag)
        s.epoch = self.epoch
        s.pos = len(self.steps[queue])
        self.steps[queue].append(s)
        self.dma_sems[semname] += 1
        s.dma_cum = self.dma_sems[semname]
        prev = self._last_dma.get(semname)
        deps = tuple(deps) + ((prev,) if prev is not None else ())
        self._last_dma[semname] = s
        self._mkdeps(s, reads, writes, deps)
        return s

    def build(self, nc, engs, prog_sems, dma_sem_handles):
        for e in ENGINES:
            for s in self.steps[e]:
                for d in s.deps:
                    d.need_inc = True
        for e in ENGINES:
            cum = 0
            for s in self.steps[e]:
                if s.is_dma:
                    continue
                if s.need_inc:
                    cum += 1
                    s.inc_cum = cum
        def emit_engine(ename, eng):
            hw = {}
            for s in self.steps[ename]:
                waits = {}
                for d in s.deps:
                    if d.is_dma:
                        key = ("dma", d.sem)
                        if s.join_totals is not None:
                            val = 16 * s.join_totals.get(d.sem, d.dma_cum)
                        elif d.epoch < s.epoch:
                            val = 16 * self.epoch_snapshots[d.epoch].get(
                                d.sem, d.dma_cum)
                        else:
                            val = 16 * d.dma_cum
                    else:
                        key = ("eng", d.eng)
                        val = d.inc_cum
                        assert val > 0, (d.eng, d.tag, "dep without inc")
                    if waits.get(key, 0) < val:
                        waits[key] = val
                for key, val in sorted(waits.items(), key=lambda kv: repr(kv[0])):
                    if hw.get(key, 0) >= val:
                        continue
                    hw[key] = val
                    semh = (dma_sem_handles[key[1]] if key[0] == "dma"
                            else prog_sems[key[1]])
                    eng.wait_ge(semh, val)
                if s.fn is None:
                    inst = None
                else:
                    inst = s.fn(eng)
                if s.is_dma:
                    assert inst is not None, s.tag
                    inst.then_inc(dma_sem_handles[s.sem], 16)
                elif s.need_inc:
                    assert inst is not None, (s.tag, "need inc but fn returned None")
                    inst.then_inc(prog_sems[s.eng], 1)
        return emit_engine



def ceil_div(a, b):
    return (a + b - 1) // b


_NAME_PREFIX = [""]


class SBuf:

    def __init__(self, ctx, nc, name, shape, dtype, ntiles=1):
        name = _NAME_PREFIX[0] + name
        self.h = ctx.enter_context(nc.sbuf_tensor(name, shape, dtype))
        self.bufs = [Buf(f"{name}.{i}") for i in range(ntiles)]
        self.name = name

    def __getitem__(self, idx):
        return self.h[idx]

    @property
    def b(self):
        assert len(self.bufs) == 1
        return self.bufs[0]


class PSum:
    def __init__(self, ctx, nc, name, shape, dtype):
        self.h = ctx.enter_context(nc.psum_tensor(name, shape, dtype))
        self.buf = Buf(name)
        self.name = name

    def __getitem__(self, idx):
        return self.h[idx]

    @property
    def b(self):
        return self.buf



def emit_nat_to_fm(P, R, src, dst, pfx, sem_ld, sem_st):
    nat = [R[pfx + "nat0"], R[pfx + "nat1"]]
    col = [R[pfx + "col0"], R[pfx + "col1"]]
    ps = [R["psA"], R["psB"]]
    ident = R["ident_f32"]
    nT = T // 128
    for t in range(nT):
        nb = nat[t % 2]
        cb = col[t % 2]
        P.dma("sp", (lambda e, nb=nb, t=t: e.dma_start(
            out=nb[:], in_=src[t * 128:(t + 1) * 128, :])), sem_ld,
            reads=(), writes=(nb.b,), tag=f"{pfx}ld{t}")
        for dg in range(D // 128):
            p = ps[dg % 2]
            P.step("pe", (lambda e, nb=nb, p=p, dg=dg: e.transpose(
                p[:, 0:128], nb[:, dg * 128:(dg + 1) * 128], ident[:])),
                reads=(nb.b, ident.b), writes=(p.b,), tag=f"{pfx}tr{t}.{dg}")
            P.step("dve", (lambda e, cb=cb, p=p, dg=dg: e.tensor_copy(
                cb[:, dg * 128:(dg + 1) * 128], p[:, 0:128])),
                reads=(p.b,), writes=(cb.b,), tag=f"{pfx}ev{t}.{dg}")
        dst_view = dst[:, t * 128:(t + 1) * 128].rearrange(
            "(g p) c -> p g c", p=128)
        P.dma("act", (lambda e, cb=cb, dst_view=dst_view: e.dma_start(
            out=dst_view, in_=cb[:].rearrange("p (g c) -> p g c", c=128))),
            sem_st, reads=(cb.b,), writes=(), tag=f"{pfx}st{t}")


def emit_fm_to_nat(P, R, src, dst, pfx, sem_ld, sem_st):
    nat = [R[pfx + "nat0"], R[pfx + "nat1"]]
    col = [R[pfx + "col0"], R[pfx + "col1"]]
    ps = [R["psA"], R["psB"]]
    ident = R["ident_f32"]
    nT = T // 128
    for t in range(nT):
        cb = col[t % 2]
        nb = nat[t % 2]
        src_view = src[:, t * 128:(t + 1) * 128].rearrange(
            "(g p) c -> p g c", p=128)
        P.dma("sp", (lambda e, cb=cb, src_view=src_view: e.dma_start(
            out=cb[:].rearrange("p (g c) -> p g c", c=128), in_=src_view)),
            sem_ld, reads=(), writes=(cb.b,), tag=f"{pfx}ld{t}")
        for dg in range(D // 128):
            p = ps[dg % 2]
            P.step("pe", (lambda e, cb=cb, p=p, dg=dg: e.transpose(
                p[:, 0:128], cb[:, dg * 128:(dg + 1) * 128], ident[:])),
                reads=(cb.b, ident.b), writes=(p.b,), tag=f"{pfx}tr{t}.{dg}")
            P.step("dve", (lambda e, nb=nb, p=p, dg=dg: e.tensor_copy(
                nb[:, dg * 128:(dg + 1) * 128], p[:, 0:128])),
                reads=(p.b,), writes=(nb.b,), tag=f"{pfx}ev{t}.{dg}")
        P.dma("act", (lambda e, nb=nb, t=t: e.dma_start(
            out=dst[t * 128:(t + 1) * 128, :], in_=nb[:])),
            sem_st, reads=(nb.b,), writes=(), tag=f"{pfx}st{t}")


def build_engine_sections(nc, P, ctx):
    prog_sems = {e: ctx.enter_context(nc.semaphore("sem_" + e)) for e in ENGINES}
    dma_handles = {n: ctx.enter_context(nc.semaphore("dma_" + n))
                   for n in P.dma_sem_names}
    block = ctx.enter_context(nc.Block())
    emit = P.build(nc, None, prog_sems, dma_handles)

    final_waits = [(n, 16 * c) for n, c in P.dma_sems.items() if c > 0]

    @block.sync
    def _(s):
        emit("sp", s)
        for n, v in final_waits:
            s.wait_ge(dma_handles[n], v)

    @block.vector
    def _(v):
        emit("dve", v)

    @block.scalar
    def _(sc):
        emit("act", sc)

    @block.tensor
    def _(t):
        emit("pe", t)

    @block.gpsimd
    def _(g):
        emit("pool", g)



def emit_ln_store(P, R, rt, rt_bufs, xout_blocks, xout_bufs, banks, tag,
                  width=512):
    s1, s2, c0b, c1b = banks["s1"], banks["s2"], banks["c0"], banks["c1"]
    onc = R["ones_col_bf"]
    onr = R["ones_row_f"]
    for dg in range(16):
        rb = R["ln_rb0"]
        rq = R["ln_rq0"]
        P.step("act", (lambda e, rb=rb, dg=dg: e.copy(rb[:], rt[dg])),
               reads=(rt_bufs[dg],), writes=(rb.b,), tag=f"{tag}rb{dg}")
        P.step("dve", (lambda e, rb=rb, rq=rq: e.tensor_tensor(
            rq[:], rb[:], rb[:], op=ALU.mult)),
            reads=(rb.b,), writes=(rq.b,), tag=f"{tag}rq{dg}")
        P.step("pe", (lambda e, rb=rb, dg=dg: e.matmul(
            s1[0:1, 0:width], onc[:], rb[:, 0:width],
            start=(dg == 0), stop=(dg == 15))),
            reads=(rb.b, onc.b), writes=(s1.b,), tag=f"{tag}s1_{dg}")
        P.step("pe", (lambda e, rq=rq, dg=dg: e.matmul(
            s2[0:1, 0:width], onc[:], rq[:, 0:width],
            start=(dg == 0), stop=(dg == 15))),
            reads=(rq.b, onc.b), writes=(s2.b,), tag=f"{tag}s2_{dg}")
    mean, ex2, var, c0r = R["ln_mean"], R["ln_ex2"], R["ln_var"], R["ln_c0"]
    P.step("act", (lambda e: e.activation(mean[:], s1[0:1, 0:width], AF.Copy,
                                          scale=1.0 / D)),
           reads=(s1.b,), writes=(mean.b,), tag=f"{tag}mean")
    P.step("act", (lambda e: e.activation(ex2[:], s2[0:1, 0:width], AF.Copy,
                                          scale=1.0 / D)),
           reads=(s2.b,), writes=(ex2.b,), tag=f"{tag}ex2")
    P.step("dve", (lambda e: e.tensor_tensor(var[:], mean[:], mean[:],
                                             op=ALU.mult)),
           reads=(mean.b,), writes=(var.b,), tag=f"{tag}msq")
    P.step("dve", (lambda e: e.tensor_tensor(var[:], ex2[:], var[:],
                                             op=ALU.subtract)),
           reads=(ex2.b, var.b), writes=(var.b,), tag=f"{tag}var")
    P.step("act", (lambda e: e.activation(var[:], var[:], AF.Ln,
                                          bias=R["cvec"][0:1, 0:1])),
           reads=(var.b, R["cvec"].b), writes=(var.b,), tag=f"{tag}lnv")
    P.step("act", (lambda e: e.activation(var[:], var[:], AF.Exp, scale=-0.5)),
           reads=(var.b,), writes=(var.b,), tag=f"{tag}rstd")
    P.step("dve", (lambda e: e.scalar_tensor_tensor(
        c0r[:], mean[:], -1.0, var[:], op0=ALU.mult, op1=ALU.mult)),
        reads=(mean.b, var.b), writes=(c0r.b,), tag=f"{tag}c0")
    P.step("pe", (lambda e: e.matmul(c0b[:, 0:width], onr[:], c0r[:],
                                     start=True, stop=True)),
           reads=(c0r.b, onr.b), writes=(c0b.b,), tag=f"{tag}repc0")
    P.step("pe", (lambda e: e.matmul(c1b[:, 0:width], onr[:], var[:],
                                     start=True, stop=True)),
           reads=(var.b, onr.b), writes=(c1b.b,), tag=f"{tag}repc1")
    c0s, c1s = R["ln_c0s"], R["ln_c1s"]
    P.step("act", (lambda e: e.activation(c0s[:], c0b[:, 0:width], AF.Copy)),
           reads=(c0b.b,), writes=(c0s.b,), tag=f"{tag}c0s")
    P.step("act", (lambda e: e.activation(c1s[:], c1b[:, 0:width], AF.Copy)),
           reads=(c1b.b,), writes=(c1s.b,), tag=f"{tag}c1s")
    for dg in range(16):
        xn = R["ln_xn0"]
        P.step("dve", (lambda e, xn=xn, dg=dg: e.tensor_tensor(
            xn[:], rt[dg], c1s[:], op=ALU.mult)),
            reads=(rt_bufs[dg], c1s.b), writes=(xn.b,), tag=f"{tag}n1_{dg}")
        P.step("dve", (lambda e, xn=xn: e.tensor_tensor(
            xn[:], xn[:], c0s[:], op=ALU.add)),
            reads=(c0s.b, xn.b), writes=(xn.b,), tag=f"{tag}n2_{dg}")
        wb_ = xout_bufs[dg]
        if not isinstance(wb_, tuple):
            wb_ = (wb_,)
        P.dma("act", (lambda e, xn=xn, dg=dg: e.dma_start(
            out=xout_blocks[dg], in_=xn[:])), R.sem_st,
            reads=(xn.b,), writes=wb_, tag=f"{tag}st{dg}")



def emit_quant(P, R, w_dram, w_bufs, what_dram, what_bufs, nrow_tiles, ncol,
               tag):
    nchunk = ncol // 2048
    for rt in range(nrow_tiles):
        for ch in range(nchunk):
            idx = rt * nchunk + ch
            nat = R["qn0"]
            wb, rr = R["qwb"], R["qr"]
            aa = R["qa"]; bb = R["qwb"]
            qs, qsb, qrc = R["qs"], R["qsb"], R["qrc"]
            qsb2 = R["qsb2"]
            src = w_dram[rt * 128:(rt + 1) * 128, ch * 2048:(ch + 1) * 2048]
            P.dma("sp", (lambda e, nat=nat, src=src: e.dma_start(
                out=nat[:], in_=src)), R.sem_w,
                reads=(w_bufs[idx],), writes=(nat.b,), tag=f"{tag}ld{idx}")
            P.step("act", (lambda e, nat=nat: e.copy(R["qwb"][:], nat[:])),
                   reads=(nat.b,), writes=(wb.b,), tag=f"{tag}wb{idx}")
            P.step("dve", (lambda e: e.tensor_reduce(
                qs[:], R["qwb"][:].rearrange("p (g k) -> p g k", k=64),
                axis=AX.X, op=ALU.add, apply_absolute_value=True)),
                reads=(wb.b,), writes=(qs.b,), tag=f"{tag}as{idx}")
            P.step("dve", (lambda e: e.tensor_single_scalar(
                qsb[:], qs[:], 1.0 / 64, ALU.mult)),
                reads=(qs.b,), writes=(qsb.b,), tag=f"{tag}sc{idx}")
            P.step("dve", (lambda e: e.tensor_single_scalar(
                qsb2[:], qsb[:], 1e-8, ALU.max)),
                reads=(qsb.b,), writes=(qsb2.b,), tag=f"{tag}sc2{idx}")
            P.step("dve", (lambda e: e.reciprocal(qrc[:], qsb2[:])),
                   reads=(qsb2.b,), writes=(qrc.b,), tag=f"{tag}rc{idx}")
            P.step("dve", (lambda e: e.tensor_tensor(
                rr[:].rearrange("p (g k) -> p g k", k=64),
                R["qwb"][:].rearrange("p (g k) -> p g k", k=64),
                qrc[:].unsqueeze(2).to_broadcast([128, 32, 64]), op=ALU.mult)),
                reads=(wb.b, qrc.b), writes=(rr.b,), tag=f"{tag}r{idx}")
            P.step("dve", (lambda e: e.tensor_single_scalar(
                aa[:], rr[:], 0.5, ALU.is_gt)),
                reads=(rr.b,), writes=(aa.b,), tag=f"{tag}a{idx}")
            P.step("dve", (lambda e: e.tensor_single_scalar(
                bb[:], rr[:], -0.5, ALU.is_lt)),
                reads=(rr.b,), writes=(bb.b,), tag=f"{tag}b{idx}")
            P.step("dve", (lambda e: e.tensor_tensor(
                aa[:], aa[:], bb[:], op=ALU.subtract)),
                reads=(bb.b, aa.b), writes=(aa.b,), tag=f"{tag}q{idx}")
            P.step("dve", (lambda e: e.tensor_tensor(
                aa[:].rearrange("p (g k) -> p g k", k=64),
                aa[:].rearrange("p (g k) -> p g k", k=64),
                qsb2[:].unsqueeze(2).to_broadcast([128, 32, 64]), op=ALU.mult)),
                reads=(qsb2.b, aa.b), writes=(aa.b,), tag=f"{tag}wh{idx}")
            dst = what_dram[rt * 128:(rt + 1) * 128, ch * 2048:(ch + 1) * 2048]
            P.dma("act", (lambda e, dst=dst: e.dma_start(
                out=dst, in_=R["qa"][:])), R.sem_w,
                reads=(aa.b,), writes=(what_bufs[idx],), tag=f"{tag}st{idx}")


class Res(dict):
    __getattr__ = dict.__getitem__
    __setattr__ = dict.__setitem__


def emit_gu_transform(P, R, what_dram, what_bufs, wT_dram, wT_bufs, tag):
    for kind in range(2):
        for hblk in range(4):
            h0 = kind * 8192 + hblk * 2048
            for dg in range(16):
                tt = R["tt0"]
                src = what_dram[h0:h0 + 2048, dg * 128:(dg + 1) * 128]
                rbufs = [what_bufs[(h0 + i * 128) // 128] for i in range(16)]
                P.dma("sp", (lambda e, tt=tt, src=src: e.dma_start(
                    out=tt[:], in_=src, transpose=True)), R.sem_w,
                    reads=tuple(rbufs), writes=(tt.b,), tag=f"{tag}tl{kind}.{hblk}.{dg}")
                slots = [(hblk * 4 + i) * 32 + kind * 16 + dg for i in range(4)]
                dst = wT_dram.rearrange("(hb r) p c -> hb r p c", r=32)[
                    hblk * 4:hblk * 4 + 4, kind * 16 + dg, :, :].transpose([1, 0, 2])
                P.dma("act", (lambda e, tt=tt, dst=dst: e.dma_start(
                    out=dst, in_=tt[:].rearrange("p (b c) -> p b c", c=512))),
                    R.sem_w, reads=(tt.b,),
                    writes=tuple(wT_bufs[s] for s in slots),
                    tag=f"{tag}ts{kind}.{hblk}.{dg}")


def emit_wp_transform(P, R, what_dram, what_bufs, wT_dram, wT_bufs, tag):
    for kh in range(64):
        tt = R["tt0"]
        src = what_dram[0:2048, kh * 128:(kh + 1) * 128]
        rbufs = [what_bufs[i] for i in range(len(what_bufs))]
        P.dma("sp", (lambda e, tt=tt, src=src: e.dma_start(
            out=tt[:], in_=src, transpose=True)), R.sem_w,
            reads=tuple(rbufs), writes=(tt.b,), tag=f"{tag}tl{kh}")
        slots = [i * 64 + kh for i in range(4)]
        dst = wT_dram.rearrange("(db r) p c -> db r p c", r=64)[
            :, kh, :, :].transpose([1, 0, 2])
        P.dma("act", (lambda e, tt=tt, dst=dst: e.dma_start(
            out=dst, in_=tt[:].rearrange("p (b c) -> p b c", c=512))),
            R.sem_w, reads=(tt.b,),
            writes=tuple(wT_bufs[s] for s in slots), tag=f"{tag}ts{kh}")


def emit_ffn(P, R, B, xin, xin_bufs, wguT, wguT_bufs, wprojT, wprojT_bufs,
             xout, xout_bufs, tag="f", dbg=None):
    EPS32 = 1.1920929e-7
    xin_r = xin.rearrange("(g p) c -> p g c", p=128)
    for tau in range(4):
        tcols = slice(tau * 512, (tau + 1) * 512)
        tb = [xin_bufs[tau * 4 + i] for i in range(4)]
        x1bf = R["x1bf"]
        P.dma("pool", (lambda e, x1bf=x1bf, tcols=tcols: e.dma_start(
            out=x1bf[:], in_=xin_r[:, :, tcols])), R.sem_x,
            reads=tuple(tb), writes=(x1bf.b,), tag=f"{tag}xbf{tau}")
        psG, psU, s2h = B[0], B[1], B[2]
        for hb in range(16):
            wgg, wgu_ = R["wg_g"], R["wg_u"]
            for kind, wt in ((0, wgg), (1, wgu_)):
                slots = [hb * 32 + kind * 16 + dg for dg in range(16)]
                src = wguT.rearrange("(hb r) p c -> hb r p c", r=32)[
                    hb, kind * 16:(kind + 1) * 16, :, :].transpose([1, 0, 2])
                P.dma("sp", (lambda e, wt=wt, src=src: e.dma_start(
                    out=wt[:].rearrange("p (b c) -> p b c", c=512), in_=src)),
                    R.sem_w, reads=tuple(wguT_bufs[s] for s in slots),
                    writes=(wt.b,), tag=f"{tag}wg{tau}.{hb}.{kind}")
            for m in range(4):
                hidx = hb * 4 + m
                def mmG(e, wgg=wgg, x1bf=x1bf, m=m):
                    for dg in range(16):
                        inst = e.matmul(psG[:, 0:512],
                                        wgg[:, dg * 512 + m * 128:dg * 512 + (m + 1) * 128],
                                        x1bf[:, dg * 512:(dg + 1) * 512],
                                        start=(dg == 0), stop=(dg == 15))
                    return inst
                def mmU(e, wgu_=wgu_, x1bf=x1bf, m=m):
                    for dg in range(16):
                        inst = e.matmul(psU[:, 0:512],
                                        wgu_[:, dg * 512 + m * 128:dg * 512 + (m + 1) * 128],
                                        x1bf[:, dg * 512:(dg + 1) * 512],
                                        start=(dg == 0), stop=(dg == 15))
                    return inst
                P.step("pe", mmG, reads=(wgg.b, x1bf.b), writes=(psG.b,),
                       tag=f"{tag}G{tau}.{hidx}")
                P.step("pe", mmU, reads=(wgu_.b, x1bf.b), writes=(psU.b,),
                       tag=f"{tag}U{tau}.{hidx}")
                sg = R["sg0"]
                hsq = R["hq0"]
                P.step("act", (lambda e, sg=sg: e.activation(
                    sg[:], psG[:, 0:512], AF.Silu)),
                    reads=(psG.b,), writes=(sg.b,), tag=f"{tag}sg{tau}.{hidx}")
                hv = R["harena"][:, hidx * 512:(hidx + 1) * 512]
                hvb = R["harena"].bufs[hidx]
                P.step("dve", (lambda e, sg=sg, hv=hv: e.tensor_tensor(
                    hv, sg[:], psU[:, 0:512], op=ALU.mult)),
                    reads=(sg.b, psU.b), writes=(hvb,), tag=f"{tag}h{tau}.{hidx}")
                P.step("dve", (lambda e, hsq=hsq, hv=hv: e.tensor_tensor(
                    hsq[:], hv, hv, op=ALU.mult)),
                    reads=(hvb,), writes=(hsq.b,), tag=f"{tag}hq{tau}.{hidx}")
                if dbg is not None and tau == 0:
                    P.dma("pool", (lambda e, hv=hv, hidx=hidx: e.dma_start(
                        out=dbg["h"][hidx], in_=hv)), R.sem_st,
                        reads=(hvb,), tag=f"{tag}dbgh{hidx}")
                P.step("pe", (lambda e, hsq=hsq, hidx=hidx: e.matmul(
                    s2h[0:1, 0:512], R["ones_col_bf"][:], hsq[:],
                    start=(hidx == 0), stop=(hidx == 63))),
                    reads=(hsq.b, R["ones_col_bf"].b), writes=(s2h.b,),
                    tag=f"{tag}s2h{tau}.{hidx}")
        row = R["ln_var"]
        P.step("act", (lambda e: e.activation(row[:], s2h[0:1, 0:512], AF.Ln,
                                              bias=R["cvec"][0:1, 1:2],
                                              scale=1.0 / 8192)),
               reads=(s2h.b, R["cvec"].b), writes=(row.b,), tag=f"{tag}rl{tau}")
        P.step("act", (lambda e: e.activation(row[:], row[:], AF.Exp, scale=-0.5)),
               reads=(row.b,), writes=(row.b,), tag=f"{tag}re{tau}")
        rep = B[7]
        P.step("pe", (lambda e: e.matmul(rep[:, 0:512], R["ones_row_f"][:],
                                         row[:], start=True, stop=True)),
               reads=(row.b, R["ones_row_f"].b), writes=(rep.b,),
               tag=f"{tag}rrep{tau}")
        rstdh = R["rstdh"]
        P.step("act", (lambda e: e.activation(rstdh[:], rep[:, 0:512], AF.Copy)),
               reads=(rep.b,), writes=(rstdh.b,), tag=f"{tag}rcp{tau}")
        if dbg is not None and tau == 0:
            P.dma("pool", (lambda e: e.dma_start(out=dbg["rs"], in_=rstdh[:])),
                  R.sem_st, reads=(rstdh.b,), tag=f"{tag}dbgrs")
        psOs = [B[3], B[4], B[5], B[6]]
        for db in range(4):
            for kb in range(8):
                wp = R["wp"]
                slots = list(range(db * 64 + kb * 8, db * 64 + kb * 8 + 8))
                src = wprojT.rearrange("(a r) p c -> a r p c", r=8)[
                    db * 8 + kb, :, :, :].transpose([1, 0, 2])
                P.dma("sp", (lambda e, wp=wp, src=src: e.dma_start(
                    out=wp[:].rearrange("p (r c) -> p r c", c=512), in_=src)), R.sem_w,
                    reads=tuple(wprojT_bufs[s] for s in slots),
                    writes=(wp.b,), tag=f"{tag}wp{tau}.{db}.{kb}")
                for mm in range(4):
                    def mmO(e, wp=wp, kb=kb, mm=mm):
                        for k8 in range(8):
                            kh = kb * 8 + k8
                            inst = e.matmul(
                                psOs[mm][:, 0:512],
                                wp[:, k8 * 512 + mm * 128:k8 * 512 + (mm + 1) * 128],
                                R["harena"][:, kh * 512:(kh + 1) * 512],
                                start=(kh == 0), stop=(kh == 63))
                        return inst
                    hb_reads = tuple(R["harena"].bufs[kb * 8 + k8] for k8 in range(8))
                    P.step("pe", mmO, reads=(wp.b,) + hb_reads,
                           writes=(psOs[mm].b,), tag=f"{tag}O{tau}.{db}.{kb}.{mm}")
            for mm in range(4):
                dg = db * 4 + mm
                tf = R["tf0"]
                xf = R["xf0"]
                P.dma("sp", (lambda e, xf=xf, dg=dg, tcols=tcols: e.dma_start(
                    out=xf[:], in_=xin[dg * 128:(dg + 1) * 128, tcols])),
                    R.sem_x, reads=tuple(xin_bufs[tau * 4 + i] for i in range(4)), writes=(xf.b,),
                    tag=f"{tag}xf{tau}.{dg}")
                P.step("dve", (lambda e, tf=tf, mm=mm: e.tensor_tensor(
                    tf[:], rstdh[:], psOs[mm][:, 0:512], op=ALU.mult)),
                    reads=(rstdh.b, psOs[mm].b), writes=(tf.b,),
                    tag=f"{tag}t{tau}.{dg}")
                rt = R["rt"][:, dg * 512:(dg + 1) * 512]
                rtb = R["rt"].bufs[dg]
                P.step("dve", (lambda e, tf=tf, xf=xf, rt=rt: e.tensor_tensor(
                    rt, tf[:], xf[:], op=ALU.add)),
                    reads=(tf.b, xf.b), writes=(rtb,), tag=f"{tag}r{tau}.{dg}")
                if dbg is not None and tau == 0:
                    P.dma("pool", (lambda e, rt=rt, dg=dg: e.dma_start(
                        out=dbg["rt"][dg], in_=rt)), R.sem_st,
                        reads=(rtb,), tag=f"{tag}dbgrt{dg}")
        rt_views = [R["rt"][:, dg * 512:(dg + 1) * 512] for dg in range(16)]
        xout_blk = [xout[dg * 128:(dg + 1) * 128, tcols] for dg in range(16)]
        xo_bufs = [tuple(xout_bufs[tau * 4 + i] for i in range(4))
                   for dg in range(16)]
        banks = {"s1": B[0], "s2": B[1], "c0": B[2], "c1": B[7]}
        emit_ln_store(P, R, rt_views, R["rt"].bufs, xout_blk, xo_bufs, banks,
                      tag=f"{tag}ln{tau}.")



def dram_bufs(n, name):
    return [Buf(f"{name}.{i}") for i in range(n)]


def build_ffn_program(xin_name="s1", xout_name="s2", debug=False):
    nc = bass.Bass("TRN2", target_bir_lowering=False, debug=False)
    xin = nc.dram_tensor(xin_name, [D, T], F32, kind="ExternalInput").ap()
    wgu = nc.dram_tensor("L0_ffn_w_gate_up", [16384, 2048], F32,
                         kind="ExternalInput").ap()
    wproj = nc.dram_tensor("L0_ffn_w_proj", [2048, 8192], F32,
                           kind="ExternalInput").ap()
    xout = nc.dram_tensor(xout_name, [D, T], F32, kind="ExternalOutput").ap()
    kind_dbg = "ExternalOutput" if debug else "Internal"
    wgu_hat = nc.dram_tensor("wgu_hat", [16384, 2048], BF16, kind=kind_dbg).ap()
    wproj_hat = nc.dram_tensor("wproj_hat", [2048, 8192], BF16, kind=kind_dbg).ap()
    wguT = nc.dram_tensor("wguT", [512, 128, 512], BF16).ap()
    wprojT = nc.dram_tensor("wprojT", [256, 128, 512], BF16).ap()

    ctx = ExitStack()
    P = Prog()
    R = Res()
    R["sem_w"] = P.new_dma_sem("w")
    R["sem_x"] = P.new_dma_sem("x")
    R["sem_st"] = P.new_dma_sem("st")
    R["sem_c"] = P.new_dma_sem("c")

    onesc = nc.dram_tensor("ones_col", [128, 1], F32, kind="ExternalInput").ap()
    onesr = nc.dram_tensor("ones_row", [1, 128], F32, kind="ExternalInput").ap()
    cvec = nc.dram_tensor("cvec", [128, 8], F32, kind="ExternalInput").ap()
    R["ones_col_bf"] = SBuf(ctx, nc, "ones_col_bf", [128, 1], BF16)
    R["ones_row_f"] = SBuf(ctx, nc, "ones_row_f", [1, 128], F32)
    P.dma("pool", lambda e: e.dma_start(out=R["ones_col_bf"][:], in_=onesc),
          R.sem_c, writes=(R["ones_col_bf"].b,), tag="c1")
    P.dma("sp", lambda e: e.dma_start(out=R["ones_row_f"][:], in_=onesr),
          R.sem_c, writes=(R["ones_row_f"].b,), tag="c2")
    R["cvec"] = SBuf(ctx, nc, "cvec_s", [128, 8], F32)
    P.dma("sp", lambda e: e.dma_start(out=R["cvec"][:], in_=cvec),
          R.sem_c, writes=(R["cvec"].b,), tag="c3")

    alloc_ffn_res(ctx, nc, R)
    B = [PSum(ctx, nc, f"bank{i}", [128, 512], F32) for i in range(8)]

    wgu_b = dram_bufs(128, "wgu")
    wproj_b = dram_bufs(64, "wproj")
    wguh_b = dram_bufs(128, "wguh")
    wproh_b = dram_bufs(64, "wproh")
    wguT_b = dram_bufs(512, "wguT")
    wprojT_b = dram_bufs(256, "wprojT")
    xin_b = dram_bufs(16, "xin")
    xout_b = dram_bufs(16, "xout")

    emit_quant(P, R, wgu, wgu_b, wgu_hat, wguh_b, 128, 2048, "qg")
    emit_quant(P, R, wproj, wproj_b, wproj_hat, wproh_b, 16, 8192, "qp")
    emit_gu_transform(P, R, wgu_hat, wguh_b, wguT, wguT_b, "tg")
    emit_wp_transform(P, R, wproj_hat, wproh_b, wprojT, wprojT_b, "tp")
    dbg = None
    if debug:
        dbg_h = nc.dram_tensor("dbg_h", [64, 128, 512], BF16, kind="ExternalOutput").ap()
        dbg_rt = nc.dram_tensor("dbg_rt", [16, 128, 512], F32, kind="ExternalOutput").ap()
        dbg_rs = nc.dram_tensor("dbg_rs", [128, 512], F32, kind="ExternalOutput").ap()
        dbg = {"h": dbg_h, "rt": dbg_rt, "rs": dbg_rs}
    emit_ffn(P, R, B, xin, xin_b, wguT, wguT_b, wprojT, wprojT_b,
             xout, xout_b, tag="f", dbg=dbg)
    build_engine_sections(nc, P, ctx)
    ctx.close()
    return nc


def alloc_ffn_res(ctx, nc, R):
    R["qn0"] = SBuf(ctx, nc, "qn0", [128, 2048], F32)
    R["qwb"] = SBuf(ctx, nc, "qwb", [128, 2048], BF16)
    R["qr"] = SBuf(ctx, nc, "qr", [128, 2048], BF16)
    R["qa"] = SBuf(ctx, nc, "qa", [128, 2048], BF16)
    R["qs"] = SBuf(ctx, nc, "qs", [128, 32], F32)
    R["qsb"] = SBuf(ctx, nc, "qsb", [128, 32], BF16)
    R["qsb2"] = SBuf(ctx, nc, "qsb2", [128, 32], BF16)
    R["qrc"] = SBuf(ctx, nc, "qrc", [128, 32], F32)
    R["tt0"] = SBuf(ctx, nc, "tt0", [128, 2048], BF16)
    R["x1bf"] = SBuf(ctx, nc, "x1bf", [128, 8192], BF16)
    R["wg_g"] = SBuf(ctx, nc, "wg_g", [128, 8192], BF16)
    R["wg_u"] = SBuf(ctx, nc, "wg_u", [128, 8192], BF16)
    R["wp"] = SBuf(ctx, nc, "wp", [128, 4096], BF16)
    R["harena"] = SBuf(ctx, nc, "harena", [128, 64 * 512], BF16, ntiles=64)
    R["rt"] = SBuf(ctx, nc, "rt", [128, 16 * 512], F32, ntiles=16)
    R["rstdh"] = SBuf(ctx, nc, "rstdh", [128, 512], F32)
    for nm in ["sg0", "tf0", "xf0"]:
        R[nm] = SBuf(ctx, nc, nm, [128, 512], F32)
    for nm in ["hq0", "ln_rb0", "ln_rq0"]:
        R[nm] = SBuf(ctx, nc, nm, [128, 512], BF16)
    for nm in ["ln_xn0", "ln_c0s", "ln_c1s"]:
        R[nm] = SBuf(ctx, nc, nm, [128, 512], F32)
    for nm in ["ln_mean", "ln_ex2", "ln_var", "ln_c0"]:
        R[nm] = SBuf(ctx, nc, nm, [1, 512], F32)



def emit_attn(P, R, B, nc, xin, xin_bufs, io, xout, xout_bufs, tag="a"):
    xin_r = xin.rearrange("(g p) c -> p g c", p=128)
    SC = 1.0 / np.sqrt(128.0)
    qsb_bufs = io["qs_bufs"]; ksb_bufs = io["ks_bufs"]; vsb_bufs = io["vs_bufs"]
    for tau in range(4):
        tcols = slice(tau * 512, (tau + 1) * 512)
        tb = tuple(xin_bufs[tau * 4 + i] for i in range(4))
        x2bf = R["x1bf"]
        P.dma("pool", (lambda e, x2bf=x2bf, tcols=tcols: e.dma_start(
            out=x2bf[:], in_=xin_r[:, :, tcols])), R.sem_x,
            reads=tb, writes=(x2bf.b,), tag=f"{tag}xbf{tau}")
        for proj, wT, bias_sb, scr, scr_bufs in (
                ("q", io["wq"], R["bq"], io["qs_s"], qsb_bufs),
                ("k", io["wk"], R["bk"], io["ks_s"], ksb_bufs)):
            for dt in range(16):
                wc = R["wqk"]
                src = wT[:, dt * 128:(dt + 1) * 128].rearrange(
                    "(g p) c -> p g c", p=128)
                P.dma("pool", (lambda e, wc=wc, src=src: e.dma_start(
                    out=wc[:].rearrange("p (g c) -> p g c", c=128), in_=src)),
                    R.sem_w, writes=(wc.b,), tag=f"{tag}w{proj}{tau}.{dt}")
                ps = B[0]
                def mm(e, wc=wc, x2bf=x2bf):
                    for dg in range(16):
                        inst = e.matmul(ps[:, 0:512],
                                        wc[:, dg * 128:(dg + 1) * 128],
                                        x2bf[:, dg * 512:(dg + 1) * 512],
                                        start=(dg == 0), stop=(dg == 15))
                    return inst
                P.step("pe", mm, reads=(wc.b, x2bf.b), writes=(ps.b,),
                       tag=f"{tag}mm{proj}{tau}.{dt}")
                qraw = R["qraw"]
                P.step("act", (lambda e, qraw=qraw, dt=dt, bias_sb=bias_sb:
                               e.activation(qraw[:], ps[:, 0:512], AF.Identity,
                                            bias=bias_sb[:, dt:dt + 1])),
                       reads=(ps.b, bias_sb.b), writes=(qraw.b,),
                       tag=f"{tag}ev{proj}{tau}.{dt}")
                ps2 = B[3]
                P.step("pe", (lambda e, qraw=qraw, ps2=ps2: e.matmul(
                    ps2[:, 0:512], R["swap_bf"][:], qraw[:],
                    start=True, stop=True)),
                    reads=(qraw.b, R["swap_bf"].b), writes=(ps2.b,),
                    tag=f"{tag}sw{proj}{tau}.{dt}")
                swb = R["swb"]
                P.step("act", (lambda e, swb=swb, ps2=ps2: e.activation(
                    swb[:], ps2[:, 0:512], AF.Copy)),
                    reads=(ps2.b,), writes=(swb.b,), tag=f"{tag}swe{proj}{tau}.{dt}")
                t1 = R["rt1"]
                qa = R["qkarena"]
                qav = qa[:, 0:512]
                P.step("dve", (lambda e, qraw=qraw, t1=t1, tcols=tcols:
                               e.tensor_tensor(t1[:], qraw[:],
                                               R["ctab"][:, tcols], op=ALU.mult)),
                       reads=(qraw.b, R["ctab"].b), writes=(t1.b,),
                       tag=f"{tag}r1{proj}{tau}.{dt}")
                t2 = R["rt2"]
                P.step("dve", (lambda e, swb=swb, t2=t2, tcols=tcols:
                                e.tensor_tensor(t2[:], swb[:],
                                                R["stab"][:, tcols], op=ALU.mult)),
                       reads=(swb.b, R["stab"].b), writes=(t2.b,),
                       tag=f"{tag}r2{proj}{tau}.{dt}")
                P.step("dve", (lambda e, t1=t1, t2=t2, qav=qav:
                               e.tensor_tensor(qav, t1[:], t2[:], op=ALU.add)),
                       reads=(t1.b, t2.b), writes=(qa.b,), tag=f"{tag}r3{proj}{tau}.{dt}")
                P.dma("act", (lambda e, qav=qav, scr=scr, dt=dt, tcols=tcols:
                              e.dma_start(out=scr[dt * 128:(dt + 1) * 128, tcols],
                                          in_=qav)), R.sem_st,
                      reads=(qa.b,), writes=(scr_bufs[dt * 4 + tau],),
                      tag=f"{tag}st{proj}{tau}.{dt}")
        for dc in range(4):
            wv = R["wvc"]
            src = io["wv"][:, dc * 512:(dc + 1) * 512].rearrange(
                "(g p) c -> p g c", p=128)
            P.dma("pool", (lambda e, wv=wv, src=src: e.dma_start(
                out=wv[:].rearrange("p (g c) -> p g c", c=512), in_=src)),
                R.sem_w, writes=(wv.b,), tag=f"{tag}wv{tau}.{dc}")
            for tt in range(4):
                ps = B[1 + tt % 2]
                def mmv(e, wv=wv, x2bf=x2bf, tt=tt, dc=dc, ps=ps):
                    for dg in range(16):
                        e.matmul(ps[:, 0:512],
                                 x2bf[:, dg * 512 + tt * 128:dg * 512 + (tt + 1) * 128],
                                 wv[:, dg * 512:(dg + 1) * 512],
                                 start=(dg == 0), stop=False)
                    inst = e.matmul(ps[:, 0:512], R["ones_row_f"][:],
                                    R["bv"][0:1, dc * 512:(dc + 1) * 512],
                                    start=False, stop=True)
                    return inst
                P.step("pe", mmv,
                       reads=(wv.b, x2bf.b, R["ones_row_f"].b, R["bv"].b),
                       writes=(ps.b,), tag=f"{tag}mmv{tau}.{dc}.{tt}")
                vsb = R["vsb0" if tt % 2 == 0 else "vsb1"]
                P.step("act", (lambda e, ps=ps, vsb=vsb: e.activation(
                    vsb[:], ps[:, 0:512], AF.Copy)),
                    reads=(ps.b,), writes=(vsb.b,), tag=f"{tag}evv{tau}.{dc}.{tt}")
                P.dma("act", (lambda e, vsb=vsb, tau=tau, tt=tt, dc=dc:
                              e.dma_start(out=io["vs_s"][
                                  tau * 512 + tt * 128:tau * 512 + (tt + 1) * 128,
                                  dc * 512:(dc + 1) * 512], in_=vsb[:])),
                      R.sem_st, reads=(vsb.b,),
                      writes=(vsb_bufs[tau * 4 + dc],), tag=f"{tag}stv{tau}.{dc}.{tt}")
    for qt in range(4):
        tcols = slice(qt * 512, (qt + 1) * 512)
        for h in range(16):
            qtile = R["qtile"]
            P.dma("sp", (lambda e, qtile=qtile, h=h, tcols=tcols: e.dma_start(
                out=qtile[:], in_=io["qs_s"][h * 128:(h + 1) * 128, tcols])),
                R.sem_x, reads=(qsb_bufs[h * 4 + qt],), writes=(qtile.b,),
                tag=f"{tag}lq{qt}.{h}")
            kh, vh = R["kh"], R["vh"]
            for kc in range(qt + 1):
                P.dma("sp", (lambda e, kh=kh, h=h, kc=kc: e.dma_start(
                    out=kh[:, kc * 512:(kc + 1) * 512],
                    in_=io["ks_s"][h * 128:(h + 1) * 128, kc * 512:(kc + 1) * 512])),
                    R.sem_x, reads=(ksb_bufs[h * 4 + kc],), writes=(kh.b,),
                    tag=f"{tag}lk{qt}.{h}.{kc}")
                P.dma("sp", (lambda e, vh=vh, h=h, kc=kc: e.dma_start(
                    out=vh[:].rearrange("p (g c) -> p g c", c=128)[:, kc * 4:(kc + 1) * 4, :],
                    in_=io["vs_s"][kc * 512:(kc + 1) * 512,
                                   h * 128:(h + 1) * 128].rearrange(
                        "(g p) c -> p g c", p=128))),
                    R.sem_x, reads=(vsb_bufs[kc * 4 + h // 4],), writes=(vh.b,),
                    tag=f"{tag}lv{qt}.{h}.{kc}")
            pso, psl = B[4], B[5]
            nkb = (qt + 1) * 4
            for kb in range(nkb):
                pss = B[2 + kb % 2]
                P.step("pe", (lambda e, pss=pss, kh=kh, qtile=qtile, kb=kb:
                              e.matmul(pss[:, 0:512],
                                       kh[:, kb * 128:(kb + 1) * 128],
                                       qtile[:], start=True, stop=True)),
                       reads=(kh.b, qtile.b), writes=(pss.b,),
                       tag=f"{tag}s{qt}.{h}.{kb}")
                pt = R["pt0" if kb % 2 == 0 else "pt1"]
                P.step("act", (lambda e, pss=pss, pt=pt: e.activation(
                    pt[:], pss[:, 0:512], AF.Exp, scale=SC,
                    bias=R["cvec"][:, 2:3])),
                    reads=(pss.b, R["cvec"].b), writes=(pt.b,),
                    tag=f"{tag}e{qt}.{h}.{kb}")
                s_off = kb - qt * 4
                if s_off >= 0:
                    P.step("dve", (lambda e, pt=pt, s_off=s_off: e.tensor_tensor(
                        pt[:], pt[:], R["masks"][:, s_off * 512:(s_off + 1) * 512],
                        op=ALU.mult)),
                        reads=(R["masks"].b, pt.b), writes=(pt.b,),
                        tag=f"{tag}m{qt}.{h}.{kb}")
                P.step("pe", (lambda e, pt=pt, vh=vh, kb=kb, pso=pso, nkb=nkb:
                              e.matmul(pso[:, 0:512],
                                       vh[:, kb * 128:(kb + 1) * 128],
                                       pt[:], start=(kb == 0), stop=(kb == nkb - 1))),
                       reads=(vh.b, pt.b), writes=(pso.b,),
                       tag=f"{tag}o{qt}.{h}.{kb}")
                P.step("pe", (lambda e, pt=pt, kb=kb, psl=psl, nkb=nkb:
                              e.matmul(psl[0:1, 0:512], R["ones_col_bf"][:],
                                       pt[:], start=(kb == 0), stop=(kb == nkb - 1))),
                       reads=(pt.b, R["ones_col_bf"].b), writes=(psl.b,),
                       tag=f"{tag}l{qt}.{h}.{kb}")
            linv = R["linv"]
            P.step("dve", (lambda e, linv=linv, psl=psl: e.reciprocal(
                linv[0:1, :], psl[0:1, 0:512])),
                reads=(psl.b,), writes=(linv.b,), tag=f"{tag}li{qt}.{h}")
            psr = B[6]
            P.step("pe", (lambda e, linv=linv, psr=psr: e.matmul(
                psr[:, 0:512], R["ones_row_f"][:], linv[0:1, :],
                start=True, stop=True)),
                reads=(linv.b, R["ones_row_f"].b), writes=(psr.b,),
                tag=f"{tag}lr{qt}.{h}")
            linv_s = R["linv_s"]
            P.step("act", (lambda e, linv_s=linv_s, psr=psr: e.activation(
                linv_s[:], psr[:, 0:512], AF.Copy)),
                reads=(psr.b,), writes=(linv_s.b,), tag=f"{tag}lc{qt}.{h}")
            at = R["attT"][:, h * 512:(h + 1) * 512]
            atb = R["attT"].bufs[h]
            P.step("dve", (lambda e, at=at, pso=pso, linv_s=linv_s:
                           e.tensor_tensor(at, linv_s[:], pso[:, 0:512],
                                           op=ALU.mult)),
                   reads=(linv_s.b, pso.b), writes=(atb,), tag=f"{tag}av{qt}.{h}")
        for dt in range(16):
            wo = R["woc"]
            src = io["wo"][:, dt * 128:(dt + 1) * 128].rearrange(
                "(g p) c -> p g c", p=128)
            P.dma("pool", (lambda e, wo=wo, src=src: e.dma_start(
                out=wo[:].rearrange("p (g c) -> p g c", c=128), in_=src)),
                R.sem_w, writes=(wo.b,), tag=f"{tag}wo{qt}.{dt}")
            ps = B[7]
            def mmo(e, wo=wo):
                for dg in range(16):
                    inst = e.matmul(ps[:, 0:512],
                                    wo[:, dg * 128:(dg + 1) * 128],
                                    R["attT"][:, dg * 512:(dg + 1) * 512],
                                    start=(dg == 0), stop=(dg == 15))
                return inst
            P.step("pe", mmo, reads=(wo.b,) + tuple(R["attT"].bufs),
                   writes=(ps.b,), tag=f"{tag}mmo{qt}.{dt}")
            tf = R["tf0"]
            P.step("act", (lambda e, tf=tf, ps=ps, dt=dt: e.activation(
                tf[:], ps[:, 0:512], AF.Identity, bias=R["bo"][:, dt:dt + 1])),
                reads=(ps.b, R["bo"].b), writes=(tf.b,), tag=f"{tag}to{qt}.{dt}")
            xf = R["xf0"]
            P.dma("sp", (lambda e, xf=xf, dt=dt, tcols=tcols: e.dma_start(
                out=xf[:], in_=xin[dt * 128:(dt + 1) * 128, tcols])),
                R.sem_x, reads=tuple(xin_bufs[qt * 4 + i] for i in range(4)),
                writes=(xf.b,), tag=f"{tag}xo{qt}.{dt}")
            rt = R["rt"][:, dt * 512:(dt + 1) * 512]
            rtb = R["rt"].bufs[dt]
            P.step("dve", (lambda e, tf=tf, xf=xf, rt=rt: e.tensor_tensor(
                rt, tf[:], xf[:], op=ALU.add)),
                reads=(tf.b, xf.b), writes=(rtb,), tag=f"{tag}r{qt}.{dt}")
        rt_views = [R["rt"][:, dg * 512:(dg + 1) * 512] for dg in range(16)]
        xout_blk = [xout[dg * 128:(dg + 1) * 128, tcols] for dg in range(16)]
        xo_bufs = [tuple(xout_bufs[qt * 4 + i] for i in range(4))
                   for dg in range(16)]
        banks = {"s1": B[0], "s2": B[1], "c0": B[2], "c1": B[3]}
        emit_ln_store(P, R, rt_views, R["rt"].bufs, xout_blk, xo_bufs, banks,
                      tag=f"{tag}ln{qt}.")


def alloc_attn_res(ctx, nc, R):
    R["x1bf"] = SBuf(ctx, nc, "x1bf", [128, 8192], BF16)
    R["wqk"] = SBuf(ctx, nc, "wqk", [128, 2048], BF16)
    R["wvc"] = SBuf(ctx, nc, "wvc", [128, 8192], BF16)
    R["woc"] = SBuf(ctx, nc, "woc", [128, 2048], BF16)
    R["qraw"] = SBuf(ctx, nc, "qraw", [128, 512], BF16)
    R["swb"] = SBuf(ctx, nc, "swb", [128, 512], BF16)
    R["rt1"] = SBuf(ctx, nc, "rt1", [128, 512], BF16)
    R["rt2"] = SBuf(ctx, nc, "rt2", [128, 512], BF16)
    R["qkarena"] = SBuf(ctx, nc, "qkarena", [128, 512], BF16)
    R["vsb0"] = SBuf(ctx, nc, "vsb0", [128, 512], BF16)
    R["vsb1"] = SBuf(ctx, nc, "vsb1", [128, 512], BF16)
    R["qtile"] = SBuf(ctx, nc, "qtile", [128, 512], BF16)
    R["kh"] = SBuf(ctx, nc, "kh", [128, 2048], BF16)
    R["vh"] = SBuf(ctx, nc, "vh", [128, 2048], BF16)
    R["pt0"] = SBuf(ctx, nc, "pt0", [128, 512], BF16)
    R["pt1"] = SBuf(ctx, nc, "pt1", [128, 512], BF16)
    R["linv"] = SBuf(ctx, nc, "linv", [1, 512], F32)
    R["linv_s"] = SBuf(ctx, nc, "linv_s", [128, 512], F32)
    R["attT"] = SBuf(ctx, nc, "attT", [128, 16 * 512], BF16, ntiles=16)
    R["rt"] = SBuf(ctx, nc, "rt", [128, 16 * 512], F32, ntiles=16)
    for nm in ["tf0", "xf0"]:
        R[nm] = SBuf(ctx, nc, nm, [128, 512], F32)
    for nm in ["ln_rb0", "ln_rq0"]:
        R[nm] = SBuf(ctx, nc, nm, [128, 512], BF16)
    for nm in ["ln_xn0", "ln_c0s", "ln_c1s"]:
        R[nm] = SBuf(ctx, nc, nm, [128, 512], F32)
    for nm in ["ln_mean", "ln_ex2", "ln_var", "ln_c0"]:
        R[nm] = SBuf(ctx, nc, nm, [1, 512], F32)


def build_attn_program(xin_name="s2", xout_name="s3"):
    nc = bass.Bass("TRN2", target_bir_lowering=False, debug=False)
    xin = nc.dram_tensor(xin_name, [D, T], F32, kind="ExternalInput").ap()
    wq = nc.dram_tensor("wqT", [D, D], F32, kind="ExternalInput").ap()
    wk = nc.dram_tensor("wkT", [D, D], F32, kind="ExternalInput").ap()
    wv = nc.dram_tensor("wvT", [D, D], F32, kind="ExternalInput").ap()
    wo = nc.dram_tensor("woT", [D, D], F32, kind="ExternalInput").ap()
    biases = nc.dram_tensor("attn_biases", [128, 64], F32, kind="ExternalInput").ap()
    cos_d = nc.dram_tensor("rope_ctab", [128, T], F32, kind="ExternalInput").ap()
    sin_d = nc.dram_tensor("rope_stab", [128, T], F32, kind="ExternalInput").ap()
    swap_d = nc.dram_tensor("rope_swap", [128, 128], F32, kind="ExternalInput").ap()
    masks_d = nc.dram_tensor("causal_masks", [128, 2048], F32, kind="ExternalInput").ap()
    onesc = nc.dram_tensor("ones_col", [128, 1], F32, kind="ExternalInput").ap()
    onesr = nc.dram_tensor("ones_row", [1, 128], F32, kind="ExternalInput").ap()
    cvec = nc.dram_tensor("cvec", [128, 8], F32, kind="ExternalInput").ap()
    bvrow = nc.dram_tensor("bv_row", [1, 2048], F32, kind="ExternalInput").ap()
    xout = nc.dram_tensor(xout_name, [D, T], F32, kind="ExternalOutput").ap()
    qs_s = nc.dram_tensor("qs_s", [D, T], BF16).ap()
    ks_s = nc.dram_tensor("ks_s", [D, T], BF16).ap()
    vs_s = nc.dram_tensor("vs_s", [T, D], BF16).ap()

    ctx = ExitStack()
    P = Prog()
    R = Res()
    for s in ("w", "x", "st", "c"):
        R["sem_" + s] = P.new_dma_sem(s)
    alloc_attn_res(ctx, nc, R)
    B = [PSum(ctx, nc, f"bank{i}", [128, 512], F32) for i in range(8)]
    R["ones_col_bf"] = SBuf(ctx, nc, "ones_col_bf", [128, 1], BF16)
    R["ones_row_f"] = SBuf(ctx, nc, "ones_row_f", [1, 128], F32)
    R["cvec"] = SBuf(ctx, nc, "cvec_s", [128, 8], F32)
    R["ctab"] = SBuf(ctx, nc, "ctab", [128, T], BF16)
    R["stab"] = SBuf(ctx, nc, "stab", [128, T], BF16)
    R["swap_bf"] = SBuf(ctx, nc, "swap_bf", [128, 128], BF16)
    R["masks"] = SBuf(ctx, nc, "masks", [128, 2048], BF16)
    R["bq"] = SBuf(ctx, nc, "bq", [128, 16], F32)
    R["bk"] = SBuf(ctx, nc, "bk", [128, 16], F32)
    R["bo"] = SBuf(ctx, nc, "bo", [128, 16], F32)
    R["bv"] = SBuf(ctx, nc, "bv", [1, 2048], F32)
    for nm, src_ in (("ones_col_bf", onesc), ("ones_row_f", onesr),
                     ("cvec", cvec), ("ctab", cos_d), ("stab", sin_d),
                     ("swap_bf", swap_d), ("masks", masks_d), ("bv", bvrow)):
        P.dma("pool" if nm in ("ones_col_bf", "masks", "ctab", "stab",
                               "swap_bf") else "sp",
              (lambda e, nm=nm, src_=src_: e.dma_start(out=R[nm][:], in_=src_)),
              R.sem_c, writes=(R[nm].b,), tag="c_" + nm)
    for i, nm in enumerate(("bq", "bk", "bo")):
        P.dma("sp", (lambda e, nm=nm, i=i: e.dma_start(
            out=R[nm][:], in_=biases[:, i * 16:(i + 1) * 16])),
            R.sem_c, writes=(R[nm].b,), tag="c_" + nm)

    io = {"wq": wq, "wk": wk, "wv": wv, "wo": wo,
          "qs_s": qs_s, "ks_s": ks_s, "vs_s": vs_s,
          "qs_bufs": dram_bufs(64, "qs"), "ks_bufs": dram_bufs(64, "ks"),
          "vs_bufs": dram_bufs(16, "vs")}
    xin_b = dram_bufs(16, "xin")
    xout_b = dram_bufs(16, "xout")
    emit_attn(P, R, B, nc, xin, xin_b, io, xout, xout_b)
    build_engine_sections(nc, P, ctx)
    ctx.close()
    return nc



def emit_moe(P, R, B, nc, xin, xin_bufs, io, xout, xout_bufs, tag="m"):
    xin_r = xin.rearrange("(g p) c -> p g c", p=128)
    for tau in range(4):
        tcols = slice(tau * 512, (tau + 1) * 512)
        tb = tuple(xin_bufs[tau * 4 + i] for i in range(4))
        x3bf = R["x1bf"]
        P.dma("pool", (lambda e, x3bf=x3bf, tcols=tcols: e.dma_start(
            out=x3bf[:], in_=xin_r[:, :, tcols])), R.sem_x,
            reads=tb, writes=(x3bf.b,), tag=f"{tag}xbf{tau}")
        psl = B[0]
        for dg in range(16):
            xf = R["xf0" if dg % 2 == 0 else "xf1"]
            P.dma("sp", (lambda e, xf=xf, dg=dg, tcols=tcols: e.dma_start(
                out=xf[:], in_=xin[dg * 128:(dg + 1) * 128, tcols])),
                R.sem_x, reads=tb, writes=(xf.b,), tag=f"{tag}lgx{tau}.{dg}")
            P.step("pe", (lambda e, xf=xf, dg=dg: e.matmul(
                psl[0:8, 0:512], R["gwf"][:, dg * 8:(dg + 1) * 8], xf[:],
                start=(dg == 0), stop=(dg == 15))),
                reads=(xf.b, R["gwf"].b), writes=(psl.b,),
                tag=f"{tag}lg{tau}.{dg}")
        lg = R["lgs"]
        P.step("act", (lambda e: e.activation(lg[:], psl[0:8, 0:512], AF.Copy)),
               reads=(psl.b,), writes=(lg.b,), tag=f"{tag}lge{tau}")
        for b4 in range(4):
            pst = B[1]
            P.step("pe", (lambda e, b4=b4, pst=pst: e.transpose(
                pst[0:128, 0:8], lg[0:8, b4 * 128:(b4 + 1) * 128],
                R["ident_f"][0:8, 0:8])),
                reads=(lg.b, R["ident_f"].b), writes=(pst.b,),
                tag=f"{tag}tt{tau}.{b4}")
            lgt = R["lgt"]
            P.step("act", (lambda e, lgt=lgt, pst=pst: e.activation(
                lgt[:], pst[0:128, 0:8], AF.Copy)),
                reads=(pst.b,), writes=(lgt.b,), tag=f"{tag}tte{tau}.{b4}")
            m1, m2, g1 = R["m1"], R["m2"], R["g1"]
            lg2, eq1, eq2, gts = R["lg2"], R["eq1"], R["eq2"], R["gts"]
            P.step("dve", (lambda e, lgt=lgt: e.reduce_max(
                m1[:], lgt[:], axis=AX.X)),
                reads=(lgt.b,), writes=(m1.b,), tag=f"{tag}m1{tau}.{b4}")
            P.step("dve", (lambda e, lgt=lgt: e.tensor_scalar(
                eq1[:], lgt[:], m1[:], None, ALU.is_ge)),
                reads=(lgt.b, m1.b), writes=(eq1.b,), tag=f"{tag}e1{tau}.{b4}")
            P.step("dve", (lambda e, lgt=lgt: e.scalar_tensor_tensor(
                lg2[:], eq1[:], -1e30, lgt[:], op0=ALU.mult, op1=ALU.add)),
                reads=(eq1.b, lgt.b), writes=(lg2.b,), tag=f"{tag}l2{tau}.{b4}")
            P.step("dve", (lambda e: e.reduce_max(
                m2[:], lg2[:], axis=AX.X)),
                reads=(lg2.b,), writes=(m2.b,), tag=f"{tag}m2{tau}.{b4}")
            P.step("dve", (lambda e: e.tensor_scalar(
                eq2[:], lg2[:], m2[:], None, ALU.is_ge)),
                reads=(lg2.b, m2.b), writes=(eq2.b,), tag=f"{tag}e2{tau}.{b4}")
            P.step("dve", (lambda e: e.tensor_tensor(
                g1[:], m1[:], m2[:], op=ALU.subtract)),
                reads=(m1.b, m2.b), writes=(g1.b,), tag=f"{tag}d{tau}.{b4}")
            P.step("act", (lambda e: e.activation(g1[:], g1[:], AF.Sigmoid)),
                   reads=(g1.b,), writes=(g1.b,), tag=f"{tag}sg{tau}.{b4}")
            P.step("dve", (lambda e: e.scalar_tensor_tensor(
                m2[:], g1[:], -1.0, R["onecol"][:], op0=ALU.mult, op1=ALU.add)),
                reads=(g1.b, R["onecol"].b), writes=(m2.b,), tag=f"{tag}g2{tau}.{b4}")
            P.step("dve", (lambda e: e.tensor_scalar(
                eq2[:], eq2[:], m2[:], None, ALU.mult)),
                reads=(m2.b, eq2.b), writes=(eq2.b,), tag=f"{tag}q2{tau}.{b4}")
            P.step("dve", (lambda e: e.scalar_tensor_tensor(
                gts[:], eq1[:], g1[:], eq2[:], op0=ALU.mult, op1=ALU.add)),
                reads=(eq1.b, g1.b, eq2.b), writes=(gts.b,), tag=f"{tag}g{tau}.{b4}")
            pst2 = B[2]
            P.step("pe", (lambda e, pst2=pst2: e.transpose(
                pst2[0:8, 0:128], gts[:], R["ident_f"][:])),
                reads=(gts.b, R["ident_f"].b), writes=(pst2.b,),
                tag=f"{tag}tb{tau}.{b4}")
            P.step("act", (lambda e, b4=b4, pst2=pst2: e.activation(
                R["gatesT"][0:8, b4 * 128:(b4 + 1) * 128], pst2[0:8, 0:128],
                AF.Copy)),
                reads=(pst2.b,), writes=(R["gatesT"].b,), tag=f"{tag}gT{tau}.{b4}")
        for e_ in range(8):
            psr = B[1 + e_ % 2]
            P.step("pe", (lambda e, e_=e_, psr=psr: e.matmul(
                psr[:, 0:512], R["sel8"][:, e_ * 128:(e_ + 1) * 128],
                R["gatesT"][0:8, 0:512], start=True, stop=True)),
                reads=(R["gatesT"].b, R["sel8"].b), writes=(psr.b,),
                tag=f"{tag}rep{tau}.{e_}")
            gv = R["geb"][:, e_ * 512:(e_ + 1) * 512]
            P.step("act", (lambda e, gv=gv, psr=psr: e.activation(
                gv, psr[:, 0:512], AF.Copy)),
                reads=(psr.b,), writes=(R["geb"].bufs[e_],), tag=f"{tag}repe{tau}.{e_}")
        for e_ in range(8):
            for ht in range(8):
                w1c = R["w1c"]
                src = io["w1"][e_, :, ht * 128:(ht + 1) * 128].rearrange(
                    "(g p) c -> p g c", p=128)
                P.dma("pool", (lambda e, w1c=w1c, src=src: e.dma_start(
                    out=w1c[:].rearrange("p (g c) -> p g c", c=128), in_=src)),
                    R.sem_w, writes=(w1c.b,), tag=f"{tag}w1{tau}.{e_}.{ht}")
                psh = B[3 + ht % 2]
                def mmh(e, w1c=w1c, x3bf=x3bf, psh=psh):
                    for dg in range(16):
                        inst = e.matmul(psh[:, 0:512],
                                        w1c[:, dg * 128:(dg + 1) * 128],
                                        x3bf[:, dg * 512:(dg + 1) * 512],
                                        start=(dg == 0), stop=(dg == 15))
                    return inst
                P.step("pe", mmh, reads=(w1c.b, x3bf.b), writes=(psh.b,),
                       tag=f"{tag}h{tau}.{e_}.{ht}")
                hbt = R["hbt0" if ht % 2 == 0 else "hbt1"]
                P.step("act", (lambda e, hbt=hbt, psh=psh, e_=e_, ht=ht:
                               e.activation(hbt[:], psh[:, 0:512], AF.Gelu,
                                            bias=R["b1s"][:, e_ * 8 + ht:e_ * 8 + ht + 1])),
                       reads=(psh.b, R["b1s"].b), writes=(hbt.b,),
                       tag=f"{tag}ge{tau}.{e_}.{ht}")
                hidx = e_ * 8 + ht
                hv = R["harena"][:, hidx * 512:(hidx + 1) * 512]
                P.step("dve",
                       (lambda e, hbt=hbt, hv=hv, e_=e_: e.tensor_tensor(
                           hv, hbt[:], R["geb"][:, e_ * 512:(e_ + 1) * 512],
                           op=ALU.mult)),
                       reads=(hbt.b, R["geb"].bufs[e_]),
                       writes=(R["harena"].bufs[hidx],), tag=f"{tag}hg{tau}.{hidx}")
        for st in range(16):
            swc = R["w1c"]
            src = io["sw1"][:, st * 128:(st + 1) * 128].rearrange(
                "(g p) c -> p g c", p=128)
            P.dma("pool", (lambda e, swc=swc, src=src: e.dma_start(
                out=swc[:].rearrange("p (g c) -> p g c", c=128), in_=src)),
                R.sem_w, writes=(swc.b,), tag=f"{tag}sw1{tau}.{st}")
            psh = B[3 + st % 2]
            def mms(e, swc=swc, x3bf=x3bf, psh=psh):
                for dg in range(16):
                    inst = e.matmul(psh[:, 0:512],
                                    swc[:, dg * 128:(dg + 1) * 128],
                                    x3bf[:, dg * 512:(dg + 1) * 512],
                                    start=(dg == 0), stop=(dg == 15))
                return inst
            P.step("pe", mms, reads=(swc.b, x3bf.b), writes=(psh.b,),
                   tag=f"{tag}sh{tau}.{st}")
            shv = R["sharena"][:, st * 512:(st + 1) * 512]
            P.step("act", (lambda e, shv=shv, psh=psh, st=st: e.activation(
                shv, psh[:, 0:512], AF.Gelu, bias=R["sb1s"][:, st:st + 1])),
                reads=(psh.b, R["sb1s"].b), writes=(R["sharena"].bufs[st],),
                tag=f"{tag}she{tau}.{st}")
        for dt in range(16):
            w2c = R["w2c"]
            src = io["w2"][:, :, dt * 128:(dt + 1) * 128].rearrange(
                "E (g p) c -> p (E g) c", p=128)
            P.dma("pool", (lambda e, w2c=w2c, src=src: e.dma_start(
                out=w2c[:].rearrange("p (a c) -> p a c", c=128), in_=src)),
                R.sem_w, writes=(w2c.b,), tag=f"{tag}w2{tau}.{dt}")
            sw2c = R["sw2c"]
            src2 = io["sw2"][:, dt * 128:(dt + 1) * 128].rearrange(
                "(g p) c -> p g c", p=128)
            P.dma("pool", (lambda e, sw2c=sw2c, src2=src2: e.dma_start(
                out=sw2c[:].rearrange("p (g c) -> p g c", c=128), in_=src2)),
                R.sem_w, writes=(sw2c.b,), tag=f"{tag}sw2{tau}.{dt}")
            pso = B[5 + dt % 2]
            def mmo(e, w2c=w2c, sw2c=sw2c, dt=dt, pso=pso):
                for k in range(64):
                    e.matmul(pso[:, 0:512], w2c[:, k * 128:(k + 1) * 128],
                             R["harena"][:, k * 512:(k + 1) * 512],
                             start=(k == 0), stop=False)
                e.matmul(pso[:, 0:512], R["b2s"][:, dt * 128:(dt + 1) * 128],
                         R["gatesT"][0:8, 0:512], start=False, stop=False)
                for g in range(16):
                    e.matmul(pso[:, 0:512], sw2c[:, g * 128:(g + 1) * 128],
                             R["sharena"][:, g * 512:(g + 1) * 512],
                             start=False, stop=False)
                inst = e.matmul(pso[:, 0:512],
                                R["sb2s"][0:1, dt * 128:(dt + 1) * 128],
                                R["ones512"][:], start=False, stop=True)
                return inst
            P.step("pe", mmo,
                   reads=(w2c.b, sw2c.b, R["b2s"].b, R["gatesT"].b,
                          R["sb2s"].b, R["ones512"].b)
                   + tuple(R["harena"].bufs) + tuple(R["sharena"].bufs),
                   writes=(pso.b,), tag=f"{tag}o{tau}.{dt}")
            tf = R["tf0"]
            P.step("act", (lambda e, tf=tf, pso=pso: e.activation(
                tf[:], pso[:, 0:512], AF.Copy)),
                reads=(pso.b,), writes=(tf.b,), tag=f"{tag}te{tau}.{dt}")
            xf = R["xf0" if dt % 2 == 0 else "xf1"]
            P.dma("sp", (lambda e, xf=xf, dt=dt, tcols=tcols: e.dma_start(
                out=xf[:], in_=xin[dt * 128:(dt + 1) * 128, tcols])),
                R.sem_x, reads=tb, writes=(xf.b,), tag=f"{tag}xo{tau}.{dt}")
            rt = R["rt"][:, dt * 512:(dt + 1) * 512]
            P.step("dve", (lambda e, tf=tf, xf=xf, rt=rt: e.tensor_tensor(
                rt, tf[:], xf[:], op=ALU.add)),
                reads=(tf.b, xf.b), writes=(R["rt"].bufs[dt],),
                tag=f"{tag}r{tau}.{dt}")
        rt_views = [R["rt"][:, dg * 512:(dg + 1) * 512] for dg in range(16)]
        xout_blk = [xout[dg * 128:(dg + 1) * 128, tcols] for dg in range(16)]
        xo_bufs = [tuple(xout_bufs[tau * 4 + i] for i in range(4))
                   for dg in range(16)]
        banks = {"s1": B[0], "s2": B[1], "c0": B[2], "c1": B[3]}
        emit_ln_store(P, R, rt_views, R["rt"].bufs, xout_blk, xo_bufs, banks,
                      tag=f"{tag}ln{tau}.")


def alloc_moe_res(ctx, nc, R):
    R["x1bf"] = SBuf(ctx, nc, "x1bf", [128, 8192], BF16)
    R["harena"] = SBuf(ctx, nc, "harena", [128, 64 * 512], BF16, ntiles=64)
    R["sharena"] = SBuf(ctx, nc, "sharena", [128, 16 * 512], BF16, ntiles=16)
    R["geb"] = SBuf(ctx, nc, "geb", [128, 8 * 512], BF16, ntiles=8)
    R["rt"] = SBuf(ctx, nc, "rt", [128, 16 * 512], F32, ntiles=16)
    R["w1c"] = SBuf(ctx, nc, "w1c", [128, 2048], BF16)
    R["w2c"] = SBuf(ctx, nc, "w2c", [128, 8192], BF16)
    R["sw2c"] = SBuf(ctx, nc, "sw2c", [128, 2048], BF16)
    R["gwf"] = SBuf(ctx, nc, "gwf", [128, 128], F32)
    R["lgs"] = SBuf(ctx, nc, "lgs", [8, 512], F32)
    R["lgt"] = SBuf(ctx, nc, "lgt", [128, 8], F32)
    R["gatesT"] = SBuf(ctx, nc, "gatesT", [8, 512], BF16)
    for nm in ["m1", "m2", "g1"]:
        R[nm] = SBuf(ctx, nc, nm, [128, 1], F32)
    for nm in ["lg2", "eq1", "eq2"]:
        R[nm] = SBuf(ctx, nc, nm, [128, 8], F32)
    R["gts"] = SBuf(ctx, nc, "gts", [128, 8], F32)
    R["onecol"] = SBuf(ctx, nc, "onecol", [128, 1], F32)
    R["hbt0"] = SBuf(ctx, nc, "hbt0", [128, 512], BF16)
    R["hbt1"] = SBuf(ctx, nc, "hbt1", [128, 512], BF16)
    for nm in ["tf0", "xf0", "xf1"]:
        R[nm] = SBuf(ctx, nc, nm, [128, 512], F32)
    R["b1s"] = SBuf(ctx, nc, "b1s", [128, 64], F32)
    R["sb1s"] = SBuf(ctx, nc, "sb1s", [128, 16], F32)
    R["b2s"] = SBuf(ctx, nc, "b2s", [8, 2048], BF16)
    R["sb2s"] = SBuf(ctx, nc, "sb2s", [1, 2048], BF16)
    R["ones512"] = SBuf(ctx, nc, "ones512", [1, 512], BF16)
    R["ones_row_bf"] = SBuf(ctx, nc, "ones_row_bf", [1, 128], BF16)
    R["sel8"] = SBuf(ctx, nc, "sel8_s", [8, 1024], BF16)
    R["ident_f"] = SBuf(ctx, nc, "ident_f", [128, 128], F32)
    R["ident_bf"] = SBuf(ctx, nc, "ident_bf", [128, 128], BF16)
    for nm in ["ln_rb0", "ln_rq0"]:
        R[nm] = SBuf(ctx, nc, nm, [128, 512], BF16)
    for nm in ["ln_xn0", "ln_c0s", "ln_c1s"]:
        R[nm] = SBuf(ctx, nc, nm, [128, 512], F32)
    for nm in ["ln_mean", "ln_ex2", "ln_var", "ln_c0"]:
        R[nm] = SBuf(ctx, nc, nm, [1, 512], F32)


def build_moe_program(xin_name="s3", xout_name="s4"):
    nc = bass.Bass("TRN2", target_bir_lowering=False, debug=False)
    xin = nc.dram_tensor(xin_name, [D, T], F32, kind="ExternalInput").ap()
    gate_w = nc.dram_tensor("L1_ffn_gate_w", [2048, 8], F32, kind="ExternalInput").ap()
    w1 = nc.dram_tensor("L1_ffn_w1", [8, 2048, 1024], F32, kind="ExternalInput").ap()
    b1 = nc.dram_tensor("moe_b1", [128, 64], F32, kind="ExternalInput").ap()
    w2 = nc.dram_tensor("L1_ffn_w2", [8, 1024, 2048], F32, kind="ExternalInput").ap()
    b2 = nc.dram_tensor("L1_ffn_b2", [8, 2048], F32, kind="ExternalInput").ap()
    sw1 = nc.dram_tensor("L1_ffn_shared_w1", [2048, 2048], F32, kind="ExternalInput").ap()
    sb1 = nc.dram_tensor("moe_sb1", [128, 16], F32, kind="ExternalInput").ap()
    sw2 = nc.dram_tensor("L1_ffn_shared_w2", [2048, 2048], F32, kind="ExternalInput").ap()
    sb2 = nc.dram_tensor("moe_sb2", [1, 2048], F32, kind="ExternalInput").ap()
    onesc = nc.dram_tensor("ones_col", [128, 1], F32, kind="ExternalInput").ap()
    onesr = nc.dram_tensor("ones_row", [1, 128], F32, kind="ExternalInput").ap()
    ones5 = nc.dram_tensor("ones_5", [1, 512], F32, kind="ExternalInput").ap()
    cvec = nc.dram_tensor("cvec", [128, 8], F32, kind="ExternalInput").ap()
    ident = nc.dram_tensor("ident", [128, 128], F32, kind="ExternalInput").ap()
    sel8_d = nc.dram_tensor("sel8", [8, 1024], F32, kind="ExternalInput").ap()
    xout = nc.dram_tensor(xout_name, [D, T], F32, kind="ExternalOutput").ap()

    ctx = ExitStack()
    P = Prog()
    R = Res()
    for s in ("w", "x", "st", "c"):
        R["sem_" + s] = P.new_dma_sem(s)
    alloc_moe_res(ctx, nc, R)
    B = [PSum(ctx, nc, f"bank{i}", [128, 512], F32) for i in range(8)]
    R["ones_col_bf"] = SBuf(ctx, nc, "ones_col_bf", [128, 1], BF16)
    R["ones_row_f"] = SBuf(ctx, nc, "ones_row_f", [1, 128], F32)
    R["cvec"] = SBuf(ctx, nc, "cvec_s", [128, 8], F32)
    for nm, src_, q in (("ones_col_bf", onesc, "pool"), ("ones_row_f", onesr, "sp"),
                        ("ones_row_bf", onesr, "pool"), ("cvec", cvec, "sp"),

                        ("b1s", b1, "sp"), ("sb1s", sb1, "sp"),
                        ("b2s", b2, "pool"), ("sb2s", sb2, "pool"),
                        ("ones512", ones5, "pool"), ("ident_f", ident, "sp"),
                        ("ident_bf", ident, "pool"), ("onecol", onesc, "sp"),
                        ("sel8", sel8_d, "pool")):
        P.dma(q, (lambda e, nm=nm, src_=src_: e.dma_start(out=R[nm][:], in_=src_)),
              R.sem_c, writes=(R[nm].b,), tag="c_" + nm)

    P.dma("sp", (lambda e: e.dma_start(
        out=R["gwf"][:].rearrange("p (g c) -> p g c", c=8),
        in_=gate_w.rearrange("(g p) c -> p g c", p=128))),
        R.sem_c, writes=(R["gwf"].b,), tag="c_gwf")
    io = {"w1": w1, "w2": w2, "sw1": sw1, "sw2": sw2}
    xin_b = dram_bufs(16, "xin")
    xout_b = dram_bufs(16, "xout")
    emit_moe(P, R, B, nc, xin, xin_b, io, xout, xout_b)
    build_engine_sections(nc, P, ctx)
    ctx.close()
    return nc



TA = 512
NCA = TA // 128


def _rw_consts_A(P, ctx2, nc, R, dram):
    R["maa_w1"] = SBuf(ctx2, nc, "maa_w1s", [128, 16 * 160], BF16)
    R["decay_w1"] = SBuf(ctx2, nc, "decay_w1s", [128, 16 * 64], BF16)
    R["decay_w2"] = SBuf(ctx2, nc, "decay_w2s", [64, 2048], BF16)
    R["maa_w2a"] = SBuf(ctx2, nc, "maa_w2as", [96, 2048], BF16)
    R["maa_w2b"] = SBuf(ctx2, nc, "maa_w2bs", [64, 2048], BF16)
    for nm in ["maax", "maar", "maak", "maav", "maaw", "maag", "tdvec", "bg"]:
        R[nm] = SBuf(ctx2, nc, nm + "_s", [128, 16], F32)
    for nm, src_ in (("decay_w2", dram["decay_w2"]), ("maa_w2a", dram["maa_w2a"]),
                     ("maa_w2b", dram["maa_w2b"])):
        P.dma("pool", (lambda e, nm=nm, src_=src_, R=R: e.dma_start(
            out=R[nm][:], in_=src_)), R.sem_c, writes=(R[nm].b,), tag="rcA" + nm)
    P.dma("pool", (lambda e, R=R: e.dma_start(
        out=R["maa_w1"][:].rearrange("p (g c) -> p g c", c=160),
        in_=dram["maa_w1"].rearrange("(g p) c -> p g c", p=128))),
        R.sem_c, writes=(R["maa_w1"].b,), tag="rcAw1")
    P.dma("pool", (lambda e, R=R: e.dma_start(
        out=R["decay_w1"][:].rearrange("p (g c) -> p g c", c=64),
        in_=dram["decay_w1"].rearrange("(g p) c -> p g c", p=128))),
        R.sem_c, writes=(R["decay_w1"].b,), tag="rcAdw1")
    for i, nm in enumerate(["maax", "maar", "maak", "maav", "maaw", "maag",
                            "tdvec", "bg"]):
        P.dma("sp", (lambda e, nm=nm, i=i, R=R: e.dma_start(
            out=R[nm][:], in_=dram["vecs"][:, i * 16:(i + 1) * 16])),
            R.sem_c, writes=(R[nm].b,), tag="rcA" + nm)


def emit_rwkv_slot(P, nc, B, semn, dram, io, xin, xin_bufs, xout, xout_bufs,
                   eLca, eLca_buf, tag="w"):
    xin_r = xin.rearrange("(g p) c -> p g c", p=128)
    mixb = {f: dram_bufs(4, f"mx{f}") for f in range(4)}
    elab = dram_bufs(4, "elab")
    ekfb = dram_bufs(4, "ekfb")
    qdb = dram_bufs(8, "qdb")
    kdb = dram_bufs(8, "kdb")
    gb_ = dram_bufs(8, "gb")
    vb_ = dram_bufs(16, "vb")
    ygb = dram_bufs(4, "ygb")

    with ExitStack() as ctx2:
        _NAME_PREFIX[0] = "rwA_"
        RA = Res()
        RA.update(semn)
        _rw_consts_A(P, ctx2, nc, RA, dram)
        RA["x513"] = SBuf(ctx2, nc, "x513", [128, 16 * (TA + 1)], F32)
        RA["mixa"] = SBuf(ctx2, nc, "mixa", [128, 16 * TA], BF16, ntiles=16)
        RA["eLt"] = SBuf(ctx2, nc, "eLt", [128, 16 * TA], BF16, ntiles=16)
        RA["ekt"] = SBuf(ctx2, nc, "ekt", [128, 16 * TA], BF16, ntiles=16)
        RA["t5a"] = SBuf(ctx2, nc, "t5a", [96, TA], BF16)
        RA["t5b"] = SBuf(ctx2, nc, "t5b", [64, TA], BF16)
        RA["tdb"] = SBuf(ctx2, nc, "tdb", [64, TA], BF16)
        for nm in ["ptile", "satile", "etile", "untile", "epntile",
                   "uf0", "uf1", "tf0", "tf1", "dx0", "dx1"]:
            RA[nm] = SBuf(ctx2, nc, nm, [128, TA], F32)
        for tau in range(4):
            tcols = slice(tau * TA, (tau + 1) * TA)
            tb = tuple(xin_bufs[tau * 4 + i] for i in range(4))
            x513 = RA["x513"]
            x5v = x513[:].rearrange("p (g c) -> p g c", c=TA + 1)
            if tau == 0:
                P.step("dve", (lambda e, x5v=x5v: e.memset(x5v[:, :, 0:1], 0.0)),
                       writes=(x513.b,), tag=f"{tag}x0")
                P.dma("sp", (lambda e, x5v=x5v: e.dma_start(
                    out=x5v[:, :, 1:TA + 1], in_=xin_r[:, :, 0:TA])), RA.sem_x,
                    reads=tb, writes=(x513.b,), tag=f"{tag}xl{tau}")
            else:
                src = xin_r[:, :, tau * TA - 1:tau * TA + TA]
                P.dma("sp", (lambda e, x5v=x5v, src=src: e.dma_start(
                    out=x5v, in_=src)), RA.sem_x,
                    reads=tb + (xin_bufs[tau * 4 - 1],), writes=(x513.b,),
                    tag=f"{tag}xl{tau}")
            def xv(g):
                return x5v[:, g, 1:TA + 1]
            mixa = RA["mixa"]
            dxt = [RA["dx0"], RA["dx1"]]
            for g in range(16):
                dd = dxt[g % 2]
                P.step("dve", (lambda e, g=g, dd=dd: e.tensor_tensor(
                    dd[:], x5v[:, g, 0:TA], xv(g), op=ALU.subtract)),
                    reads=(x513.b,), writes=(dd.b,), tag=f"{tag}dx{tau}.{g}")
                P.step("dve", (lambda e, g=g, dd=dd: e.scalar_tensor_tensor(
                    mixa[:, g * TA:(g + 1) * TA], dd[:],
                    RA["maax"][:, g:g + 1], xv(g), op0=ALU.mult, op1=ALU.add)),
                    reads=(dd.b, RA["maax"].b, x513.b),
                    writes=(mixa.bufs[g],), tag=f"{tag}xx{tau}.{g}")
            psA6, psB7 = B[6], B[7]
            def t5mm(e):
                for g in range(16):
                    e.matmul(psA6[0:96, 0:TA],
                             RA["maa_w1"][:, g * 160:g * 160 + 96],
                             mixa[:, g * TA:(g + 1) * TA],
                             start=(g == 0), stop=(g == 15))
                for g in range(16):
                    inst = e.matmul(psB7[0:64, 0:TA],
                                    RA["maa_w1"][:, g * 160 + 96:g * 160 + 160],
                                    mixa[:, g * TA:(g + 1) * TA],
                                    start=(g == 0), stop=(g == 15))
                return inst
            P.step("pe", t5mm, reads=tuple(mixa.bufs) + (RA["maa_w1"].b,),
                   writes=(psA6.b, psB7.b), tag=f"{tag}t5{tau}")
            t5a, t5b = RA["t5a"], RA["t5b"]
            P.step("act", (lambda e: e.activation(t5a[:], psA6[0:96, 0:TA],
                                                  AF.Tanh)),
                   reads=(psA6.b,), writes=(t5a.b,), tag=f"{tag}t5a{tau}")
            P.step("act", (lambda e: e.activation(t5b[:], psB7[0:64, 0:TA],
                                                  AF.Tanh)),
                   reads=(psB7.b,), writes=(t5b.b,), tag=f"{tag}t5b{tau}")

            def emit_mix(f, maa, mtag):
                if f < 3:
                    t5src, t5base = t5a, f * 32
                    w2src, rbase = RA["maa_w2a"], f * 32
                else:
                    t5src, t5base = t5b, (f - 3) * 32
                    w2src, rbase = RA["maa_w2b"], (f - 3) * 32
                for g in range(16):
                    psm = B[6 + g % 2]
                    P.step("pe", (lambda e, g=g, psm=psm, t5src=t5src,
                                  t5base=t5base, w2src=w2src, rbase=rbase:
                                  e.matmul(psm[:, 0:TA],
                                           w2src[rbase:rbase + 32,
                                                 g * 128:(g + 1) * 128],
                                           t5src[t5base:t5base + 32, :],
                                           start=True, stop=True)),
                           reads=(w2src.b, t5src.b), writes=(psm.b,),
                           tag=f"{tag}{mtag}m{tau}.{g}")
                    uf = RA["uf0" if g % 2 == 0 else "uf1"]
                    P.step("act", (lambda e, uf=uf, psm=psm, g=g, maa=maa:
                                   e.activation(uf[:], psm[:, 0:TA],
                                                AF.Identity,
                                                bias=RA[maa][:, g:g + 1])),
                           reads=(psm.b, RA[maa].b), writes=(uf.b,),
                           tag=f"{tag}{mtag}u{tau}.{g}")
                    dd = dxt[g % 2]
                    P.step("dve", (lambda e, g=g, dd=dd: e.tensor_tensor(
                        dd[:], x5v[:, g, 0:TA], xv(g), op=ALU.subtract)),
                        reads=(x513.b,), writes=(dd.b,),
                        tag=f"{tag}{mtag}d{tau}.{g}")
                    tf = RA["tf0" if g % 2 == 0 else "tf1"]
                    P.step("dve", (lambda e, uf=uf, tf=tf, dd=dd:
                                   e.tensor_tensor(tf[:], uf[:], dd[:],
                                                   op=ALU.mult)),
                           reads=(uf.b, dd.b), writes=(tf.b,),
                           tag=f"{tag}{mtag}t{tau}.{g}")
                    P.step("dve", (lambda e, tf=tf, g=g: e.tensor_tensor(
                        mixa[:, g * TA:(g + 1) * TA], tf[:], xv(g),
                        op=ALU.add)),
                        reads=(tf.b, x513.b), writes=(mixa.bufs[g],),
                        tag=f"{tag}{mtag}x{tau}.{g}")

            def store_mix(f, tau):
                dst = dram["mix_s"][f][:, tau * TA:(tau + 1) * TA].rearrange(
                    "(g p) c -> p g c", p=128)
                P.dma("act", (lambda e, dst=dst: e.dma_start(
                    out=dst, in_=RA["mixa"][:].rearrange("p (g c) -> p g c",
                                                        c=TA))),
                    RA.sem_st, reads=tuple(RA["mixa"].bufs),
                    writes=(mixb[f][tau],), tag=f"{tag}ms{f}.{tau}")

            emit_mix(3, "maaw", "w")
            psy1 = B[0]
            def y1mm(e):
                for g in range(16):
                    inst = e.matmul(psy1[0:64, 0:TA],
                                    RA["decay_w1"][:, g * 64:(g + 1) * 64],
                                    mixa[:, g * TA:(g + 1) * TA],
                                    start=(g == 0), stop=(g == 15))
                return inst
            P.step("pe", y1mm, reads=tuple(mixa.bufs) + (RA["decay_w1"].b,),
                   writes=(psy1.b,), tag=f"{tag}y1{tau}")
            tdb = RA["tdb"]
            P.step("act", (lambda e: e.activation(tdb[:], psy1[0:64, 0:TA],
                                                  AF.Tanh)),
                   reads=(psy1.b,), writes=(tdb.b,), tag=f"{tag}td{tau}")
            for g in range(16):
                psw = B[1 + g % 2]
                P.step("pe", (lambda e, g=g, psw=psw: e.matmul(
                    psw[:, 0:TA], RA["decay_w2"][:, g * 128:(g + 1) * 128],
                    tdb[:], start=True, stop=True)),
                    reads=(tdb.b, RA["decay_w2"].b), writes=(psw.b,),
                    tag=f"{tag}wd{tau}.{g}")
                pt = RA["ptile"]
                P.step("act", (lambda e, psw=psw, g=g, pt=pt: e.activation(
                    pt[:], psw[:, 0:TA], AF.Exp, bias=RA["tdvec"][:, g:g + 1])),
                    reads=(psw.b, RA["tdvec"].b), writes=(pt.b,),
                    tag=f"{tag}p{tau}.{g}")
                sa = RA["satile"]
                for n in range(NCA):
                    P.step("dve", (lambda e, pt=pt, sa=sa, n=n:
                                   e.tensor_tensor_scan(
                        sa[:, n * 128:(n + 1) * 128],
                        pt[:, n * 128:(n + 1) * 128],
                        pt[:, n * 128:(n + 1) * 128], 0.0, ALU.add,
                        ALU.bypass)),
                        reads=(pt.b,), writes=(sa.b,),
                        tag=f"{tag}sc{tau}.{g}.{n}")
                P.step("act", (lambda e, sa=sa, g=g: e.activation(
                    RA["eLt"][:, g * TA:(g + 1) * TA], sa[:], AF.Exp,
                    scale=-1.0)),
                    reads=(sa.b,), writes=(RA["eLt"].bufs[g],),
                    tag=f"{tag}eL{tau}.{g}")
                et = RA["etile"]
                P.step("act", (lambda e, sa=sa, et=et: e.activation(
                    et[:], sa[:], AF.Exp)),
                    reads=(sa.b,), writes=(et.b,), tag=f"{tag}eS{tau}.{g}")
                epn = RA["epntile"]
                P.step("act", (lambda e, pt=pt, epn=epn: e.activation(
                    epn[:], pt[:], AF.Exp, scale=-1.0)),
                    reads=(pt.b,), writes=(epn.b,), tag=f"{tag}epn{tau}.{g}")
                un = RA["untile"]
                P.step("dve", (lambda e, epn=epn, un=un: e.tensor_single_scalar(
                    un[:], epn[:], -1.0, ALU.mult)),
                    reads=(epn.b,), writes=(un.b,), tag=f"{tag}un{tau}.{g}")
                P.step("dve", (lambda e, un=un, et=et, g=g:
                               e.scalar_tensor_tensor(
                    RA["ekt"][:, g * TA:(g + 1) * TA], un[:], 1.0, et[:],
                    op0=ALU.add, op1=ALU.mult)),
                    reads=(un.b, et.b), writes=(RA["ekt"].bufs[g],),
                    tag=f"{tag}ekf{tau}.{g}")
                P.step("act", (lambda e, sa=sa, g=g, tau=tau: e.activation(
                    eLca[:, g * 16 + tau * NCA:g * 16 + tau * NCA + NCA],
                    sa[:].rearrange("p (n c) -> p n c", c=128)[:, :, 127:128],
                    AF.Exp, scale=-1.0)),
                    reads=(sa.b,), writes=(eLca_buf,),
                    tag=f"{tag}eLc{tau}.{g}")
            for arena, scratch, bl in (("eLt", "ela_s", elab),
                                       ("ekt", "ekf_s", ekfb)):
                dst = dram[scratch][:, tau * TA:(tau + 1) * TA].rearrange(
                    "(g p) c -> p g c", p=128)
                P.dma("sp", (lambda e, dst=dst, arena=arena: e.dma_start(
                    out=dst, in_=RA[arena][:].rearrange("p (g c) -> p g c",
                                                       c=TA))),
                    RA.sem_st, reads=tuple(RA[arena].bufs),
                    writes=(bl[tau],), tag=f"{tag}es{arena}.{tau}")
            emit_mix(0, "maar", "r"); store_mix(0, tau)
            emit_mix(1, "maak", "k"); store_mix(1, tau)
            emit_mix(2, "maav", "v"); store_mix(2, tau)
            emit_mix(4, "maag", "g"); store_mix(3, tau)
        P.join_barrier()

    with ExitStack() as ctx2:
        _NAME_PREFIX[0] = "rwP_"
        RP = Res()
        RP.update(semn)
        RP["wbig"] = SBuf(ctx2, nc, "wbig", [128, 8 * 2048], BF16, ntiles=8)
        RP["mixt0"] = SBuf(ctx2, nc, "mixt0", [128, 16 * TA], BF16)
        RP["mixt1"] = SBuf(ctx2, nc, "mixt1", [128, 16 * TA], BF16)
        RP["sct0"] = SBuf(ctx2, nc, "sct0", [128, 16 * TA], BF16)
        RP["sct1"] = SBuf(ctx2, nc, "sct1", [128, 16 * TA], BF16)
        RP["oar0"] = SBuf(ctx2, nc, "oar0", [128, 8 * TA], BF16, ntiles=8)
        RP["oar1"] = SBuf(ctx2, nc, "oar1", [128, 8 * TA], BF16, ntiles=8)
        RP["bg"] = SBuf(ctx2, nc, "bg_s2", [128, 16], F32)
        P.dma("sp", (lambda e, RP=RP: e.dma_start(
            out=RP["bg"][:], in_=dram["vecs"][:, 112:128])),
            RP.sem_c, writes=(RP["bg"].b,), tag="rcPbg")

        def wload(g2, slot, wdram):
            src = wdram[:, g2 * 128:(g2 + 1) * 128].rearrange(
                "(g p) c -> p g c", p=128)
            P.dma("pool", (lambda e, slot=slot, src=src: e.dma_start(
                out=RP["wbig"][:, slot * 2048:(slot + 1) * 2048].rearrange(
                    "p (g c) -> p g c", c=128), in_=src)),
                RP.sem_w, writes=(RP["wbig"].bufs[slot],),
                tag=f"{tag}wl{g2}", stream=f"wbig{slot}")

        def proj_pass(pname, wdram, mixf, evac, scload=None):
            for half in range(2):
                for i8 in range(8):
                    wload(half * 8 + i8, i8, wdram)
                for tau in range(4):
                    mt = RP["mixt0" if tau % 2 == 0 else "mixt1"]
                    src = dram["mix_s"][mixf][:, tau * TA:(tau + 1) * TA].rearrange(
                        "(g p) c -> p g c", p=128)
                    P.dma("sp", (lambda e, mt=mt, src=src: e.dma_start(
                        out=mt[:].rearrange("p (g c) -> p g c", c=TA),
                        in_=src)), RP.sem_x, reads=(mixb[mixf][tau],),
                        writes=(mt.b,), tag=f"{tag}{pname}mx{half}.{tau}")
                    sct = None
                    if scload is not None:
                        sct = RP["sct0" if tau % 2 == 0 else "sct1"]
                        src2 = dram[scload[0]][:, tau * TA:(tau + 1) * TA].rearrange(
                            "(g p) c -> p g c", p=128)
                        P.dma("act", (lambda e, sct=sct, src2=src2: e.dma_start(
                            out=sct[:].rearrange("p (g c) -> p g c", c=TA),
                            in_=src2)), RP.sem_x, reads=(scload[1][tau],),
                            writes=(sct.b,), tag=f"{tag}{pname}sc{half}.{tau}")
                    oa = RP["oar0" if tau % 2 == 0 else "oar1"]
                    for i8 in range(8):
                        g2 = half * 8 + i8
                        ps = B[i8 % 2]
                        def mm(e, i8=i8, mt=mt, ps=ps):
                            for g in range(16):
                                inst = e.matmul(
                                    ps[:, 0:TA],
                                    RP["wbig"][:, i8 * 2048 + g * 128:
                                              i8 * 2048 + (g + 1) * 128],
                                    mt[:, g * TA:(g + 1) * TA],
                                    start=(g == 0), stop=(g == 15))
                            return inst
                        P.step("pe", mm, reads=(RP["wbig"].bufs[i8], mt.b),
                               writes=(ps.b,), tag=f"{tag}{pname}p{half}.{tau}.{i8}")
                        evac(g2, i8, tau, ps, oa, sct)
                    dst = dram[pname + "_s"][:, tau * TA:(tau + 1) * TA].rearrange(
                        "(g p) c -> p g c", p=128)[:, half * 8:(half + 1) * 8, :]
                    blist = {"qd": qdb, "kd": kdb, "g": gb_}[pname]
                    P.dma("act", (lambda e, oa=oa, dst=dst: e.dma_start(
                        out=dst, in_=oa[:].rearrange("p (g c) -> p g c",
                                                     c=TA))),
                        RP.sem_st, reads=tuple(oa.bufs),
                        writes=(blist[half * 4 + tau],),
                        tag=f"{tag}{pname}st{half}.{tau}")

        def ev_q(g2, i8, tau, ps, oa, sct):
            P.step("dve", (lambda e, i8=i8, g2=g2, ps=ps, oa=oa, sct=sct:
                           e.tensor_tensor(
                oa[:, i8 * TA:(i8 + 1) * TA],
                sct[:, g2 * TA:(g2 + 1) * TA] if False else
                sct[:].rearrange("p (g c) -> p g c", c=TA)[:, g2, :],
                ps[:, 0:TA], op=ALU.mult)),
                reads=(ps.b, sct.b), writes=(oa.bufs[i8],),
                tag=f"{tag}qd{tau}.{g2}")
        def ev_k(g2, i8, tau, ps, oa, sct):
            P.step("dve", (lambda e, i8=i8, g2=g2, ps=ps, oa=oa, sct=sct:
                           e.tensor_tensor(
                oa[:, i8 * TA:(i8 + 1) * TA],
                sct[:].rearrange("p (g c) -> p g c", c=TA)[:, g2, :],
                ps[:, 0:TA], op=ALU.mult)),
                reads=(ps.b, sct.b), writes=(oa.bufs[i8],),
                tag=f"{tag}kd{tau}.{g2}")
        def ev_g(g2, i8, tau, ps, oa, sct):
            P.step("act", (lambda e, i8=i8, g2=g2, ps=ps, oa=oa: e.activation(
                oa[:, i8 * TA:(i8 + 1) * TA], ps[:, 0:TA],
                AF.Silu, bias=RP["bg"][:, g2:g2 + 1])),
                reads=(ps.b, RP["bg"].b), writes=(oa.bufs[i8],),
                tag=f"{tag}ge{tau}.{g2}")

        proj_pass("qd", io["wq"], 0, ev_q, scload=("ela_s", elab))
        proj_pass("kd", io["wk"], 1, ev_k, scload=("ekf_s", ekfb))
        proj_pass("g", io["wg"], 3, ev_g)
        for dc in range(4):
            wvslot = RP["wbig"]
            src = io["wv"][:, dc * 512:(dc + 1) * 512].rearrange(
                "(g p) c -> p g c", p=128)
            P.dma("pool", (lambda e, dc=dc, src=src: e.dma_start(
                out=wvslot[:, (dc % 2) * 8192:(dc % 2) * 8192 + 8192].rearrange(
                    "p (g c) -> p g c", c=512), in_=src)),
                RP.sem_w, writes=tuple(wvslot.bufs[(dc % 2) * 4:(dc % 2) * 4 + 4]),
                tag=f"{tag}vw{dc}", stream=f"wbigv{dc % 2}")
            for tau in range(4):
                mt = RP["mixt0" if tau % 2 == 0 else "mixt1"]
                src2 = dram["mix_s"][2][:, tau * TA:(tau + 1) * TA].rearrange(
                    "(g p) c -> p g c", p=128)
                P.dma("sp", (lambda e, mt=mt, src2=src2: e.dma_start(
                    out=mt[:].rearrange("p (g c) -> p g c", c=TA), in_=src2)),
                    RP.sem_x, reads=(mixb[2][tau],), writes=(mt.b,),
                    tag=f"{tag}vmx{dc}.{tau}")
                oa = RP["oar0" if tau % 2 == 0 else "oar1"]
                for tk in range(NCA):
                    ps = B[2 + tk % 2]
                    def mmv(e, mt=mt, tk=tk, dc=dc, ps=ps):
                        for g in range(16):
                            inst = e.matmul(
                                ps[:, 0:512],
                                mt[:, g * TA + tk * 128:g * TA + (tk + 1) * 128],
                                RP["wbig"][:, (dc % 2) * 8192 + g * 512:
                                          (dc % 2) * 8192 + (g + 1) * 512],
                                start=(g == 0), stop=(g == 15))
                        return inst
                    P.step("pe", mmv,
                           reads=tuple(RP["wbig"].bufs[(dc % 2) * 4:(dc % 2) * 4 + 4])
                           + (mt.b,), writes=(ps.b,),
                           tag=f"{tag}v{dc}.{tau}.{tk}")
                    P.step("act", (lambda e, ps=ps, tk=tk, oa=oa:
                                   e.activation(
                        oa[:, tk * 512:(tk + 1) * 512], ps[:, 0:512],
                        AF.Copy)),
                        reads=(ps.b,), writes=(oa.bufs[tk],),
                        tag=f"{tag}ve{dc}.{tau}.{tk}")
                dst = dram["v_s"][tau * TA:(tau + 1) * TA,
                                  dc * 512:(dc + 1) * 512].rearrange(
                    "(tk p) c -> p tk c", p=128)
                P.dma("act", (lambda e, oa=oa, dst=dst: e.dma_start(
                    out=dst, in_=oa[:, 0:NCA * 512].rearrange(
                        "p (tk c) -> p tk c", c=512))),
                    RP.sem_st, reads=tuple(oa.bufs[0:NCA]),
                    writes=(vb_[dc * 4 + tau],), tag=f"{tag}vst{dc}.{tau}")
        P.join_barrier()

    with ExitStack() as ctx2:
        _NAME_PREFIX[0] = "rwB_"
        RB = Res()
        RB.update(semn)
        RB["qdT"] = SBuf(ctx2, nc, "qdT", [128, 16 * TA], BF16, ntiles=16)
        RB["kdT"] = SBuf(ctx2, nc, "kdT", [128, 16 * TA], BF16, ntiles=16)
        RB["garena"] = SBuf(ctx2, nc, "garena", [128, 16 * TA], BF16, ntiles=16)
        RB["ygarena"] = SBuf(ctx2, nc, "ygarena", [128, 16 * TA], BF16, ntiles=16)
        RB["varena"] = SBuf(ctx2, nc, "varena", [128, NCA * 2048], BF16,
                           ntiles=NCA * 4)
        RB["S_f32"] = SBuf(ctx2, nc, "S_f32", [128, 16 * 64], F32, ntiles=16)
        RB["S_bf"] = SBuf(ctx2, nc, "S_bf", [128, 16 * 64], BF16, ntiles=16)
        RB["Sodd"] = SBuf(ctx2, nc, "Sodd", [64, 16 * 64], F32)
        RB["SoddHi"] = SBuf(ctx2, nc, "SoddHi", [128, 16 * 64], F32)
        for nm in ["amT0", "amT1"]:
            RB[nm] = SBuf(ctx2, nc, nm, [128, 128], BF16)
        for nm in ["kt0", "kt1"]:
            RB[nm] = SBuf(ctx2, nc, nm, [128, 64], BF16)
        for nm in ["ke0", "ke1"]:
            RB[nm] = SBuf(ctx2, nc, nm, [128, 128], BF16)
        for nm in ["ysb0", "ysb1"]:
            RB[nm] = SBuf(ctx2, nc, nm, [128, 128], BF16)
        RB["ident64"] = SBuf(ctx2, nc, "ident64s", [128, 64], BF16)
        RB["ident_bf"] = SBuf(ctx2, nc, "identbfs", [128, 128], BF16)
        RB["maskA"] = SBuf(ctx2, nc, "maskA", [128, 128], BF16)
        for nm, src_ in (("ident64", dram["ident64"]), ("ident_bf", dram["ident"]),
                         ("maskA", dram["maskA"])):
            P.dma("pool", (lambda e, nm=nm, src_=src_, RB=RB: e.dma_start(
                out=RB[nm][:], in_=src_)), RB.sem_c, writes=(RB[nm].b,),
                tag="rcB" + nm)
        P.step("dve", (lambda e: e.memset(RB["S_f32"][:], 0.0)),
               writes=tuple(RB["S_f32"].bufs), tag=f"{tag}S0")
        P.step("dve", (lambda e: e.memset(RB["S_bf"][:], 0.0)),
               writes=tuple(RB["S_bf"].bufs), tag=f"{tag}Sb0")
        for tau in range(4):
            for arena, scr, bl in (("qdT", "qd_s", qdb), ("kdT", "kd_s", kdb),
                                   ("garena", "g_s", gb_)):
                for half in range(2):
                    src = dram[scr][:, tau * TA:(tau + 1) * TA].rearrange(
                        "(g p) c -> p g c", p=128)[:, half * 8:(half + 1) * 8, :]
                    q_ = "sp" if arena != "garena" else "act"
                    P.dma(q_, (lambda e, src=src, arena=arena, half=half:
                               e.dma_start(
                        out=RB[arena][:, half * 8 * TA:(half + 1) * 8 * TA].rearrange(
                            "p (g c) -> p g c", c=TA), in_=src)),
                        RB.sem_x, reads=(bl[half * 4 + tau],),
                        writes=tuple(RB[arena].bufs[half * 8:(half + 1) * 8]),
                        tag=f"{tag}B{arena}{tau}.{half}",
                        stream=f"{arena}{half}")
            for dc in range(4):
                src = dram["v_s"][tau * TA:(tau + 1) * TA,
                                  dc * 512:(dc + 1) * 512].rearrange(
                    "(tk p) c -> p tk c", p=128)
                P.dma("act", (lambda e, src=src, dc=dc: e.dma_start(
                    out=RB["varena"][:].rearrange("p (tk c) -> p tk c",
                                                 c=2048)[:, :, dc * 512:(dc + 1) * 512],
                    in_=src)), RB.sem_x, reads=(vb_[dc * 4 + tau],),
                    writes=tuple(RB["varena"].bufs[tk * 4 + dc]
                                 for tk in range(NCA)),
                    tag=f"{tag}Bv{tau}.{dc}", stream=f"varena{dc}")
            for n in range(NCA):
                gc = tau * NCA + n
                for hp in range(16):
                    psAT = B[0]
                    amTs = [RB["amT0"], RB["amT1"]]
                    ktoks = [RB["kt0"], RB["kt1"]]
                    for sub in range(2):
                        rb = sub * 64
                        P.step("pe", (lambda e, hp=hp, n=n, rb=rb, psAT=psAT:
                                      e.matmul(psAT[:, 0:128],
                                               RB["kdT"][rb:rb + 64,
                                                        hp * TA + n * 128:hp * TA + (n + 1) * 128],
                                               RB["qdT"][rb:rb + 64,
                                                        hp * TA + n * 128:hp * TA + (n + 1) * 128],
                                               start=True, stop=True)),
                               reads=(RB["kdT"].bufs[hp], RB["qdT"].bufs[hp]),
                               writes=(psAT.b,), tag=f"{tag}A{gc}.{hp}.{sub}")
                        amT = amTs[sub]
                        P.step("dve", (lambda e, amT=amT, psAT=psAT:
                                       e.tensor_tensor(
                            amT[:], psAT[:, 0:128], RB["maskA"][:],
                            op=ALU.mult)),
                            reads=(psAT.b, RB["maskA"].b), writes=(amT.b,),
                            tag=f"{tag}Am{gc}.{hp}.{sub}")
                        ke = RB["ke0" if sub == 0 else "ke1"]
                        P.step("dve", (lambda e, hp=hp, n=n, rb=rb, ke=ke,
                                        gc=gc: e.tensor_scalar(
                            ke[rb:rb + 64, :],
                            RB["kdT"][rb:rb + 64,
                                     hp * TA + n * 128:hp * TA + (n + 1) * 128],
                            eLca[rb:rb + 64, hp * 16 + gc:hp * 16 + gc + 1],
                            None, ALU.mult)),
                            reads=(RB["kdT"].bufs[hp], eLca_buf),
                            writes=(ke.b,), tag=f"{tag}ke{gc}.{hp}.{sub}")
                        psK = B[3 + sub]
                        P.step("pe", (lambda e, ke=ke, rb=rb, psK=psK:
                                      e.transpose(
                            psK[:, 0:32].bitcast(BF16), ke[rb:rb + 64, :],
                            RB["ident64"][rb:rb + 64, :])),
                            reads=(ke.b, RB["ident64"].b), writes=(psK.b,),
                            tag=f"{tag}kt{gc}.{hp}.{sub}")
                        P.step("act", (lambda e, sub=sub, psK=psK,
                                       ktoks=ktoks: e.activation(
                            ktoks[sub][:], psK[:, 0:32].bitcast(BF16),
                            AF.Copy)),
                            reads=(psK.b,), writes=(ktoks[sub].b,),
                            tag=f"{tag}kte{gc}.{hp}.{sub}")
                    psY = B[1 + hp % 2]
                    def mmy2(e, hp=hp, n=n, amTs=amTs, psY=psY):
                        inst = None
                        for sub in range(2):
                            rb = sub * 64
                            inst = e.matmul(psY[:, rb:rb + 64],
                                            amTs[sub][:],
                                            RB["varena"][:, n * 2048 + hp * 128 + rb:
                                                        n * 2048 + hp * 128 + rb + 64],
                                            start=(sub == 0), stop=False)
                            inst = e.matmul(psY[:, rb:rb + 64],
                                            RB["qdT"][rb:rb + 64,
                                                     hp * TA + n * 128:hp * TA + (n + 1) * 128],
                                            RB["S_bf"][rb:rb + 64,
                                                      hp * 64:(hp + 1) * 64],
                                            start=False, stop=(sub == 1))
                        return inst
                    vread = (RB["varena"].bufs[n * 4 + hp // 4],)
                    P.step("pe", mmy2,
                           reads=(amTs[0].b, amTs[1].b, RB["S_bf"].bufs[hp],
                                  RB["qdT"].bufs[hp]) + vread,
                           writes=(psY.b,), tag=f"{tag}Y{gc}.{hp}")
                    ysb = RB["ysb0" if hp % 2 == 0 else "ysb1"]
                    P.step("dve", (lambda e, psY=psY, ysb=ysb: e.tensor_copy(
                        ysb[:], psY[:, 0:128])),
                        reads=(psY.b,), writes=(ysb.b,), tag=f"{tag}ye{gc}.{hp}")
                    psT = B[6 + hp % 2]
                    P.step("pe", (lambda e, ysb=ysb, psT=psT: e.transpose(
                        psT[:, 0:64].bitcast(BF16), ysb[:], RB["ident_bf"][:])),
                        reads=(ysb.b, RB["ident_bf"].b), writes=(psT.b,),
                        tag=f"{tag}yt{gc}.{hp}")
                    P.step("dve", (lambda e, psT=psT, hp=hp, n=n:
                                   e.tensor_tensor(
                        RB["ygarena"][:, hp * TA + n * 128:hp * TA + (n + 1) * 128],
                        RB["garena"][:, hp * TA + n * 128:hp * TA + (n + 1) * 128],
                        psT[:, 0:64].bitcast(BF16), op=ALU.mult)),
                        reads=(psT.b, RB["garena"].bufs[hp]),
                        writes=(RB["ygarena"].bufs[hp],),
                        tag=f"{tag}yg{gc}.{hp}")
                    psS = B[5]
                    P.step("pe", (lambda e, ktoks=ktoks, hp=hp, n=n, psS=psS:
                                  e.matmul(psS[0:64, 0:64], ktoks[0][:],
                                           RB["varena"][:, n * 2048 + hp * 128:
                                                       n * 2048 + hp * 128 + 64],
                                           start=True, stop=True)),
                           reads=(ktoks[0].b,) + vread, writes=(psS.b,),
                           tag=f"{tag}SaA{gc}.{hp}")
                    P.step("dve", (lambda e, hp=hp, gc=gc, psS=psS:
                                   e.scalar_tensor_tensor(
                        RB["S_f32"][0:64, hp * 64:(hp + 1) * 64],
                        RB["S_f32"][0:64, hp * 64:(hp + 1) * 64],
                        eLca[0:64, hp * 16 + gc:hp * 16 + gc + 1],
                        psS[0:64, 0:64], op0=ALU.mult, op1=ALU.add)),
                        reads=(eLca_buf, psS.b, RB["S_f32"].bufs[hp]),
                        writes=(RB["S_f32"].bufs[hp],), tag=f"{tag}SuA{gc}.{hp}")
                    P.step("pe", (lambda e, ktoks=ktoks, hp=hp, n=n, psS=psS:
                                  e.matmul(psS[0:64, 64:128], ktoks[1][:],
                                           RB["varena"][:, n * 2048 + hp * 128 + 64:
                                                       n * 2048 + hp * 128 + 128],
                                           start=False, stop=True)),
                           reads=(ktoks[1].b,) + vread, writes=(psS.b,),
                           tag=f"{tag}SaB{gc}.{hp}")
                    P.step("dve", (lambda e, hp=hp, psS=psS: e.tensor_copy(
                        RB["Sodd"][0:64, hp * 64:(hp + 1) * 64],
                        psS[0:64, 64:128])),
                        reads=(psS.b,), writes=(RB["Sodd"].b,),
                        tag=f"{tag}Se{gc}.{hp}")
                P.dma("sp", (lambda e: e.dma_start(
                    out=RB["SoddHi"][64:128, :], in_=RB["Sodd"][0:64, :])),
                    RB.sem_x, reads=(RB["Sodd"].b,), writes=(RB["SoddHi"].b,),
                    tag=f"{tag}Smv{gc}")
                for hp in range(16):
                    P.step("dve", (lambda e, hp=hp, gc=gc:
                                   e.scalar_tensor_tensor(
                        RB["S_f32"][64:128, hp * 64:(hp + 1) * 64],
                        RB["S_f32"][64:128, hp * 64:(hp + 1) * 64],
                        eLca[64:128, hp * 16 + gc:hp * 16 + gc + 1],
                        RB["SoddHi"][64:128, hp * 64:(hp + 1) * 64],
                        op0=ALU.mult, op1=ALU.add)),
                        reads=(eLca_buf, RB["SoddHi"].b, RB["S_f32"].bufs[hp]),
                        writes=(RB["S_f32"].bufs[hp],), tag=f"{tag}SuB{gc}.{hp}")
                    P.step("act", (lambda e, hp=hp: e.copy(
                        RB["S_bf"][:, hp * 64:(hp + 1) * 64],
                        RB["S_f32"][:, hp * 64:(hp + 1) * 64])),
                        reads=(RB["S_f32"].bufs[hp],),
                        writes=(RB["S_bf"].bufs[hp],), tag=f"{tag}Sc{gc}.{hp}")
            dst = dram["yg_s"][:, tau * TA:(tau + 1) * TA].rearrange(
                "(g p) c -> p g c", p=128)
            P.dma("act", (lambda e, dst=dst: e.dma_start(
                out=dst, in_=RB["ygarena"][:].rearrange("p (g c) -> p g c",
                                                       c=TA))),
                RB.sem_st, reads=tuple(RB["ygarena"].bufs),
                writes=(ygb[tau],), tag=f"{tag}ygs{tau}")
        P.join_barrier()

    with ExitStack() as ctx2:
        _NAME_PREFIX[0] = "rwC_"
        RC = Res()
        RC.update(semn)
        RC["ygt"] = SBuf(ctx2, nc, "ygt", [128, 16 * TA], BF16, ntiles=16)
        RC["wst0"] = SBuf(ctx2, nc, "wst0", [128, 2048], BF16)
        RC["wst1"] = SBuf(ctx2, nc, "wst1", [128, 2048], BF16)
        RC["rt"] = SBuf(ctx2, nc, "rt", [128, 16 * 512], F32, ntiles=16)
        for nm in ["tf0", "tf1", "xf0", "xf1"]:
            RC[nm] = SBuf(ctx2, nc, nm, [128, 512], F32)
        for nm in ["ln_rb0", "ln_rq0"]:
            RC[nm] = SBuf(ctx2, nc, nm, [128, 512], BF16)
        for nm in ["ln_xn0", "ln_c0s", "ln_c1s"]:
            RC[nm] = SBuf(ctx2, nc, nm, [128, 512], F32)
        for nm in ["ln_mean", "ln_ex2", "ln_var", "ln_c0"]:
            RC[nm] = SBuf(ctx2, nc, nm, [1, 512], F32)
        RC["ones_col_bf"] = SBuf(ctx2, nc, "onescbf", [128, 1], BF16)
        RC["ones_row_f"] = SBuf(ctx2, nc, "onesrf", [1, 128], F32)
        RC["cvec"] = SBuf(ctx2, nc, "cvecs", [128, 8], F32)
        P.dma("pool", (lambda e, RC=RC: e.dma_start(
            out=RC["ones_col_bf"][:], in_=dram["onesc"])), RC.sem_c,
            writes=(RC["ones_col_bf"].b,), tag="rcCoc")
        P.dma("sp", (lambda e, RC=RC: e.dma_start(
            out=RC["ones_row_f"][:], in_=dram["onesr"])), RC.sem_c,
            writes=(RC["ones_row_f"].b,), tag="rcCor")
        P.dma("sp", (lambda e, RC=RC: e.dma_start(
            out=RC["cvec"][:], in_=dram["cvec"])), RC.sem_c,
            writes=(RC["cvec"].b,), tag="rcCcv")
        for tau in range(4):
            tcols = slice(tau * TA, (tau + 1) * TA)
            src = dram["yg_s"][:, tcols].rearrange("(g p) c -> p g c", p=128)
            P.dma("sp", (lambda e, src=src: e.dma_start(
                out=RC["ygt"][:].rearrange("p (g c) -> p g c", c=TA),
                in_=src)), RC.sem_x, reads=(ygb[tau],),
                writes=tuple(RC["ygt"].bufs), tag=f"{tag}Cyg{tau}")
            for g2 in range(16):
                wc = RC["wst0" if g2 % 2 == 0 else "wst1"]
                srcw = io["wo"][:, g2 * 128:(g2 + 1) * 128].rearrange(
                    "(g p) c -> p g c", p=128)
                P.dma("pool",
                      (lambda e, wc=wc, srcw=srcw: e.dma_start(
                    out=wc[:].rearrange("p (g c) -> p g c", c=128),
                    in_=srcw)), RC.sem_w, writes=(wc.b,),
                    tag=f"{tag}Cw{tau}.{g2}")
                pso = B[6 + g2 % 2]
                def mmo(e, wc=wc, pso=pso):
                    for g in range(16):
                        inst = e.matmul(pso[:, 0:TA],
                                        wc[:, g * 128:(g + 1) * 128],
                                        RC["ygt"][:, g * TA:(g + 1) * TA],
                                        start=(g == 0), stop=(g == 15))
                    return inst
                P.step("pe", mmo, reads=(wc.b,) + tuple(RC["ygt"].bufs),
                       writes=(pso.b,), tag=f"{tag}Co{tau}.{g2}")
                tf = RC["tf0" if g2 % 2 == 0 else "tf1"]
                P.step("act", (lambda e, tf=tf, pso=pso: e.activation(
                    tf[:], pso[:, 0:TA], AF.Copy)),
                    reads=(pso.b,), writes=(tf.b,), tag=f"{tag}Coe{tau}.{g2}")
                xf = RC["xf0" if g2 % 2 == 0 else "xf1"]
                P.dma("sp", (lambda e, xf=xf, g2=g2, tcols=tcols: e.dma_start(
                    out=xf[:], in_=xin[g2 * 128:(g2 + 1) * 128, tcols])),
                    RC.sem_x, reads=tuple(xin_bufs[tau * 4 + i]
                                         for i in range(4)),
                    writes=(xf.b,), tag=f"{tag}Cx{tau}.{g2}")
                rt = RC["rt"][:, g2 * 512:(g2 + 1) * 512]
                P.step("dve", (lambda e, tf=tf, xf=xf, rt=rt:
                                e.tensor_tensor(rt, tf[:], xf[:],
                                                op=ALU.add)),
                       reads=(tf.b, xf.b), writes=(RC["rt"].bufs[g2],),
                       tag=f"{tag}Cr{tau}.{g2}")
            rt_views = [RC["rt"][:, dg * 512:(dg + 1) * 512] for dg in range(16)]
            xout_blk = [xout[dg * 128:(dg + 1) * 128, tcols]
                        for dg in range(16)]
            xo_bufs = [tuple(xout_bufs[tau * 4 + i] for i in range(4))
                       for dg in range(16)]
            banks = {"s1": B[2], "s2": B[3], "c0": B[4], "c1": B[5]}
            emit_ln_store(P, RC, rt_views, RC["rt"].bufs, xout_blk, xo_bufs,
                          banks, tag=f"{tag}ln{tau}.", width=512)
        P.join_barrier()


def build_rwkv_program(xin_name="s0", xout_name="s1", debug=False):
    nc = bass.Bass("TRN2", target_bir_lowering=False, debug=False)
    xin = nc.dram_tensor(xin_name, [D, T], F32, kind="ExternalInput").ap()
    names = ["Wq", "Wk", "Wv", "Wg", "Wo"]
    wd = {n: nc.dram_tensor("rwkv_" + n + "T", [D, D], F32,
                            kind="ExternalInput").ap() for n in names}
    dram = {}
    dram["maa_w1"] = nc.dram_tensor("L0_attn_maa_w1", [2048, 160], F32, kind="ExternalInput").ap()
    dram["maa_w2a"] = nc.dram_tensor("rwkv_maa_w2a", [96, 2048], F32, kind="ExternalInput").ap()
    dram["maa_w2b"] = nc.dram_tensor("rwkv_maa_w2b", [64, 2048], F32, kind="ExternalInput").ap()
    dram["ident64"] = nc.dram_tensor("rwkv_ident64", [128, 64], F32, kind="ExternalInput").ap()
    dram["decay_w1"] = nc.dram_tensor("L0_attn_decay_w1", [2048, 64], F32, kind="ExternalInput").ap()
    dram["decay_w2"] = nc.dram_tensor("L0_attn_decay_w2", [64, 2048], F32, kind="ExternalInput").ap()
    dram["vecs"] = nc.dram_tensor("rwkv_vecs", [128, 128], F32, kind="ExternalInput").ap()
    dram["maskA"] = nc.dram_tensor("rwkv_maskA", [128, 128], F32, kind="ExternalInput").ap()
    dram["ident"] = nc.dram_tensor("ident", [128, 128], F32, kind="ExternalInput").ap()
    dram["onesc"] = nc.dram_tensor("ones_col", [128, 1], F32, kind="ExternalInput").ap()
    dram["onesr"] = nc.dram_tensor("ones_row", [1, 128], F32, kind="ExternalInput").ap()
    dram["cvec"] = nc.dram_tensor("cvec", [128, 8], F32, kind="ExternalInput").ap()
    xout = nc.dram_tensor(xout_name, [D, T], F32, kind="ExternalOutput").ap()
    kd_ = "ExternalOutput" if debug else "Internal"
    mix_s = [nc.dram_tensor(f"mix{f}_s", [D, T], BF16, kind=kd_).ap()
             for f in range(4)]
    dram["mix_s"] = mix_s
    for nm in ["ela_s", "ekf_s", "qd_s", "kd_s", "g_s", "yg_s"]:
        dram[nm] = nc.dram_tensor(nm, [D, T], BF16, kind=kd_).ap()
    dram["v_s"] = nc.dram_tensor("v_s", [T, D], BF16, kind=kd_).ap()

    ctx = ExitStack()
    P = Prog()
    semn = {}
    for s in ("w", "x", "st", "c"):
        semn["sem_" + s] = P.new_dma_sem(s)
    B = [PSum(ctx, nc, f"bank{i}", [128, 512], F32) for i in range(8)]
    eLca_sb = SBuf(ctx, nc, "eLca_g", [128, 256], F32)
    io = {"wq": wd["Wq"], "wk": wd["Wk"], "wv": wd["Wv"], "wg": wd["Wg"],
          "wo": wd["Wo"]}
    xin_b = dram_bufs(16, "xin")
    xout_b = dram_bufs(16, "xout")
    emit_rwkv_slot(P, nc, B, semn, dram, io, xin, xin_b, xout, xout_b,
                   eLca_sb, eLca_sb.b)
    _NAME_PREFIX[0] = ""
    build_engine_sections(nc, P, ctx)
    ctx.close()
    return nc



def build_entry_program():
    nc = bass.Bass("TRN2", target_bir_lowering=False, debug=False)
    x = nc.dram_tensor("x_nat", [T, D], F32, kind="ExternalInput").ap()
    ident_f32 = nc.dram_tensor("ident", [128, 128], F32, kind="ExternalInput").ap()
    out = nc.dram_tensor("s0", [D, T], F32, kind="ExternalOutput").ap()
    ctx = ExitStack()
    P = Prog()
    R = {}
    for nm in ["e_nat0", "e_nat1", "e_col0", "e_col1"]:
        R[nm] = SBuf(ctx, nc, nm, [128, 2048], F32)
    R["psA"] = PSum(ctx, nc, "psA", [128, 512], F32)
    R["psB"] = PSum(ctx, nc, "psB", [128, 512], F32)
    R["ident_f32"] = SBuf(ctx, nc, "ident_f32_s", [128, 128], F32)
    s_ld = P.new_dma_sem("ld")
    s_st = P.new_dma_sem("st")
    s_c = P.new_dma_sem("c")
    P.dma("sp", lambda e: e.dma_start(out=R["ident_f32"][:], in_=ident_f32),
          s_c, writes=(R["ident_f32"].b,), tag="ldid")
    emit_nat_to_fm(P, R, x, out, "e_", s_ld, s_st)
    build_engine_sections(nc, P, ctx)
    ctx.close()
    return nc


def build_exit_program():
    nc = bass.Bass("TRN2", target_bir_lowering=False, debug=False)
    xin = nc.dram_tensor("s4", [D, T], F32, kind="ExternalInput").ap()
    ident_f32 = nc.dram_tensor("ident", [128, 128], F32, kind="ExternalInput").ap()
    out = nc.dram_tensor("out_nat", [T, D], F32, kind="ExternalOutput").ap()
    ctx = ExitStack()
    P = Prog()
    R = {}
    for nm in ["x_nat0", "x_nat1", "x_col0", "x_col1"]:
        R[nm] = SBuf(ctx, nc, nm, [128, 2048], F32)
    R["psA"] = PSum(ctx, nc, "psA", [128, 512], F32)
    R["psB"] = PSum(ctx, nc, "psB", [128, 512], F32)
    R["ident_f32"] = SBuf(ctx, nc, "ident_f32_s", [128, 128], F32)
    s_ld = P.new_dma_sem("ld")
    s_st = P.new_dma_sem("st")
    s_c = P.new_dma_sem("c")
    P.dma("sp", lambda e: e.dma_start(out=R["ident_f32"][:], in_=ident_f32),
          s_c, writes=(R["ident_f32"].b,), tag="ldid")
    emit_fm_to_nat(P, R, xin, out, "x_", s_ld, s_st)
    build_engine_sections(nc, P, ctx)
    ctx.close()
    return nc



_PROG_CACHE = {}


def _get_prog(name, builder):
    if name not in _PROG_CACHE:
        _PROG_CACHE[name] = builder()
    return _PROG_CACHE[name]


def _host_consts():
    c = {}
    c["ident"] = np.eye(128, dtype=np.float32)
    c["ones_col"] = np.ones((128, 1), np.float32)
    c["ones_row"] = np.ones((1, 128), np.float32)
    c["ones_5"] = np.ones((1, 512), np.float32)
    cvec = np.zeros((128, 8), np.float32)
    cvec[:, 0] = 1e-5
    cvec[:, 1] = 1.1920929e-7
    cvec[:, 2] = -8.0
    c["cvec"] = cvec
    return c


def _prep_rwkv(inp):
    m = {}
    for n in ["Wq", "Wk", "Wv", "Wg", "Wo"]:
        m["rwkv_" + n + "T"] = np.ascontiguousarray(
            inp["L0_attn_" + n].T).astype(np.float32)
    m["L0_attn_maa_w1"] = inp["L0_attn_maa_w1"]
    w2 = inp["L0_attn_maa_w2"]
    m["rwkv_maa_w2a"] = np.ascontiguousarray(
        np.concatenate([w2[0], w2[1], w2[2]], axis=0))
    m["rwkv_maa_w2b"] = np.ascontiguousarray(
        np.concatenate([w2[3], w2[4]], axis=0))
    m["rwkv_ident64"] = np.tile(np.eye(64, dtype=np.float32), (2, 1))
    m["L0_attn_decay_w1"] = inp["L0_attn_decay_w1"]
    m["L0_attn_decay_w2"] = inp["L0_attn_decay_w2"]
    vecs = np.zeros((128, 128), np.float32)
    order = [inp["L0_attn_maa_x"].reshape(-1), inp["L0_attn_maa_r"].reshape(-1),
             inp["L0_attn_maa_k"].reshape(-1), inp["L0_attn_maa_v"].reshape(-1),
             inp["L0_attn_maa_w"].reshape(-1), inp["L0_attn_maa_g"].reshape(-1),
             inp["L0_attn_time_decay"].reshape(-1), inp["L0_attn_bg"].reshape(-1)]
    for i, a in enumerate(order):
        vecs[:, i * 16:(i + 1) * 16] = np.asarray(a, np.float32).reshape(16, 128).T
    m["rwkv_vecs"] = vecs
    m["rwkv_maskA"] = (np.arange(128)[:, None] <=
                       np.arange(128)[None, :]).astype(np.float32)
    return m


def _prep_attn(inp):
    m = {}
    m["wqT"] = np.ascontiguousarray(inp["L1_attn_Wq"].T)
    m["wkT"] = np.ascontiguousarray(inp["L1_attn_Wk"].T)
    m["wvT"] = np.ascontiguousarray(inp["L1_attn_Wv"].T)
    m["woT"] = np.ascontiguousarray(inp["L1_attn_Wo"].T)
    biases = np.zeros((128, 64), np.float32)
    biases[:, 0:16] = np.asarray(inp["L1_attn_bq"]).reshape(16, 128).T
    biases[:, 16:32] = np.asarray(inp["L1_attn_bk"]).reshape(16, 128).T
    biases[:, 32:48] = np.asarray(inp["L1_attn_bo"]).reshape(16, 128).T
    m["attn_biases"] = biases
    inv_freq = 1.0 / 10000.0 ** (np.arange(64, dtype=np.float32) / 64)
    angles = np.arange(T, dtype=np.float32)[:, None] * inv_freq[None, :]
    cos_p = np.cos(angles).astype(np.float32)
    sin_p = np.sin(angles).astype(np.float32)
    ctab = np.zeros((128, T), np.float32)
    stab = np.zeros((128, T), np.float32)
    for cch in range(128):
        pair = cch // 2
        ctab[cch] = cos_p[:, pair]
        stab[cch] = (-sin_p[:, pair]) if cch % 2 == 0 else sin_p[:, pair]
    m["rope_ctab"] = ctab
    m["rope_stab"] = stab
    swap = np.zeros((128, 128), np.float32)
    for i in range(128):
        swap[i, i ^ 1] = 1.0
    m["rope_swap"] = swap
    masks = np.zeros((128, 2048), np.float32)
    for s_ in range(4):
        masks[:, s_ * 512:(s_ + 1) * 512] = (
            np.arange(128)[:, None] + 128 * s_ <= np.arange(512)[None, :])
    m["causal_masks"] = masks
    m["bv_row"] = np.asarray(inp["L1_attn_bv"]).reshape(1, 2048)
    return m


def _prep_moe(inp):
    m = {}
    m["L1_ffn_gate_w"] = inp["L1_ffn_gate_w"]
    m["L1_ffn_w1"] = inp["L1_ffn_w1"]
    m["L1_ffn_w2"] = inp["L1_ffn_w2"]
    m["L1_ffn_b2"] = inp["L1_ffn_b2"]
    m["L1_ffn_shared_w1"] = inp["L1_ffn_shared_w1"]
    m["L1_ffn_shared_w2"] = inp["L1_ffn_shared_w2"]
    b1 = np.asarray(inp["L1_ffn_b1"])
    b1s = np.zeros((128, 64), np.float32)
    for e in range(8):
        for ht in range(8):
            b1s[:, e * 8 + ht] = b1[e, ht * 128:(ht + 1) * 128]
    m["moe_b1"] = b1s
    m["moe_sb1"] = np.asarray(inp["L1_ffn_shared_b1"]).reshape(16, 128).T.copy()
    m["moe_sb2"] = np.asarray(inp["L1_ffn_shared_b2"]).reshape(1, 2048)
    m["sel8"] = np.concatenate(
        [np.eye(8, dtype=np.float32)[:, e:e + 1] @ np.ones((1, 128), np.float32)
         for e in range(8)], axis=1)
    return m


def kernel(**inputs):
    inputs = {k: np.asarray(v) for k, v in inputs.items()}
    x_full = inputs["L0_attn_hidden_states"]
    consts = _host_consts()
    rw = _prep_rwkv(inputs)
    at = _prep_attn(inputs)
    mo = _prep_moe(inputs)
    ffn_in = {"L0_ffn_w_gate_up": inputs["L0_ffn_w_gate_up"],
              "L0_ffn_w_proj": inputs["L0_ffn_w_proj"]}

    cores = list(range(NCORES))
    nc_e = _get_prog("entry", build_entry_program)
    maps = [{"x_nat": np.ascontiguousarray(x_full[c]), "ident": consts["ident"]}
            for c in cores]
    res = run_bass_kernel_spmd(nc_e, maps, core_ids=cores)
    s0 = [r["s0"] for r in res.results]
    nc_r = _get_prog("rwkv", build_rwkv_program)
    maps = [dict(rw, s0=s0[c], ident=consts["ident"],
                 ones_col=consts["ones_col"], ones_row=consts["ones_row"],
                 cvec=consts["cvec"]) for c in cores]
    res = run_bass_kernel_spmd(nc_r, maps, core_ids=cores)
    s1 = [r["s1"] for r in res.results]
    nc_f = _get_prog("ffn", build_ffn_program)
    maps = [dict(ffn_in, s1=s1[c], ones_col=consts["ones_col"],
                 ones_row=consts["ones_row"], cvec=consts["cvec"])
            for c in cores]
    res = run_bass_kernel_spmd(nc_f, maps, core_ids=cores)
    s2 = [r["s2"] for r in res.results]
    nc_a = _get_prog("attn", build_attn_program)
    maps = [dict(at, s2=s2[c], ones_col=consts["ones_col"],
                 ones_row=consts["ones_row"], cvec=consts["cvec"])
            for c in cores]
    res = run_bass_kernel_spmd(nc_a, maps, core_ids=cores)
    s3 = [r["s3"] for r in res.results]
    nc_m = _get_prog("moe", build_moe_program)
    maps = [dict(mo, s3=s3[c], ones_col=consts["ones_col"],
                 ones_row=consts["ones_row"], ones_5=consts["ones_5"],
                 cvec=consts["cvec"], ident=consts["ident"]) for c in cores]
    res = run_bass_kernel_spmd(nc_m, maps, core_ids=cores)
    s4 = [r["s4"] for r in res.results]
    nc_x = _get_prog("exit", build_exit_program)
    maps = [{"s4": s4[c], "ident": consts["ident"]} for c in cores]
    res = run_bass_kernel_spmd(nc_x, maps, core_ids=cores)
    out = np.stack([r["out_nat"] for r in res.results], axis=0)
    return out.astype(np.float32)



def build_fused_program(debug=False):
    nc = bass.Bass("TRN2", target_bir_lowering=False, debug=False)
    x_nat = nc.dram_tensor("x_nat", [T, D], F32, kind="ExternalInput").ap()
    ident = nc.dram_tensor("ident", [128, 128], F32, kind="ExternalInput").ap()
    onesc = nc.dram_tensor("ones_col", [128, 1], F32, kind="ExternalInput").ap()
    onesr = nc.dram_tensor("ones_row", [1, 128], F32, kind="ExternalInput").ap()
    ones5 = nc.dram_tensor("ones_5", [1, 512], F32, kind="ExternalInput").ap()
    cvec = nc.dram_tensor("cvec", [128, 8], F32, kind="ExternalInput").ap()
    wd = {n: nc.dram_tensor("rwkv_" + n + "T", [D, D], F32,
                            kind="ExternalInput").ap()
          for n in ["Wq", "Wk", "Wv", "Wg", "Wo"]}
    maa_w1 = nc.dram_tensor("L0_attn_maa_w1", [2048, 160], F32, kind="ExternalInput").ap()
    maa_w2a_d = nc.dram_tensor("rwkv_maa_w2a", [96, 2048], F32, kind="ExternalInput").ap()
    maa_w2b_d = nc.dram_tensor("rwkv_maa_w2b", [64, 2048], F32, kind="ExternalInput").ap()
    ident64_d = nc.dram_tensor("rwkv_ident64", [128, 64], F32, kind="ExternalInput").ap()
    decay_w1 = nc.dram_tensor("L0_attn_decay_w1", [2048, 64], F32, kind="ExternalInput").ap()
    decay_w2 = nc.dram_tensor("L0_attn_decay_w2", [64, 2048], F32, kind="ExternalInput").ap()
    vecs = nc.dram_tensor("rwkv_vecs", [128, 128], F32, kind="ExternalInput").ap()
    maskA_d = nc.dram_tensor("rwkv_maskA", [128, 128], F32, kind="ExternalInput").ap()
    wgu = nc.dram_tensor("L0_ffn_w_gate_up", [16384, 2048], F32, kind="ExternalInput").ap()
    wproj = nc.dram_tensor("L0_ffn_w_proj", [2048, 8192], F32, kind="ExternalInput").ap()
    wq_a = nc.dram_tensor("wqT", [D, D], F32, kind="ExternalInput").ap()
    wk_a = nc.dram_tensor("wkT", [D, D], F32, kind="ExternalInput").ap()
    wv_a = nc.dram_tensor("wvT", [D, D], F32, kind="ExternalInput").ap()
    wo_a = nc.dram_tensor("woT", [D, D], F32, kind="ExternalInput").ap()
    biases = nc.dram_tensor("attn_biases", [128, 64], F32, kind="ExternalInput").ap()
    cos_d = nc.dram_tensor("rope_ctab", [128, T], F32, kind="ExternalInput").ap()
    sin_d = nc.dram_tensor("rope_stab", [128, T], F32, kind="ExternalInput").ap()
    swap_d = nc.dram_tensor("rope_swap", [128, 128], F32, kind="ExternalInput").ap()
    masks_d = nc.dram_tensor("causal_masks", [128, 2048], F32, kind="ExternalInput").ap()
    bvrow = nc.dram_tensor("bv_row", [1, 2048], F32, kind="ExternalInput").ap()
    gate_w = nc.dram_tensor("L1_ffn_gate_w", [2048, 8], F32, kind="ExternalInput").ap()
    w1 = nc.dram_tensor("L1_ffn_w1", [8, 2048, 1024], F32, kind="ExternalInput").ap()
    b1 = nc.dram_tensor("moe_b1", [128, 64], F32, kind="ExternalInput").ap()
    w2 = nc.dram_tensor("L1_ffn_w2", [8, 1024, 2048], F32, kind="ExternalInput").ap()
    b2 = nc.dram_tensor("L1_ffn_b2", [8, 2048], F32, kind="ExternalInput").ap()
    sw1 = nc.dram_tensor("L1_ffn_shared_w1", [2048, 2048], F32, kind="ExternalInput").ap()
    sb1 = nc.dram_tensor("moe_sb1", [128, 16], F32, kind="ExternalInput").ap()
    sw2 = nc.dram_tensor("L1_ffn_shared_w2", [2048, 2048], F32, kind="ExternalInput").ap()
    sb2 = nc.dram_tensor("moe_sb2", [1, 2048], F32, kind="ExternalInput").ap()
    sel8_d = nc.dram_tensor("sel8", [8, 1024], F32, kind="ExternalInput").ap()
    out_nat = nc.dram_tensor("out_nat", [T, D], F32, kind="ExternalOutput").ap()
    kd_ = "ExternalOutput" if debug else "Internal"
    s0 = nc.dram_tensor("s0", [D, T], F32, kind=kd_).ap()
    s1 = nc.dram_tensor("s1", [D, T], F32, kind=kd_).ap()
    s2 = nc.dram_tensor("s2", [D, T], F32, kind=kd_).ap()
    s3 = nc.dram_tensor("s3", [D, T], F32, kind=kd_).ap()
    s4 = nc.dram_tensor("s4", [D, T], F32, kind=kd_).ap()
    wgu_hat = nc.dram_tensor("wgu_hat", [16384, 2048], BF16).ap()
    wproj_hat = nc.dram_tensor("wproj_hat", [2048, 8192], BF16).ap()
    wguT = nc.dram_tensor("wguT", [512, 128, 512], BF16).ap()
    wprojT = nc.dram_tensor("wprojT", [256, 128, 512], BF16).ap()
    qs_s = nc.dram_tensor("qs_s", [D, T], BF16).ap()
    ks_s = nc.dram_tensor("ks_s", [D, T], BF16).ap()
    vs_s = nc.dram_tensor("vs_s", [T, D], BF16).ap()

    ctx = ExitStack()
    P = Prog()
    semn = {}
    for s in ("w", "x", "st", "c"):
        semn["sem_" + s] = P.new_dma_sem(s)
    B = [PSum(ctx, nc, f"bank{i}", [128, 512], F32) for i in range(8)]
    s0b = dram_bufs(16, "s0")
    s1b = dram_bufs(16, "s1")
    s2b = dram_bufs(16, "s2")
    s3b = dram_bufs(16, "s3")
    s4b = dram_bufs(16, "s4")

    def load_common(R):
        R.update(semn)
        R["ones_col_bf"] = SBuf(ctx2, nc, "ones_col_bf", [128, 1], BF16)
        R["ones_row_f"] = SBuf(ctx2, nc, "ones_row_f", [1, 128], F32)
        R["cvec"] = SBuf(ctx2, nc, "cvec_s", [128, 8], F32)
        P.dma("pool", lambda e: e.dma_start(out=R["ones_col_bf"][:], in_=onesc),
              R.sem_c, writes=(R["ones_col_bf"].b,), tag="c1")
        P.dma("sp", lambda e: e.dma_start(out=R["ones_row_f"][:], in_=onesr),
              R.sem_c, writes=(R["ones_row_f"].b,), tag="c2")
        P.dma("sp", lambda e: e.dma_start(out=R["cvec"][:], in_=cvec),
              R.sem_c, writes=(R["cvec"].b,), tag="c3")

    with ExitStack() as ctx2:
        _NAME_PREFIX[0] = "en_"
        R = Res()
        R.update(semn)
        for nm in ["e_nat0", "e_nat1", "e_col0", "e_col1"]:
            R[nm] = SBuf(ctx2, nc, nm, [128, 2048], F32)
        R["psA"] = B[0]
        R["psB"] = B[1]
        R["ident_f32"] = SBuf(ctx2, nc, "ident_f32_s", [128, 128], F32)
        P.dma("sp", (lambda e, R=R: e.dma_start(out=R["ident_f32"][:],
                                                in_=ident)),
              R.sem_c, writes=(R["ident_f32"].b,), tag="ldid")
        emit_nat_to_fm_b(P, R, x_nat, s0, s0b, "e_", R.sem_x, R.sem_st)
        P.join_barrier()
    rw_dram = {"maa_w1": maa_w1, "maa_w2a": maa_w2a_d, "maa_w2b": maa_w2b_d,
               "ident64": ident64_d, "decay_w1": decay_w1,
               "decay_w2": decay_w2, "vecs": vecs, "maskA": maskA_d,
               "ident": ident, "onesc": onesc, "onesr": onesr, "cvec": cvec}
    rw_dram["mix_s"] = [nc.dram_tensor(f"mix{f}_s", [D, T], BF16).ap()
                        for f in range(4)]
    for nm in ["ela_s", "ekf_s", "qd_s", "kd_s", "g_s", "yg_s"]:
        rw_dram[nm] = nc.dram_tensor(nm, [D, T], BF16).ap()
    rw_dram["v_s"] = nc.dram_tensor("v_s", [T, D], BF16).ap()
    eLca_sb = SBuf(ctx, nc, "eLca_g", [128, 256], F32)
    io_rw = {"wq": wd["Wq"], "wk": wd["Wk"], "wv": wd["Wv"], "wg": wd["Wg"],
             "wo": wd["Wo"]}
    emit_rwkv_slot(P, nc, B, semn, rw_dram, io_rw, s0, s0b, s1, s1b,
                   eLca_sb, eLca_sb.b)
    with ExitStack() as ctx2:
        _NAME_PREFIX[0] = "ff_"
        R = Res()
        alloc_ffn_res(ctx2, nc, R)
        load_common(R)
        wgu_b = dram_bufs(128, "wgu")
        wproj_b = dram_bufs(64, "wproj")
        wguh_b = dram_bufs(128, "wguh")
        wproh_b = dram_bufs(64, "wproh")
        wguT_b = dram_bufs(512, "wguT")
        wprojT_b = dram_bufs(256, "wprojT")
        emit_quant(P, R, wgu, wgu_b, wgu_hat, wguh_b, 128, 2048, "qg")
        emit_quant(P, R, wproj, wproj_b, wproj_hat, wproh_b, 16, 8192, "qp")
        emit_gu_transform(P, R, wgu_hat, wguh_b, wguT, wguT_b, "tg")
        emit_wp_transform(P, R, wproj_hat, wproh_b, wprojT, wprojT_b, "tp")
        emit_ffn(P, R, B, s1, s1b, wguT, wguT_b, wprojT, wprojT_b,
                 s2, s2b, tag="f")
        P.join_barrier()
    with ExitStack() as ctx2:
        _NAME_PREFIX[0] = "at_"
        R = Res()
        alloc_attn_res(ctx2, nc, R)
        load_common(R)
        R["ctab"] = SBuf(ctx2, nc, "ctab", [128, T], BF16)
        R["stab"] = SBuf(ctx2, nc, "stab", [128, T], BF16)
        R["swap_bf"] = SBuf(ctx2, nc, "swap_bf", [128, 128], BF16)
        R["masks"] = SBuf(ctx2, nc, "masks", [128, 2048], BF16)
        R["bq"] = SBuf(ctx2, nc, "bq", [128, 16], F32)
        R["bk"] = SBuf(ctx2, nc, "bk", [128, 16], F32)
        R["bo"] = SBuf(ctx2, nc, "bo", [128, 16], F32)
        R["bv"] = SBuf(ctx2, nc, "bv", [1, 2048], F32)
        for nm, src_ in (("ctab", cos_d), ("stab", sin_d),
                         ("swap_bf", swap_d), ("masks", masks_d)):
            P.dma("pool", (lambda e, nm=nm, src_=src_, R=R: e.dma_start(
                out=R[nm][:], in_=src_)), R.sem_c, writes=(R[nm].b,),
                tag="ac_" + nm)
        P.dma("sp", (lambda e, R=R: e.dma_start(out=R["bv"][:], in_=bvrow)),
              R.sem_c, writes=(R["bv"].b,), tag="ac_bv")
        for i, nm in enumerate(("bq", "bk", "bo")):
            P.dma("sp", (lambda e, nm=nm, i=i, R=R: e.dma_start(
                out=R[nm][:], in_=biases[:, i * 16:(i + 1) * 16])),
                R.sem_c, writes=(R[nm].b,), tag="ac_" + nm)
        io = {"wq": wq_a, "wk": wk_a, "wv": wv_a, "wo": wo_a,
              "qs_s": qs_s, "ks_s": ks_s, "vs_s": vs_s,
              "qs_bufs": dram_bufs(64, "qsx"), "ks_bufs": dram_bufs(64, "ksx"),
              "vs_bufs": dram_bufs(16, "vsx")}
        emit_attn(P, R, B, nc, s2, s2b, io, s3, s3b)
        P.join_barrier()
    with ExitStack() as ctx2:
        _NAME_PREFIX[0] = "mo_"
        R = Res()
        alloc_moe_res(ctx2, nc, R)
        load_common(R)
        for nm, src_, q in (("ones_row_bf", onesr, "pool"),
                            ("b1s", b1, "sp"), ("sb1s", sb1, "sp"),
                            ("b2s", b2, "pool"), ("sb2s", sb2, "pool"),
                            ("ones512", ones5, "pool"), ("ident_f", ident, "sp"),
                            ("ident_bf", ident, "pool"), ("onecol", onesc, "sp"),
                            ("sel8", sel8_d, "pool")):
            P.dma(q, (lambda e, nm=nm, src_=src_, R=R: e.dma_start(
                out=R[nm][:], in_=src_)), R.sem_c, writes=(R[nm].b,),
                tag="mc_" + nm)
        P.dma("sp", (lambda e, R=R: e.dma_start(
            out=R["gwf"][:].rearrange("p (g c) -> p g c", c=8),
            in_=gate_w.rearrange("(g p) c -> p g c", p=128))),
            R.sem_c, writes=(R["gwf"].b,), tag="mc_gwf")
        io = {"w1": w1, "w2": w2, "sw1": sw1, "sw2": sw2}
        emit_moe(P, R, B, nc, s3, s3b, io, s4, s4b)
        P.join_barrier()
    with ExitStack() as ctx2:
        _NAME_PREFIX[0] = "ex_"
        R = Res()
        R.update(semn)
        for nm in ["x_nat0", "x_nat1", "x_col0", "x_col1"]:
            R[nm] = SBuf(ctx2, nc, nm, [128, 2048], F32)
        R["psA"] = B[0]
        R["psB"] = B[1]
        R["ident_f32"] = SBuf(ctx2, nc, "ident_f32_s", [128, 128], F32)
        P.dma("sp", (lambda e, R=R: e.dma_start(out=R["ident_f32"][:],
                                                in_=ident)),
              R.sem_c, writes=(R["ident_f32"].b,), tag="xldid")
        emit_fm_to_nat_b(P, R, s4, s4b, out_nat, "x_", R.sem_x, R.sem_st)
    _NAME_PREFIX[0] = ""
    build_engine_sections(nc, P, ctx)
    ctx.close()
    return nc


def emit_nat_to_fm_b(P, R, src, dst, dst_bufs, pfx, sem_ld, sem_st):
    nat = [R[pfx + "nat0"], R[pfx + "nat1"]]
    col = [R[pfx + "col0"], R[pfx + "col1"]]
    ps = [R["psA"], R["psB"]]
    ident = R["ident_f32"]
    nT = T // 128
    for t in range(nT):
        nb = nat[t % 2]
        cb = col[t % 2]
        P.dma("sp", (lambda e, nb=nb, t=t: e.dma_start(
            out=nb[:], in_=src[t * 128:(t + 1) * 128, :])), sem_ld,
            writes=(nb.b,), tag=f"{pfx}ld{t}")
        for dg in range(D // 128):
            p = ps[dg % 2]
            P.step("pe", (lambda e, nb=nb, p=p, dg=dg: e.transpose(
                p[:, 0:128], nb[:, dg * 128:(dg + 1) * 128], ident[:])),
                reads=(nb.b, ident.b), writes=(p.b,), tag=f"{pfx}tr{t}.{dg}")
            P.step("dve", (lambda e, cb=cb, p=p, dg=dg: e.tensor_copy(
                cb[:, dg * 128:(dg + 1) * 128], p[:, 0:128])),
                reads=(p.b,), writes=(cb.b,), tag=f"{pfx}ev{t}.{dg}")
        dst_view = dst[:, t * 128:(t + 1) * 128].rearrange(
            "(g p) c -> p g c", p=128)
        P.dma("act", (lambda e, cb=cb, dst_view=dst_view: e.dma_start(
            out=dst_view, in_=cb[:].rearrange("p (g c) -> p g c", c=128))),
            sem_st, reads=(cb.b,), writes=(dst_bufs[t],), tag=f"{pfx}st{t}")


def emit_fm_to_nat_b(P, R, src, src_bufs, dst, pfx, sem_ld, sem_st):
    nat = [R[pfx + "nat0"], R[pfx + "nat1"]]
    col = [R[pfx + "col0"], R[pfx + "col1"]]
    ps = [R["psA"], R["psB"]]
    ident = R["ident_f32"]
    nT = T // 128
    for t in range(nT):
        cb = col[t % 2]
        nb = nat[t % 2]
        src_view = src[:, t * 128:(t + 1) * 128].rearrange(
            "(g p) c -> p g c", p=128)
        P.dma("sp", (lambda e, cb=cb, src_view=src_view: e.dma_start(
            out=cb[:].rearrange("p (g c) -> p g c", c=128), in_=src_view)),
            sem_ld, reads=(src_bufs[t],), writes=(cb.b,), tag=f"{pfx}ld{t}")
        for dg in range(D // 128):
            p = ps[dg % 2]
            P.step("pe", (lambda e, cb=cb, p=p, dg=dg: e.transpose(
                p[:, 0:128], cb[:, dg * 128:(dg + 1) * 128], ident[:])),
                reads=(cb.b, ident.b), writes=(p.b,), tag=f"{pfx}tr{t}.{dg}")
            P.step("dve", (lambda e, nb=nb, p=p, dg=dg: e.tensor_copy(
                nb[:, dg * 128:(dg + 1) * 128], p[:, 0:128])),
                reads=(p.b,), writes=(nb.b,), tag=f"{pfx}ev{t}.{dg}")
        P.dma("act", (lambda e, nb=nb, t=t: e.dma_start(
            out=dst[t * 128:(t + 1) * 128, :], in_=nb[:])),
            sem_st, reads=(nb.b,), writes=(), tag=f"{pfx}st{t}")


def kernel_fused(**inputs):
    inputs = {k: np.asarray(v) for k, v in inputs.items()}
    x_full = inputs["L0_attn_hidden_states"]
    consts = _host_consts()
    m = {}
    m.update(_prep_rwkv(inputs))
    m.update(_prep_attn(inputs))
    m.update(_prep_moe(inputs))
    m["L0_ffn_w_gate_up"] = inputs["L0_ffn_w_gate_up"]
    m["L0_ffn_w_proj"] = inputs["L0_ffn_w_proj"]
    m.update(consts)
    cores = list(range(NCORES))
    nc = _get_prog("fused", build_fused_program)
    maps = [dict(m, x_nat=np.ascontiguousarray(x_full[c])) for c in cores]
    res = run_bass_kernel_spmd(nc, maps, core_ids=cores)
    out = np.stack([r["out_nat"] for r in res.results], axis=0)
    return out.astype(np.float32)


kernel = kernel_fused


# version 08 (final)
# speedup vs baseline: 1.0114x; 1.0114x over previous
import numpy as np
import ml_dtypes
from contextlib import ExitStack

import concourse.bass as bass
import concourse.mybir as mybir
from concourse.bass_utils import run_bass_kernel_spmd

F32 = mybir.dt.float32
BF16 = mybir.dt.bfloat16
AF = mybir.ActivationFunctionType
ALU = mybir.AluOpType
AX = mybir.AxisListType

D = 2048
T = 2048
NCORES = 8


ENGINES = ("pe", "dve", "act", "pool", "sp")
DMA_QUEUES = ("sp", "act", "pool")


class Buf:
    __slots__ = ("name", "last_w", "readers")

    def __init__(self, name="b"):
        self.name = name
        self.last_w = None
        self.readers = []


class Step:
    __slots__ = ("eng", "fn", "deps", "need_inc", "inc_cum", "is_dma", "sem",
                 "dma_cum", "pos", "tag", "epoch", "join_totals")

    def __init__(self, eng, fn, is_dma=False, sem=None, tag=""):
        self.eng = eng
        self.fn = fn
        self.deps = []
        self.need_inc = False
        self.inc_cum = 0
        self.is_dma = is_dma
        self.sem = sem
        self.dma_cum = 0
        self.pos = 0
        self.tag = tag
        self.epoch = 0
        self.join_totals = None


class Prog:
    def __init__(self):
        self.steps = {e: [] for e in ENGINES}
        self.dma_sems = {}
        self.dma_sem_names = []
        self.init_dep = None
        self._touched = set()
        self.epoch = 0
        self.epoch_snapshots = []
        self._stream_sems = {}
        self._sem_pool = []
        self._last_dma = {}

    def _stream_sem(self, key, queue):
        key = (key, queue == "pool")
        if key in self._stream_sems:
            return self._stream_sems[key]
        if queue != "pool" and self._sem_pool:
            nm = self._sem_pool.pop()
        else:
            nm = self.new_dma_sem(f"a{len(self.dma_sem_names)}")
        self._stream_sems[key] = nm
        if queue == "pool":
            self._swdge_sems = getattr(self, "_swdge_sems", set())
            self._swdge_sems.add(nm)
        return nm

    def join_barrier(self, eng="pool"):
        bufs = tuple(self._touched)
        s = self.step(eng, (lambda e: e.engine_nop()), writes=bufs,
                      tag="joinbar")
        s.join_totals = {n: c for n, c in self.dma_sems.items() if c > 0}
        self._touched = set()
        self.init_dep = s
        self.epoch_snapshots.append(dict(self.dma_sems))
        swd = getattr(self, "_swdge_sems", set())
        self._sem_pool.extend(v for v in self._stream_sems.values()
                              if v not in swd)
        self._stream_sems = {}
        self.epoch += 1
        return s

    def new_dma_sem(self, name):
        assert name not in self.dma_sems
        self.dma_sems[name] = 0
        self.dma_sem_names.append(name)
        return name

    def _mkdeps(self, s, reads, writes, deps):
        dep = set(deps)
        for b in reads:
            if b.last_w is not None:
                dep.add(b.last_w)
        for b in writes:
            if b.last_w is not None:
                dep.add(b.last_w)
            dep.update(b.readers)
        for b in writes:
            if b.last_w is None and self.init_dep is not None:
                dep.add(self.init_dep)
        for b in reads:
            b.readers.append(s)
            self._touched.add(b)
        for b in writes:
            b.last_w = s
            b.readers = []
            self._touched.add(b)
        out = []
        for d in dep:
            if d is s:
                continue
            if (not d.is_dma) and d.eng == "pe" and s.eng == "pe":
                continue
            out.append(d)
        s.deps = out

    def step(self, eng, fn, reads=(), writes=(), deps=(), tag=""):
        s = Step(eng, fn, tag=tag)
        s.epoch = self.epoch
        s.pos = len(self.steps[eng])
        self.steps[eng].append(s)
        self._mkdeps(s, reads, writes, deps)
        return s

    def dma(self, queue, fn, sem, reads=(), writes=(), deps=(), tag="",
            stream=None):
        assert queue in DMA_QUEUES
        if stream is None:
            if writes:
                stream = writes[0].name.split(".")[0]
            elif reads:
                stream = reads[0].name.split(".")[0]
            else:
                stream = tag
        semname = self._stream_sem((sem, stream), queue)
        s = Step(queue, fn, is_dma=True, sem=semname, tag=tag)
        s.epoch = self.epoch
        s.pos = len(self.steps[queue])
        self.steps[queue].append(s)
        self.dma_sems[semname] += 1
        s.dma_cum = self.dma_sems[semname]
        prev = self._last_dma.get(semname)
        deps = tuple(deps) + ((prev,) if prev is not None else ())
        self._last_dma[semname] = s
        self._mkdeps(s, reads, writes, deps)
        return s

    def build(self, nc, engs, prog_sems, dma_sem_handles):
        for e in ENGINES:
            for s in self.steps[e]:
                for d in s.deps:
                    d.need_inc = True
        for e in ENGINES:
            cum = 0
            for s in self.steps[e]:
                if s.is_dma:
                    continue
                if s.need_inc:
                    cum += 1
                    s.inc_cum = cum
        def emit_engine(ename, eng):
            hw = {}
            for s in self.steps[ename]:
                waits = {}
                for d in s.deps:
                    if d.is_dma:
                        key = ("dma", d.sem)
                        if s.join_totals is not None:
                            val = 16 * s.join_totals.get(d.sem, d.dma_cum)
                        elif d.epoch < s.epoch:
                            val = 16 * self.epoch_snapshots[d.epoch].get(
                                d.sem, d.dma_cum)
                        else:
                            val = 16 * d.dma_cum
                    else:
                        key = ("eng", d.eng)
                        val = d.inc_cum
                        assert val > 0, (d.eng, d.tag, "dep without inc")
                    if waits.get(key, 0) < val:
                        waits[key] = val
                for key, val in sorted(waits.items(), key=lambda kv: repr(kv[0])):
                    if hw.get(key, 0) >= val:
                        continue
                    hw[key] = val
                    semh = (dma_sem_handles[key[1]] if key[0] == "dma"
                            else prog_sems[key[1]])
                    eng.wait_ge(semh, val)
                if s.fn is None:
                    inst = None
                else:
                    inst = s.fn(eng)
                if s.is_dma:
                    assert inst is not None, s.tag
                    inst.then_inc(dma_sem_handles[s.sem], 16)
                elif s.need_inc:
                    assert inst is not None, (s.tag, "need inc but fn returned None")
                    inst.then_inc(prog_sems[s.eng], 1)
        return emit_engine



def ceil_div(a, b):
    return (a + b - 1) // b


_NAME_PREFIX = [""]


class SBuf:

    def __init__(self, ctx, nc, name, shape, dtype, ntiles=1):
        name = _NAME_PREFIX[0] + name
        self.h = ctx.enter_context(nc.sbuf_tensor(name, shape, dtype))
        self.bufs = [Buf(f"{name}.{i}") for i in range(ntiles)]
        self.name = name

    def __getitem__(self, idx):
        return self.h[idx]

    @property
    def b(self):
        assert len(self.bufs) == 1
        return self.bufs[0]


class PSum:
    def __init__(self, ctx, nc, name, shape, dtype):
        self.h = ctx.enter_context(nc.psum_tensor(name, shape, dtype))
        self.buf = Buf(name)
        self.name = name

    def __getitem__(self, idx):
        return self.h[idx]

    @property
    def b(self):
        return self.buf



def emit_nat_to_fm(P, R, src, dst, pfx, sem_ld, sem_st):
    nat = [R[pfx + "nat0"], R[pfx + "nat1"]]
    col = [R[pfx + "col0"], R[pfx + "col1"]]
    ps = [R["psA"], R["psB"]]
    ident = R["ident_f32"]
    nT = T // 128
    for t in range(nT):
        nb = nat[t % 2]
        cb = col[t % 2]
        P.dma("sp", (lambda e, nb=nb, t=t: e.dma_start(
            out=nb[:], in_=src[t * 128:(t + 1) * 128, :])), sem_ld,
            reads=(), writes=(nb.b,), tag=f"{pfx}ld{t}")
        for dg in range(D // 128):
            p = ps[dg % 2]
            P.step("pe", (lambda e, nb=nb, p=p, dg=dg: e.transpose(
                p[:, 0:128], nb[:, dg * 128:(dg + 1) * 128], ident[:])),
                reads=(nb.b, ident.b), writes=(p.b,), tag=f"{pfx}tr{t}.{dg}")
            P.step("dve", (lambda e, cb=cb, p=p, dg=dg: e.tensor_copy(
                cb[:, dg * 128:(dg + 1) * 128], p[:, 0:128])),
                reads=(p.b,), writes=(cb.b,), tag=f"{pfx}ev{t}.{dg}")
        dst_view = dst[:, t * 128:(t + 1) * 128].rearrange(
            "(g p) c -> p g c", p=128)
        P.dma("act", (lambda e, cb=cb, dst_view=dst_view: e.dma_start(
            out=dst_view, in_=cb[:].rearrange("p (g c) -> p g c", c=128))),
            sem_st, reads=(cb.b,), writes=(), tag=f"{pfx}st{t}")


def emit_fm_to_nat(P, R, src, dst, pfx, sem_ld, sem_st):
    nat = [R[pfx + "nat0"], R[pfx + "nat1"]]
    col = [R[pfx + "col0"], R[pfx + "col1"]]
    ps = [R["psA"], R["psB"]]
    ident = R["ident_f32"]
    nT = T // 128
    for t in range(nT):
        cb = col[t % 2]
        nb = nat[t % 2]
        src_view = src[:, t * 128:(t + 1) * 128].rearrange(
            "(g p) c -> p g c", p=128)
        P.dma("sp", (lambda e, cb=cb, src_view=src_view: e.dma_start(
            out=cb[:].rearrange("p (g c) -> p g c", c=128), in_=src_view)),
            sem_ld, reads=(), writes=(cb.b,), tag=f"{pfx}ld{t}")
        for dg in range(D // 128):
            p = ps[dg % 2]
            P.step("pe", (lambda e, cb=cb, p=p, dg=dg: e.transpose(
                p[:, 0:128], cb[:, dg * 128:(dg + 1) * 128], ident[:])),
                reads=(cb.b, ident.b), writes=(p.b,), tag=f"{pfx}tr{t}.{dg}")
            P.step("dve", (lambda e, nb=nb, p=p, dg=dg: e.tensor_copy(
                nb[:, dg * 128:(dg + 1) * 128], p[:, 0:128])),
                reads=(p.b,), writes=(nb.b,), tag=f"{pfx}ev{t}.{dg}")
        P.dma("act", (lambda e, nb=nb, t=t: e.dma_start(
            out=dst[t * 128:(t + 1) * 128, :], in_=nb[:])),
            sem_st, reads=(nb.b,), writes=(), tag=f"{pfx}st{t}")


def build_engine_sections(nc, P, ctx):
    prog_sems = {e: ctx.enter_context(nc.semaphore("sem_" + e)) for e in ENGINES}
    dma_handles = {n: ctx.enter_context(nc.semaphore("dma_" + n))
                   for n in P.dma_sem_names}
    block = ctx.enter_context(nc.Block())
    emit = P.build(nc, None, prog_sems, dma_handles)

    final_waits = [(n, 16 * c) for n, c in P.dma_sems.items() if c > 0]

    @block.sync
    def _(s):
        emit("sp", s)
        for n, v in final_waits:
            s.wait_ge(dma_handles[n], v)

    @block.vector
    def _(v):
        emit("dve", v)

    @block.scalar
    def _(sc):
        emit("act", sc)

    @block.tensor
    def _(t):
        emit("pe", t)

    @block.gpsimd
    def _(g):
        emit("pool", g)



def emit_ln_store(P, R, rt, rt_bufs, xout_blocks, xout_bufs, banks, tag,
                  width=512):
    s1, s2, c0b, c1b = banks["s1"], banks["s2"], banks["c0"], banks["c1"]
    onc = R["ones_col_bf"]
    onr = R["ones_row_f"]
    for dg in range(16):
        rb = R["ln_rb0"]
        rq = R["ln_rq0"]
        P.step("act", (lambda e, rb=rb, dg=dg: e.copy(rb[:], rt[dg])),
               reads=(rt_bufs[dg],), writes=(rb.b,), tag=f"{tag}rb{dg}")
        P.step("dve", (lambda e, rb=rb, rq=rq: e.tensor_tensor(
            rq[:], rb[:], rb[:], op=ALU.mult)),
            reads=(rb.b,), writes=(rq.b,), tag=f"{tag}rq{dg}")
        P.step("pe", (lambda e, rb=rb, dg=dg: e.matmul(
            s1[0:1, 0:width], onc[:], rb[:, 0:width],
            start=(dg == 0), stop=(dg == 15))),
            reads=(rb.b, onc.b), writes=(s1.b,), tag=f"{tag}s1_{dg}")
        P.step("pe", (lambda e, rq=rq, dg=dg: e.matmul(
            s2[0:1, 0:width], onc[:], rq[:, 0:width],
            start=(dg == 0), stop=(dg == 15))),
            reads=(rq.b, onc.b), writes=(s2.b,), tag=f"{tag}s2_{dg}")
    mean, ex2, var, c0r = R["ln_mean"], R["ln_ex2"], R["ln_var"], R["ln_c0"]
    P.step("act", (lambda e: e.activation(mean[:], s1[0:1, 0:width], AF.Copy,
                                          scale=1.0 / D)),
           reads=(s1.b,), writes=(mean.b,), tag=f"{tag}mean")
    P.step("act", (lambda e: e.activation(ex2[:], s2[0:1, 0:width], AF.Copy,
                                          scale=1.0 / D)),
           reads=(s2.b,), writes=(ex2.b,), tag=f"{tag}ex2")
    P.step("dve", (lambda e: e.tensor_tensor(var[:], mean[:], mean[:],
                                             op=ALU.mult)),
           reads=(mean.b,), writes=(var.b,), tag=f"{tag}msq")
    P.step("dve", (lambda e: e.tensor_tensor(var[:], ex2[:], var[:],
                                             op=ALU.subtract)),
           reads=(ex2.b, var.b), writes=(var.b,), tag=f"{tag}var")
    P.step("act", (lambda e: e.activation(var[:], var[:], AF.Ln,
                                          bias=R["cvec"][0:1, 0:1])),
           reads=(var.b, R["cvec"].b), writes=(var.b,), tag=f"{tag}lnv")
    P.step("act", (lambda e: e.activation(var[:], var[:], AF.Exp, scale=-0.5)),
           reads=(var.b,), writes=(var.b,), tag=f"{tag}rstd")
    P.step("dve", (lambda e: e.scalar_tensor_tensor(
        c0r[:], mean[:], -1.0, var[:], op0=ALU.mult, op1=ALU.mult)),
        reads=(mean.b, var.b), writes=(c0r.b,), tag=f"{tag}c0")
    P.step("pe", (lambda e: e.matmul(c0b[:, 0:width], onr[:], c0r[:],
                                     start=True, stop=True)),
           reads=(c0r.b, onr.b), writes=(c0b.b,), tag=f"{tag}repc0")
    P.step("pe", (lambda e: e.matmul(c1b[:, 0:width], onr[:], var[:],
                                     start=True, stop=True)),
           reads=(var.b, onr.b), writes=(c1b.b,), tag=f"{tag}repc1")
    c0s, c1s = R["ln_c0s"], R["ln_c1s"]
    P.step("act", (lambda e: e.activation(c0s[:], c0b[:, 0:width], AF.Copy)),
           reads=(c0b.b,), writes=(c0s.b,), tag=f"{tag}c0s")
    P.step("act", (lambda e: e.activation(c1s[:], c1b[:, 0:width], AF.Copy)),
           reads=(c1b.b,), writes=(c1s.b,), tag=f"{tag}c1s")
    for dg in range(16):
        xn = R["ln_xn0"]
        P.step("dve", (lambda e, xn=xn, dg=dg: e.tensor_tensor(
            xn[:], rt[dg], c1s[:], op=ALU.mult)),
            reads=(rt_bufs[dg], c1s.b), writes=(xn.b,), tag=f"{tag}n1_{dg}")
        P.step("dve", (lambda e, xn=xn: e.tensor_tensor(
            xn[:], xn[:], c0s[:], op=ALU.add)),
            reads=(c0s.b, xn.b), writes=(xn.b,), tag=f"{tag}n2_{dg}")
        wb_ = xout_bufs[dg]
        if not isinstance(wb_, tuple):
            wb_ = (wb_,)
        P.dma("act", (lambda e, xn=xn, dg=dg: e.dma_start(
            out=xout_blocks[dg], in_=xn[:])), R.sem_st,
            reads=(xn.b,), writes=wb_, tag=f"{tag}st{dg}")



def emit_quant(P, R, w_dram, w_bufs, what_dram, what_bufs, nrow_tiles, ncol,
               tag):
    nchunk = ncol // 2048
    for rt in range(nrow_tiles):
        for ch in range(nchunk):
            idx = rt * nchunk + ch
            nat = R["qn0"]
            wb, rr = R["qwb"], R["qr"]
            aa = R["qa"]; bb = R["qwb"]
            qs, qsb, qrc = R["qs"], R["qsb"], R["qrc"]
            qsb2 = R["qsb2"]
            src = w_dram[rt * 128:(rt + 1) * 128, ch * 2048:(ch + 1) * 2048]
            P.dma("sp", (lambda e, nat=nat, src=src: e.dma_start(
                out=nat[:], in_=src)), R.sem_w,
                reads=(w_bufs[idx],), writes=(nat.b,), tag=f"{tag}ld{idx}")
            P.step("act", (lambda e, nat=nat: e.copy(R["qwb"][:], nat[:])),
                   reads=(nat.b,), writes=(wb.b,), tag=f"{tag}wb{idx}")
            P.step("dve", (lambda e: e.tensor_reduce(
                qs[:], R["qwb"][:].rearrange("p (g k) -> p g k", k=64),
                axis=AX.X, op=ALU.add, apply_absolute_value=True)),
                reads=(wb.b,), writes=(qs.b,), tag=f"{tag}as{idx}")
            P.step("dve", (lambda e: e.tensor_single_scalar(
                qsb[:], qs[:], 1.0 / 64, ALU.mult)),
                reads=(qs.b,), writes=(qsb.b,), tag=f"{tag}sc{idx}")
            P.step("dve", (lambda e: e.tensor_single_scalar(
                qsb2[:], qsb[:], 1e-8, ALU.max)),
                reads=(qsb.b,), writes=(qsb2.b,), tag=f"{tag}sc2{idx}")
            P.step("dve", (lambda e: e.reciprocal(qrc[:], qsb2[:])),
                   reads=(qsb2.b,), writes=(qrc.b,), tag=f"{tag}rc{idx}")
            P.step("dve", (lambda e: e.tensor_tensor(
                rr[:].rearrange("p (g k) -> p g k", k=64),
                R["qwb"][:].rearrange("p (g k) -> p g k", k=64),
                qrc[:].unsqueeze(2).to_broadcast([128, 32, 64]), op=ALU.mult)),
                reads=(wb.b, qrc.b), writes=(rr.b,), tag=f"{tag}r{idx}")
            P.step("dve", (lambda e: e.tensor_single_scalar(
                aa[:], rr[:], 0.5, ALU.is_gt)),
                reads=(rr.b,), writes=(aa.b,), tag=f"{tag}a{idx}")
            P.step("dve", (lambda e: e.tensor_single_scalar(
                bb[:], rr[:], -0.5, ALU.is_lt)),
                reads=(rr.b,), writes=(bb.b,), tag=f"{tag}b{idx}")
            P.step("dve", (lambda e: e.tensor_tensor(
                aa[:], aa[:], bb[:], op=ALU.subtract)),
                reads=(bb.b, aa.b), writes=(aa.b,), tag=f"{tag}q{idx}")
            P.step("dve", (lambda e: e.tensor_tensor(
                aa[:].rearrange("p (g k) -> p g k", k=64),
                aa[:].rearrange("p (g k) -> p g k", k=64),
                qsb2[:].unsqueeze(2).to_broadcast([128, 32, 64]), op=ALU.mult)),
                reads=(qsb2.b, aa.b), writes=(aa.b,), tag=f"{tag}wh{idx}")
            dst = what_dram[rt * 128:(rt + 1) * 128, ch * 2048:(ch + 1) * 2048]
            P.dma("act", (lambda e, dst=dst: e.dma_start(
                out=dst, in_=R["qa"][:])), R.sem_w,
                reads=(aa.b,), writes=(what_bufs[idx],), tag=f"{tag}st{idx}")


class Res(dict):
    __getattr__ = dict.__getitem__
    __setattr__ = dict.__setitem__


def emit_gu_transform(P, R, what_dram, what_bufs, wT_dram, wT_bufs, tag):
    for kind in range(2):
        for hblk in range(4):
            h0 = kind * 8192 + hblk * 2048
            for dg in range(16):
                tt = R["tt0"]
                src = what_dram[h0:h0 + 2048, dg * 128:(dg + 1) * 128]
                rbufs = [what_bufs[(h0 + i * 128) // 128] for i in range(16)]
                P.dma("sp", (lambda e, tt=tt, src=src: e.dma_start(
                    out=tt[:], in_=src, transpose=True)), R.sem_w,
                    reads=tuple(rbufs), writes=(tt.b,), tag=f"{tag}tl{kind}.{hblk}.{dg}")
                slots = [(hblk * 4 + i) * 32 + kind * 16 + dg for i in range(4)]
                dst = wT_dram.rearrange("(hb r) p c -> hb r p c", r=32)[
                    hblk * 4:hblk * 4 + 4, kind * 16 + dg, :, :].transpose([1, 0, 2])
                P.dma("act", (lambda e, tt=tt, dst=dst: e.dma_start(
                    out=dst, in_=tt[:].rearrange("p (b c) -> p b c", c=512))),
                    R.sem_w, reads=(tt.b,),
                    writes=tuple(wT_bufs[s] for s in slots),
                    tag=f"{tag}ts{kind}.{hblk}.{dg}")


def emit_wp_transform(P, R, what_dram, what_bufs, wT_dram, wT_bufs, tag):
    for kh in range(64):
        tt = R["tt0"]
        src = what_dram[0:2048, kh * 128:(kh + 1) * 128]
        rbufs = [what_bufs[i] for i in range(len(what_bufs))]
        P.dma("sp", (lambda e, tt=tt, src=src: e.dma_start(
            out=tt[:], in_=src, transpose=True)), R.sem_w,
            reads=tuple(rbufs), writes=(tt.b,), tag=f"{tag}tl{kh}")
        slots = [i * 64 + kh for i in range(4)]
        dst = wT_dram.rearrange("(db r) p c -> db r p c", r=64)[
            :, kh, :, :].transpose([1, 0, 2])
        P.dma("act", (lambda e, tt=tt, dst=dst: e.dma_start(
            out=dst, in_=tt[:].rearrange("p (b c) -> p b c", c=512))),
            R.sem_w, reads=(tt.b,),
            writes=tuple(wT_bufs[s] for s in slots), tag=f"{tag}ts{kh}")


def emit_ffn(P, R, B, xin, xin_bufs, wguT, wguT_bufs, wprojT, wprojT_bufs,
             xout, xout_bufs, tag="f", dbg=None):
    EPS32 = 1.1920929e-7
    xin_r = xin.rearrange("(g p) c -> p g c", p=128)
    for tau in range(4):
        tcols = slice(tau * 512, (tau + 1) * 512)
        tb = [xin_bufs[tau * 4 + i] for i in range(4)]
        x1bf = R["x1bf"]
        P.dma("pool", (lambda e, x1bf=x1bf, tcols=tcols: e.dma_start(
            out=x1bf[:], in_=xin_r[:, :, tcols])), R.sem_x,
            reads=tuple(tb), writes=(x1bf.b,), tag=f"{tag}xbf{tau}")
        psG, psU, s2h = B[0], B[1], B[2]
        for hb in range(16):
            wgg, wgu_ = R["wg_g"], R["wg_u"]
            for kind, wt in ((0, wgg), (1, wgu_)):
                slots = [hb * 32 + kind * 16 + dg for dg in range(16)]
                src = wguT.rearrange("(hb r) p c -> hb r p c", r=32)[
                    hb, kind * 16:(kind + 1) * 16, :, :].transpose([1, 0, 2])
                P.dma("sp", (lambda e, wt=wt, src=src: e.dma_start(
                    out=wt[:].rearrange("p (b c) -> p b c", c=512), in_=src)),
                    R.sem_w, reads=tuple(wguT_bufs[s] for s in slots),
                    writes=(wt.b,), tag=f"{tag}wg{tau}.{hb}.{kind}")
            for m in range(4):
                hidx = hb * 4 + m
                def mmG(e, wgg=wgg, x1bf=x1bf, m=m):
                    for dg in range(16):
                        inst = e.matmul(psG[:, 0:512],
                                        wgg[:, dg * 512 + m * 128:dg * 512 + (m + 1) * 128],
                                        x1bf[:, dg * 512:(dg + 1) * 512],
                                        start=(dg == 0), stop=(dg == 15))
                    return inst
                def mmU(e, wgu_=wgu_, x1bf=x1bf, m=m):
                    for dg in range(16):
                        inst = e.matmul(psU[:, 0:512],
                                        wgu_[:, dg * 512 + m * 128:dg * 512 + (m + 1) * 128],
                                        x1bf[:, dg * 512:(dg + 1) * 512],
                                        start=(dg == 0), stop=(dg == 15))
                    return inst
                P.step("pe", mmG, reads=(wgg.b, x1bf.b), writes=(psG.b,),
                       tag=f"{tag}G{tau}.{hidx}")
                P.step("pe", mmU, reads=(wgu_.b, x1bf.b), writes=(psU.b,),
                       tag=f"{tag}U{tau}.{hidx}")
                sg = R["sg0"]
                hsq = R["hq0"]
                P.step("act", (lambda e, sg=sg: e.activation(
                    sg[:], psG[:, 0:512], AF.Silu)),
                    reads=(psG.b,), writes=(sg.b,), tag=f"{tag}sg{tau}.{hidx}")
                hv = R["harena"][:, hidx * 512:(hidx + 1) * 512]
                hvb = R["harena"].bufs[hidx]
                P.step("dve", (lambda e, sg=sg, hv=hv: e.tensor_tensor(
                    hv, sg[:], psU[:, 0:512], op=ALU.mult)),
                    reads=(sg.b, psU.b), writes=(hvb,), tag=f"{tag}h{tau}.{hidx}")
                P.step("dve", (lambda e, hsq=hsq, hv=hv: e.tensor_tensor(
                    hsq[:], hv, hv, op=ALU.mult)),
                    reads=(hvb,), writes=(hsq.b,), tag=f"{tag}hq{tau}.{hidx}")
                if dbg is not None and tau == 0:
                    P.dma("pool", (lambda e, hv=hv, hidx=hidx: e.dma_start(
                        out=dbg["h"][hidx], in_=hv)), R.sem_st,
                        reads=(hvb,), tag=f"{tag}dbgh{hidx}")
                P.step("pe", (lambda e, hsq=hsq, hidx=hidx: e.matmul(
                    s2h[0:1, 0:512], R["ones_col_bf"][:], hsq[:],
                    start=(hidx == 0), stop=(hidx == 63))),
                    reads=(hsq.b, R["ones_col_bf"].b), writes=(s2h.b,),
                    tag=f"{tag}s2h{tau}.{hidx}")
        row = R["ln_var"]
        P.step("act", (lambda e: e.activation(row[:], s2h[0:1, 0:512], AF.Ln,
                                              bias=R["cvec"][0:1, 1:2],
                                              scale=1.0 / 8192)),
               reads=(s2h.b, R["cvec"].b), writes=(row.b,), tag=f"{tag}rl{tau}")
        P.step("act", (lambda e: e.activation(row[:], row[:], AF.Exp, scale=-0.5)),
               reads=(row.b,), writes=(row.b,), tag=f"{tag}re{tau}")
        rep = B[7]
        P.step("pe", (lambda e: e.matmul(rep[:, 0:512], R["ones_row_f"][:],
                                         row[:], start=True, stop=True)),
               reads=(row.b, R["ones_row_f"].b), writes=(rep.b,),
               tag=f"{tag}rrep{tau}")
        rstdh = R["rstdh"]
        P.step("act", (lambda e: e.activation(rstdh[:], rep[:, 0:512], AF.Copy)),
               reads=(rep.b,), writes=(rstdh.b,), tag=f"{tag}rcp{tau}")
        if dbg is not None and tau == 0:
            P.dma("pool", (lambda e: e.dma_start(out=dbg["rs"], in_=rstdh[:])),
                  R.sem_st, reads=(rstdh.b,), tag=f"{tag}dbgrs")
        psOs = [B[3], B[4], B[5], B[6]]
        for db in range(4):
            for kb in range(8):
                wp = R["wp"]
                slots = list(range(db * 64 + kb * 8, db * 64 + kb * 8 + 8))
                src = wprojT.rearrange("(a r) p c -> a r p c", r=8)[
                    db * 8 + kb, :, :, :].transpose([1, 0, 2])
                P.dma("sp", (lambda e, wp=wp, src=src: e.dma_start(
                    out=wp[:].rearrange("p (r c) -> p r c", c=512), in_=src)), R.sem_w,
                    reads=tuple(wprojT_bufs[s] for s in slots),
                    writes=(wp.b,), tag=f"{tag}wp{tau}.{db}.{kb}")
                for mm in range(4):
                    def mmO(e, wp=wp, kb=kb, mm=mm):
                        for k8 in range(8):
                            kh = kb * 8 + k8
                            inst = e.matmul(
                                psOs[mm][:, 0:512],
                                wp[:, k8 * 512 + mm * 128:k8 * 512 + (mm + 1) * 128],
                                R["harena"][:, kh * 512:(kh + 1) * 512],
                                start=(kh == 0), stop=(kh == 63))
                        return inst
                    hb_reads = tuple(R["harena"].bufs[kb * 8 + k8] for k8 in range(8))
                    P.step("pe", mmO, reads=(wp.b,) + hb_reads,
                           writes=(psOs[mm].b,), tag=f"{tag}O{tau}.{db}.{kb}.{mm}")
            for mm in range(4):
                dg = db * 4 + mm
                tf = R["tf0"]
                xf = R["xf0"]
                P.dma("sp", (lambda e, xf=xf, dg=dg, tcols=tcols: e.dma_start(
                    out=xf[:], in_=xin[dg * 128:(dg + 1) * 128, tcols])),
                    R.sem_x, reads=tuple(xin_bufs[tau * 4 + i] for i in range(4)), writes=(xf.b,),
                    tag=f"{tag}xf{tau}.{dg}")
                P.step("dve", (lambda e, tf=tf, mm=mm: e.tensor_tensor(
                    tf[:], rstdh[:], psOs[mm][:, 0:512], op=ALU.mult)),
                    reads=(rstdh.b, psOs[mm].b), writes=(tf.b,),
                    tag=f"{tag}t{tau}.{dg}")
                rt = R["rt"][:, dg * 512:(dg + 1) * 512]
                rtb = R["rt"].bufs[dg]
                P.step("dve", (lambda e, tf=tf, xf=xf, rt=rt: e.tensor_tensor(
                    rt, tf[:], xf[:], op=ALU.add)),
                    reads=(tf.b, xf.b), writes=(rtb,), tag=f"{tag}r{tau}.{dg}")
                if dbg is not None and tau == 0:
                    P.dma("pool", (lambda e, rt=rt, dg=dg: e.dma_start(
                        out=dbg["rt"][dg], in_=rt)), R.sem_st,
                        reads=(rtb,), tag=f"{tag}dbgrt{dg}")
        rt_views = [R["rt"][:, dg * 512:(dg + 1) * 512] for dg in range(16)]
        xout_blk = [xout[dg * 128:(dg + 1) * 128, tcols] for dg in range(16)]
        xo_bufs = [tuple(xout_bufs[tau * 4 + i] for i in range(4))
                   for dg in range(16)]
        banks = {"s1": B[0], "s2": B[1], "c0": B[2], "c1": B[7]}
        emit_ln_store(P, R, rt_views, R["rt"].bufs, xout_blk, xo_bufs, banks,
                      tag=f"{tag}ln{tau}.")



def dram_bufs(n, name):
    return [Buf(f"{name}.{i}") for i in range(n)]


def build_ffn_program(xin_name="s1", xout_name="s2", debug=False):
    nc = bass.Bass("TRN2", target_bir_lowering=False, debug=False)
    xin = nc.dram_tensor(xin_name, [D, T], F32, kind="ExternalInput").ap()
    wgu = nc.dram_tensor("L0_ffn_w_gate_up", [16384, 2048], F32,
                         kind="ExternalInput").ap()
    wproj = nc.dram_tensor("L0_ffn_w_proj", [2048, 8192], F32,
                           kind="ExternalInput").ap()
    xout = nc.dram_tensor(xout_name, [D, T], F32, kind="ExternalOutput").ap()
    kind_dbg = "ExternalOutput" if debug else "Internal"
    wgu_hat = nc.dram_tensor("wgu_hat", [16384, 2048], BF16, kind=kind_dbg).ap()
    wproj_hat = nc.dram_tensor("wproj_hat", [2048, 8192], BF16, kind=kind_dbg).ap()
    wguT = nc.dram_tensor("wguT", [512, 128, 512], BF16).ap()
    wprojT = nc.dram_tensor("wprojT", [256, 128, 512], BF16).ap()

    ctx = ExitStack()
    P = Prog()
    R = Res()
    R["sem_w"] = P.new_dma_sem("w")
    R["sem_x"] = P.new_dma_sem("x")
    R["sem_st"] = P.new_dma_sem("st")
    R["sem_c"] = P.new_dma_sem("c")

    onesc = nc.dram_tensor("ones_col", [128, 1], F32, kind="ExternalInput").ap()
    onesr = nc.dram_tensor("ones_row", [1, 128], F32, kind="ExternalInput").ap()
    cvec = nc.dram_tensor("cvec", [128, 8], F32, kind="ExternalInput").ap()
    R["ones_col_bf"] = SBuf(ctx, nc, "ones_col_bf", [128, 1], BF16)
    R["ones_row_f"] = SBuf(ctx, nc, "ones_row_f", [1, 128], F32)
    P.dma("pool", lambda e: e.dma_start(out=R["ones_col_bf"][:], in_=onesc),
          R.sem_c, writes=(R["ones_col_bf"].b,), tag="c1")
    P.dma("sp", lambda e: e.dma_start(out=R["ones_row_f"][:], in_=onesr),
          R.sem_c, writes=(R["ones_row_f"].b,), tag="c2")
    R["cvec"] = SBuf(ctx, nc, "cvec_s", [128, 8], F32)
    P.dma("sp", lambda e: e.dma_start(out=R["cvec"][:], in_=cvec),
          R.sem_c, writes=(R["cvec"].b,), tag="c3")

    alloc_ffn_res(ctx, nc, R)
    B = [PSum(ctx, nc, f"bank{i}", [128, 512], F32) for i in range(8)]

    wgu_b = dram_bufs(128, "wgu")
    wproj_b = dram_bufs(64, "wproj")
    wguh_b = dram_bufs(128, "wguh")
    wproh_b = dram_bufs(64, "wproh")
    wguT_b = dram_bufs(512, "wguT")
    wprojT_b = dram_bufs(256, "wprojT")
    xin_b = dram_bufs(16, "xin")
    xout_b = dram_bufs(16, "xout")

    emit_quant(P, R, wgu, wgu_b, wgu_hat, wguh_b, 128, 2048, "qg")
    emit_quant(P, R, wproj, wproj_b, wproj_hat, wproh_b, 16, 8192, "qp")
    emit_gu_transform(P, R, wgu_hat, wguh_b, wguT, wguT_b, "tg")
    emit_wp_transform(P, R, wproj_hat, wproh_b, wprojT, wprojT_b, "tp")
    dbg = None
    if debug:
        dbg_h = nc.dram_tensor("dbg_h", [64, 128, 512], BF16, kind="ExternalOutput").ap()
        dbg_rt = nc.dram_tensor("dbg_rt", [16, 128, 512], F32, kind="ExternalOutput").ap()
        dbg_rs = nc.dram_tensor("dbg_rs", [128, 512], F32, kind="ExternalOutput").ap()
        dbg = {"h": dbg_h, "rt": dbg_rt, "rs": dbg_rs}
    emit_ffn(P, R, B, xin, xin_b, wguT, wguT_b, wprojT, wprojT_b,
             xout, xout_b, tag="f", dbg=dbg)
    build_engine_sections(nc, P, ctx)
    ctx.close()
    return nc


def alloc_ffn_res(ctx, nc, R):
    R["qn0"] = SBuf(ctx, nc, "qn0", [128, 2048], F32)
    R["qwb"] = SBuf(ctx, nc, "qwb", [128, 2048], BF16)
    R["qr"] = SBuf(ctx, nc, "qr", [128, 2048], BF16)
    R["qa"] = SBuf(ctx, nc, "qa", [128, 2048], BF16)
    R["qs"] = SBuf(ctx, nc, "qs", [128, 32], F32)
    R["qsb"] = SBuf(ctx, nc, "qsb", [128, 32], BF16)
    R["qsb2"] = SBuf(ctx, nc, "qsb2", [128, 32], BF16)
    R["qrc"] = SBuf(ctx, nc, "qrc", [128, 32], F32)
    R["tt0"] = SBuf(ctx, nc, "tt0", [128, 2048], BF16)
    R["x1bf"] = SBuf(ctx, nc, "x1bf", [128, 8192], BF16)
    R["wg_g"] = SBuf(ctx, nc, "wg_g", [128, 8192], BF16)
    R["wg_u"] = SBuf(ctx, nc, "wg_u", [128, 8192], BF16)
    R["wp"] = SBuf(ctx, nc, "wp", [128, 4096], BF16)
    R["harena"] = SBuf(ctx, nc, "harena", [128, 64 * 512], BF16, ntiles=64)
    R["rt"] = SBuf(ctx, nc, "rt", [128, 16 * 512], F32, ntiles=16)
    R["rstdh"] = SBuf(ctx, nc, "rstdh", [128, 512], F32)
    for nm in ["sg0", "tf0", "xf0"]:
        R[nm] = SBuf(ctx, nc, nm, [128, 512], F32)
    for nm in ["hq0", "ln_rb0", "ln_rq0"]:
        R[nm] = SBuf(ctx, nc, nm, [128, 512], BF16)
    for nm in ["ln_xn0", "ln_c0s", "ln_c1s"]:
        R[nm] = SBuf(ctx, nc, nm, [128, 512], F32)
    for nm in ["ln_mean", "ln_ex2", "ln_var", "ln_c0"]:
        R[nm] = SBuf(ctx, nc, nm, [1, 512], F32)



def emit_attn(P, R, B, nc, xin, xin_bufs, io, xout, xout_bufs, tag="a"):
    xin_r = xin.rearrange("(g p) c -> p g c", p=128)
    SC = 1.0 / np.sqrt(128.0)
    qsb_bufs = io["qs_bufs"]; ksb_bufs = io["ks_bufs"]; vsb_bufs = io["vs_bufs"]
    for tau in range(4):
        tcols = slice(tau * 512, (tau + 1) * 512)
        tb = tuple(xin_bufs[tau * 4 + i] for i in range(4))
        x2bf = R["x1bf"]
        P.dma("pool", (lambda e, x2bf=x2bf, tcols=tcols: e.dma_start(
            out=x2bf[:], in_=xin_r[:, :, tcols])), R.sem_x,
            reads=tb, writes=(x2bf.b,), tag=f"{tag}xbf{tau}")
        for proj, wT, bias_sb, scr, scr_bufs in (
                ("q", io["wq"], R["bq"], io["qs_s"], qsb_bufs),
                ("k", io["wk"], R["bk"], io["ks_s"], ksb_bufs)):
            for dt in range(16):
                wc = R["wqk"]
                src = wT[:, dt * 128:(dt + 1) * 128].rearrange(
                    "(g p) c -> p g c", p=128)
                P.dma("pool", (lambda e, wc=wc, src=src: e.dma_start(
                    out=wc[:].rearrange("p (g c) -> p g c", c=128), in_=src)),
                    R.sem_w, writes=(wc.b,), tag=f"{tag}w{proj}{tau}.{dt}")
                ps = B[0]
                def mm(e, wc=wc, x2bf=x2bf):
                    for dg in range(16):
                        inst = e.matmul(ps[:, 0:512],
                                        wc[:, dg * 128:(dg + 1) * 128],
                                        x2bf[:, dg * 512:(dg + 1) * 512],
                                        start=(dg == 0), stop=(dg == 15))
                    return inst
                P.step("pe", mm, reads=(wc.b, x2bf.b), writes=(ps.b,),
                       tag=f"{tag}mm{proj}{tau}.{dt}")
                qraw = R["qraw"]
                P.step("act", (lambda e, qraw=qraw, dt=dt, bias_sb=bias_sb:
                               e.activation(qraw[:], ps[:, 0:512], AF.Identity,
                                            bias=bias_sb[:, dt:dt + 1])),
                       reads=(ps.b, bias_sb.b), writes=(qraw.b,),
                       tag=f"{tag}ev{proj}{tau}.{dt}")
                ps2 = B[3]
                P.step("pe", (lambda e, qraw=qraw, ps2=ps2: e.matmul(
                    ps2[:, 0:512], R["swap_bf"][:], qraw[:],
                    start=True, stop=True)),
                    reads=(qraw.b, R["swap_bf"].b), writes=(ps2.b,),
                    tag=f"{tag}sw{proj}{tau}.{dt}")
                swb = R["swb"]
                P.step("act", (lambda e, swb=swb, ps2=ps2: e.activation(
                    swb[:], ps2[:, 0:512], AF.Copy)),
                    reads=(ps2.b,), writes=(swb.b,), tag=f"{tag}swe{proj}{tau}.{dt}")
                t1 = R["rt1"]
                qa = R["qkarena"]
                qav = qa[:, 0:512]
                P.step("dve", (lambda e, qraw=qraw, t1=t1, tcols=tcols:
                               e.tensor_tensor(t1[:], qraw[:],
                                               R["ctab"][:, tcols], op=ALU.mult)),
                       reads=(qraw.b, R["ctab"].b), writes=(t1.b,),
                       tag=f"{tag}r1{proj}{tau}.{dt}")
                t2 = R["rt2"]
                P.step("dve", (lambda e, swb=swb, t2=t2, tcols=tcols:
                                e.tensor_tensor(t2[:], swb[:],
                                                R["stab"][:, tcols], op=ALU.mult)),
                       reads=(swb.b, R["stab"].b), writes=(t2.b,),
                       tag=f"{tag}r2{proj}{tau}.{dt}")
                P.step("dve", (lambda e, t1=t1, t2=t2, qav=qav:
                               e.tensor_tensor(qav, t1[:], t2[:], op=ALU.add)),
                       reads=(t1.b, t2.b), writes=(qa.b,), tag=f"{tag}r3{proj}{tau}.{dt}")
                P.dma("act", (lambda e, qav=qav, scr=scr, dt=dt, tcols=tcols:
                              e.dma_start(out=scr[dt * 128:(dt + 1) * 128, tcols],
                                          in_=qav)), R.sem_st,
                      reads=(qa.b,), writes=(scr_bufs[dt * 4 + tau],),
                      tag=f"{tag}st{proj}{tau}.{dt}")
        for dc in range(4):
            wv = R["wvc"]
            src = io["wv"][:, dc * 512:(dc + 1) * 512].rearrange(
                "(g p) c -> p g c", p=128)
            P.dma("pool", (lambda e, wv=wv, src=src: e.dma_start(
                out=wv[:].rearrange("p (g c) -> p g c", c=512), in_=src)),
                R.sem_w, writes=(wv.b,), tag=f"{tag}wv{tau}.{dc}")
            for tt in range(4):
                ps = B[1 + tt % 2]
                def mmv(e, wv=wv, x2bf=x2bf, tt=tt, dc=dc, ps=ps):
                    for dg in range(16):
                        e.matmul(ps[:, 0:512],
                                 x2bf[:, dg * 512 + tt * 128:dg * 512 + (tt + 1) * 128],
                                 wv[:, dg * 512:(dg + 1) * 512],
                                 start=(dg == 0), stop=False)
                    inst = e.matmul(ps[:, 0:512], R["ones_row_f"][:],
                                    R["bv"][0:1, dc * 512:(dc + 1) * 512],
                                    start=False, stop=True)
                    return inst
                P.step("pe", mmv,
                       reads=(wv.b, x2bf.b, R["ones_row_f"].b, R["bv"].b),
                       writes=(ps.b,), tag=f"{tag}mmv{tau}.{dc}.{tt}")
                vsb = R["vsb0" if tt % 2 == 0 else "vsb1"]
                P.step("act", (lambda e, ps=ps, vsb=vsb: e.activation(
                    vsb[:], ps[:, 0:512], AF.Copy)),
                    reads=(ps.b,), writes=(vsb.b,), tag=f"{tag}evv{tau}.{dc}.{tt}")
                P.dma("act", (lambda e, vsb=vsb, tau=tau, tt=tt, dc=dc:
                              e.dma_start(out=io["vs_s"][
                                  tau * 512 + tt * 128:tau * 512 + (tt + 1) * 128,
                                  dc * 512:(dc + 1) * 512], in_=vsb[:])),
                      R.sem_st, reads=(vsb.b,),
                      writes=(vsb_bufs[tau * 4 + dc],), tag=f"{tag}stv{tau}.{dc}.{tt}")
    for qt in range(4):
        tcols = slice(qt * 512, (qt + 1) * 512)
        for h in range(16):
            qtile = R["qtile"]
            P.dma("sp", (lambda e, qtile=qtile, h=h, tcols=tcols: e.dma_start(
                out=qtile[:], in_=io["qs_s"][h * 128:(h + 1) * 128, tcols])),
                R.sem_x, reads=(qsb_bufs[h * 4 + qt],), writes=(qtile.b,),
                tag=f"{tag}lq{qt}.{h}")
            kh, vh = R["kh"], R["vh"]
            for kc in range(qt + 1):
                P.dma("sp", (lambda e, kh=kh, h=h, kc=kc: e.dma_start(
                    out=kh[:, kc * 512:(kc + 1) * 512],
                    in_=io["ks_s"][h * 128:(h + 1) * 128, kc * 512:(kc + 1) * 512])),
                    R.sem_x, reads=(ksb_bufs[h * 4 + kc],), writes=(kh.b,),
                    tag=f"{tag}lk{qt}.{h}.{kc}")
                P.dma("sp", (lambda e, vh=vh, h=h, kc=kc: e.dma_start(
                    out=vh[:].rearrange("p (g c) -> p g c", c=128)[:, kc * 4:(kc + 1) * 4, :],
                    in_=io["vs_s"][kc * 512:(kc + 1) * 512,
                                   h * 128:(h + 1) * 128].rearrange(
                        "(g p) c -> p g c", p=128))),
                    R.sem_x, reads=(vsb_bufs[kc * 4 + h // 4],), writes=(vh.b,),
                    tag=f"{tag}lv{qt}.{h}.{kc}")
            pso, psl = B[4], B[5]
            nkb = (qt + 1) * 4
            for kb in range(nkb):
                pss = B[2 + kb % 2]
                P.step("pe", (lambda e, pss=pss, kh=kh, qtile=qtile, kb=kb:
                              e.matmul(pss[:, 0:512],
                                       kh[:, kb * 128:(kb + 1) * 128],
                                       qtile[:], start=True, stop=True)),
                       reads=(kh.b, qtile.b), writes=(pss.b,),
                       tag=f"{tag}s{qt}.{h}.{kb}")
                pt = R["pt0" if kb % 2 == 0 else "pt1"]
                P.step("act", (lambda e, pss=pss, pt=pt: e.activation(
                    pt[:], pss[:, 0:512], AF.Exp, scale=SC,
                    bias=R["cvec"][:, 2:3])),
                    reads=(pss.b, R["cvec"].b), writes=(pt.b,),
                    tag=f"{tag}e{qt}.{h}.{kb}")
                s_off = kb - qt * 4
                if s_off >= 0:
                    P.step("dve", (lambda e, pt=pt, s_off=s_off: e.tensor_tensor(
                        pt[:], pt[:], R["masks"][:, s_off * 512:(s_off + 1) * 512],
                        op=ALU.mult)),
                        reads=(R["masks"].b, pt.b), writes=(pt.b,),
                        tag=f"{tag}m{qt}.{h}.{kb}")
                P.step("pe", (lambda e, pt=pt, vh=vh, kb=kb, pso=pso, nkb=nkb:
                              e.matmul(pso[:, 0:512],
                                       vh[:, kb * 128:(kb + 1) * 128],
                                       pt[:], start=(kb == 0), stop=(kb == nkb - 1))),
                       reads=(vh.b, pt.b), writes=(pso.b,),
                       tag=f"{tag}o{qt}.{h}.{kb}")
                P.step("pe", (lambda e, pt=pt, kb=kb, psl=psl, nkb=nkb:
                              e.matmul(psl[0:1, 0:512], R["ones_col_bf"][:],
                                       pt[:], start=(kb == 0), stop=(kb == nkb - 1))),
                       reads=(pt.b, R["ones_col_bf"].b), writes=(psl.b,),
                       tag=f"{tag}l{qt}.{h}.{kb}")
            linv = R["linv"]
            P.step("dve", (lambda e, linv=linv, psl=psl: e.reciprocal(
                linv[0:1, :], psl[0:1, 0:512])),
                reads=(psl.b,), writes=(linv.b,), tag=f"{tag}li{qt}.{h}")
            psr = B[6]
            P.step("pe", (lambda e, linv=linv, psr=psr: e.matmul(
                psr[:, 0:512], R["ones_row_f"][:], linv[0:1, :],
                start=True, stop=True)),
                reads=(linv.b, R["ones_row_f"].b), writes=(psr.b,),
                tag=f"{tag}lr{qt}.{h}")
            linv_s = R["linv_s"]
            P.step("act", (lambda e, linv_s=linv_s, psr=psr: e.activation(
                linv_s[:], psr[:, 0:512], AF.Copy)),
                reads=(psr.b,), writes=(linv_s.b,), tag=f"{tag}lc{qt}.{h}")
            at = R["attT"][:, h * 512:(h + 1) * 512]
            atb = R["attT"].bufs[h]
            P.step("dve", (lambda e, at=at, pso=pso, linv_s=linv_s:
                           e.tensor_tensor(at, linv_s[:], pso[:, 0:512],
                                           op=ALU.mult)),
                   reads=(linv_s.b, pso.b), writes=(atb,), tag=f"{tag}av{qt}.{h}")
        for dt in range(16):
            wo = R["woc"]
            src = io["wo"][:, dt * 128:(dt + 1) * 128].rearrange(
                "(g p) c -> p g c", p=128)
            P.dma("pool", (lambda e, wo=wo, src=src: e.dma_start(
                out=wo[:].rearrange("p (g c) -> p g c", c=128), in_=src)),
                R.sem_w, writes=(wo.b,), tag=f"{tag}wo{qt}.{dt}")
            ps = B[7]
            def mmo(e, wo=wo):
                for dg in range(16):
                    inst = e.matmul(ps[:, 0:512],
                                    wo[:, dg * 128:(dg + 1) * 128],
                                    R["attT"][:, dg * 512:(dg + 1) * 512],
                                    start=(dg == 0), stop=(dg == 15))
                return inst
            P.step("pe", mmo, reads=(wo.b,) + tuple(R["attT"].bufs),
                   writes=(ps.b,), tag=f"{tag}mmo{qt}.{dt}")
            tf = R["tf0"]
            P.step("act", (lambda e, tf=tf, ps=ps, dt=dt: e.activation(
                tf[:], ps[:, 0:512], AF.Identity, bias=R["bo"][:, dt:dt + 1])),
                reads=(ps.b, R["bo"].b), writes=(tf.b,), tag=f"{tag}to{qt}.{dt}")
            xf = R["xf0"]
            P.dma("sp", (lambda e, xf=xf, dt=dt, tcols=tcols: e.dma_start(
                out=xf[:], in_=xin[dt * 128:(dt + 1) * 128, tcols])),
                R.sem_x, reads=tuple(xin_bufs[qt * 4 + i] for i in range(4)),
                writes=(xf.b,), tag=f"{tag}xo{qt}.{dt}")
            rt = R["rt"][:, dt * 512:(dt + 1) * 512]
            rtb = R["rt"].bufs[dt]
            P.step("dve", (lambda e, tf=tf, xf=xf, rt=rt: e.tensor_tensor(
                rt, tf[:], xf[:], op=ALU.add)),
                reads=(tf.b, xf.b), writes=(rtb,), tag=f"{tag}r{qt}.{dt}")
        rt_views = [R["rt"][:, dg * 512:(dg + 1) * 512] for dg in range(16)]
        xout_blk = [xout[dg * 128:(dg + 1) * 128, tcols] for dg in range(16)]
        xo_bufs = [tuple(xout_bufs[qt * 4 + i] for i in range(4))
                   for dg in range(16)]
        banks = {"s1": B[0], "s2": B[1], "c0": B[2], "c1": B[3]}
        emit_ln_store(P, R, rt_views, R["rt"].bufs, xout_blk, xo_bufs, banks,
                      tag=f"{tag}ln{qt}.")


def alloc_attn_res(ctx, nc, R):
    R["x1bf"] = SBuf(ctx, nc, "x1bf", [128, 8192], BF16)
    R["wqk"] = SBuf(ctx, nc, "wqk", [128, 2048], BF16)
    R["wvc"] = SBuf(ctx, nc, "wvc", [128, 8192], BF16)
    R["woc"] = SBuf(ctx, nc, "woc", [128, 2048], BF16)
    R["qraw"] = SBuf(ctx, nc, "qraw", [128, 512], BF16)
    R["swb"] = SBuf(ctx, nc, "swb", [128, 512], BF16)
    R["rt1"] = SBuf(ctx, nc, "rt1", [128, 512], BF16)
    R["rt2"] = SBuf(ctx, nc, "rt2", [128, 512], BF16)
    R["qkarena"] = SBuf(ctx, nc, "qkarena", [128, 512], BF16)
    R["vsb0"] = SBuf(ctx, nc, "vsb0", [128, 512], BF16)
    R["vsb1"] = SBuf(ctx, nc, "vsb1", [128, 512], BF16)
    R["qtile"] = SBuf(ctx, nc, "qtile", [128, 512], BF16)
    R["kh"] = SBuf(ctx, nc, "kh", [128, 2048], BF16)
    R["vh"] = SBuf(ctx, nc, "vh", [128, 2048], BF16)
    R["pt0"] = SBuf(ctx, nc, "pt0", [128, 512], BF16)
    R["pt1"] = SBuf(ctx, nc, "pt1", [128, 512], BF16)
    R["linv"] = SBuf(ctx, nc, "linv", [1, 512], F32)
    R["linv_s"] = SBuf(ctx, nc, "linv_s", [128, 512], F32)
    R["attT"] = SBuf(ctx, nc, "attT", [128, 16 * 512], BF16, ntiles=16)
    R["rt"] = SBuf(ctx, nc, "rt", [128, 16 * 512], F32, ntiles=16)
    for nm in ["tf0", "xf0"]:
        R[nm] = SBuf(ctx, nc, nm, [128, 512], F32)
    for nm in ["ln_rb0", "ln_rq0"]:
        R[nm] = SBuf(ctx, nc, nm, [128, 512], BF16)
    for nm in ["ln_xn0", "ln_c0s", "ln_c1s"]:
        R[nm] = SBuf(ctx, nc, nm, [128, 512], F32)
    for nm in ["ln_mean", "ln_ex2", "ln_var", "ln_c0"]:
        R[nm] = SBuf(ctx, nc, nm, [1, 512], F32)


def build_attn_program(xin_name="s2", xout_name="s3"):
    nc = bass.Bass("TRN2", target_bir_lowering=False, debug=False)
    xin = nc.dram_tensor(xin_name, [D, T], F32, kind="ExternalInput").ap()
    wq = nc.dram_tensor("wqT", [D, D], F32, kind="ExternalInput").ap()
    wk = nc.dram_tensor("wkT", [D, D], F32, kind="ExternalInput").ap()
    wv = nc.dram_tensor("wvT", [D, D], F32, kind="ExternalInput").ap()
    wo = nc.dram_tensor("woT", [D, D], F32, kind="ExternalInput").ap()
    biases = nc.dram_tensor("attn_biases", [128, 64], F32, kind="ExternalInput").ap()
    cos_d = nc.dram_tensor("rope_ctab", [128, T], F32, kind="ExternalInput").ap()
    sin_d = nc.dram_tensor("rope_stab", [128, T], F32, kind="ExternalInput").ap()
    swap_d = nc.dram_tensor("rope_swap", [128, 128], F32, kind="ExternalInput").ap()
    masks_d = nc.dram_tensor("causal_masks", [128, 2048], F32, kind="ExternalInput").ap()
    onesc = nc.dram_tensor("ones_col", [128, 1], F32, kind="ExternalInput").ap()
    onesr = nc.dram_tensor("ones_row", [1, 128], F32, kind="ExternalInput").ap()
    cvec = nc.dram_tensor("cvec", [128, 8], F32, kind="ExternalInput").ap()
    bvrow = nc.dram_tensor("bv_row", [1, 2048], F32, kind="ExternalInput").ap()
    xout = nc.dram_tensor(xout_name, [D, T], F32, kind="ExternalOutput").ap()
    qs_s = nc.dram_tensor("qs_s", [D, T], BF16).ap()
    ks_s = nc.dram_tensor("ks_s", [D, T], BF16).ap()
    vs_s = nc.dram_tensor("vs_s", [T, D], BF16).ap()

    ctx = ExitStack()
    P = Prog()
    R = Res()
    for s in ("w", "x", "st", "c"):
        R["sem_" + s] = P.new_dma_sem(s)
    alloc_attn_res(ctx, nc, R)
    B = [PSum(ctx, nc, f"bank{i}", [128, 512], F32) for i in range(8)]
    R["ones_col_bf"] = SBuf(ctx, nc, "ones_col_bf", [128, 1], BF16)
    R["ones_row_f"] = SBuf(ctx, nc, "ones_row_f", [1, 128], F32)
    R["cvec"] = SBuf(ctx, nc, "cvec_s", [128, 8], F32)
    R["ctab"] = SBuf(ctx, nc, "ctab", [128, T], BF16)
    R["stab"] = SBuf(ctx, nc, "stab", [128, T], BF16)
    R["swap_bf"] = SBuf(ctx, nc, "swap_bf", [128, 128], BF16)
    R["masks"] = SBuf(ctx, nc, "masks", [128, 2048], BF16)
    R["bq"] = SBuf(ctx, nc, "bq", [128, 16], F32)
    R["bk"] = SBuf(ctx, nc, "bk", [128, 16], F32)
    R["bo"] = SBuf(ctx, nc, "bo", [128, 16], F32)
    R["bv"] = SBuf(ctx, nc, "bv", [1, 2048], F32)
    for nm, src_ in (("ones_col_bf", onesc), ("ones_row_f", onesr),
                     ("cvec", cvec), ("ctab", cos_d), ("stab", sin_d),
                     ("swap_bf", swap_d), ("masks", masks_d), ("bv", bvrow)):
        P.dma("pool" if nm in ("ones_col_bf", "masks", "ctab", "stab",
                               "swap_bf") else "sp",
              (lambda e, nm=nm, src_=src_: e.dma_start(out=R[nm][:], in_=src_)),
              R.sem_c, writes=(R[nm].b,), tag="c_" + nm)
    for i, nm in enumerate(("bq", "bk", "bo")):
        P.dma("sp", (lambda e, nm=nm, i=i: e.dma_start(
            out=R[nm][:], in_=biases[:, i * 16:(i + 1) * 16])),
            R.sem_c, writes=(R[nm].b,), tag="c_" + nm)

    io = {"wq": wq, "wk": wk, "wv": wv, "wo": wo,
          "qs_s": qs_s, "ks_s": ks_s, "vs_s": vs_s,
          "qs_bufs": dram_bufs(64, "qs"), "ks_bufs": dram_bufs(64, "ks"),
          "vs_bufs": dram_bufs(16, "vs")}
    xin_b = dram_bufs(16, "xin")
    xout_b = dram_bufs(16, "xout")
    emit_attn(P, R, B, nc, xin, xin_b, io, xout, xout_b)
    build_engine_sections(nc, P, ctx)
    ctx.close()
    return nc



def emit_moe(P, R, B, nc, xin, xin_bufs, io, xout, xout_bufs, tag="m"):
    xin_r = xin.rearrange("(g p) c -> p g c", p=128)
    for tau in range(4):
        tcols = slice(tau * 512, (tau + 1) * 512)
        tb = tuple(xin_bufs[tau * 4 + i] for i in range(4))
        x3bf = R["x1bf"]
        P.dma("pool", (lambda e, x3bf=x3bf, tcols=tcols: e.dma_start(
            out=x3bf[:], in_=xin_r[:, :, tcols])), R.sem_x,
            reads=tb, writes=(x3bf.b,), tag=f"{tag}xbf{tau}")
        psl = B[0]
        for dg in range(16):
            xf = R["xf0" if dg % 2 == 0 else "xf1"]
            P.dma("sp", (lambda e, xf=xf, dg=dg, tcols=tcols: e.dma_start(
                out=xf[:], in_=xin[dg * 128:(dg + 1) * 128, tcols])),
                R.sem_x, reads=tb, writes=(xf.b,), tag=f"{tag}lgx{tau}.{dg}")
            P.step("pe", (lambda e, xf=xf, dg=dg: e.matmul(
                psl[0:8, 0:512], R["gwf"][:, dg * 8:(dg + 1) * 8], xf[:],
                start=(dg == 0), stop=(dg == 15))),
                reads=(xf.b, R["gwf"].b), writes=(psl.b,),
                tag=f"{tag}lg{tau}.{dg}")
        lg = R["lgs"]
        P.step("act", (lambda e: e.activation(lg[:], psl[0:8, 0:512], AF.Copy)),
               reads=(psl.b,), writes=(lg.b,), tag=f"{tag}lge{tau}")
        for b4 in range(4):
            pst = B[1]
            P.step("pe", (lambda e, b4=b4, pst=pst: e.transpose(
                pst[0:128, 0:8], lg[0:8, b4 * 128:(b4 + 1) * 128],
                R["ident_f"][0:8, 0:8])),
                reads=(lg.b, R["ident_f"].b), writes=(pst.b,),
                tag=f"{tag}tt{tau}.{b4}")
            lgt = R["lgt"]
            P.step("act", (lambda e, lgt=lgt, pst=pst: e.activation(
                lgt[:], pst[0:128, 0:8], AF.Copy)),
                reads=(pst.b,), writes=(lgt.b,), tag=f"{tag}tte{tau}.{b4}")
            m1, m2, g1 = R["m1"], R["m2"], R["g1"]
            lg2, eq1, eq2, gts = R["lg2"], R["eq1"], R["eq2"], R["gts"]
            P.step("dve", (lambda e, lgt=lgt: e.reduce_max(
                m1[:], lgt[:], axis=AX.X)),
                reads=(lgt.b,), writes=(m1.b,), tag=f"{tag}m1{tau}.{b4}")
            P.step("dve", (lambda e, lgt=lgt: e.tensor_scalar(
                eq1[:], lgt[:], m1[:], None, ALU.is_ge)),
                reads=(lgt.b, m1.b), writes=(eq1.b,), tag=f"{tag}e1{tau}.{b4}")
            P.step("dve", (lambda e, lgt=lgt: e.scalar_tensor_tensor(
                lg2[:], eq1[:], -1e30, lgt[:], op0=ALU.mult, op1=ALU.add)),
                reads=(eq1.b, lgt.b), writes=(lg2.b,), tag=f"{tag}l2{tau}.{b4}")
            P.step("dve", (lambda e: e.reduce_max(
                m2[:], lg2[:], axis=AX.X)),
                reads=(lg2.b,), writes=(m2.b,), tag=f"{tag}m2{tau}.{b4}")
            P.step("dve", (lambda e: e.tensor_scalar(
                eq2[:], lg2[:], m2[:], None, ALU.is_ge)),
                reads=(lg2.b, m2.b), writes=(eq2.b,), tag=f"{tag}e2{tau}.{b4}")
            P.step("dve", (lambda e: e.tensor_tensor(
                g1[:], m1[:], m2[:], op=ALU.subtract)),
                reads=(m1.b, m2.b), writes=(g1.b,), tag=f"{tag}d{tau}.{b4}")
            P.step("act", (lambda e: e.activation(g1[:], g1[:], AF.Sigmoid)),
                   reads=(g1.b,), writes=(g1.b,), tag=f"{tag}sg{tau}.{b4}")
            P.step("dve", (lambda e: e.scalar_tensor_tensor(
                m2[:], g1[:], -1.0, R["onecol"][:], op0=ALU.mult, op1=ALU.add)),
                reads=(g1.b, R["onecol"].b), writes=(m2.b,), tag=f"{tag}g2{tau}.{b4}")
            P.step("dve", (lambda e: e.tensor_scalar(
                eq2[:], eq2[:], m2[:], None, ALU.mult)),
                reads=(m2.b, eq2.b), writes=(eq2.b,), tag=f"{tag}q2{tau}.{b4}")
            P.step("dve", (lambda e: e.scalar_tensor_tensor(
                gts[:], eq1[:], g1[:], eq2[:], op0=ALU.mult, op1=ALU.add)),
                reads=(eq1.b, g1.b, eq2.b), writes=(gts.b,), tag=f"{tag}g{tau}.{b4}")
            pst2 = B[2]
            P.step("pe", (lambda e, pst2=pst2: e.transpose(
                pst2[0:8, 0:128], gts[:], R["ident_f"][:])),
                reads=(gts.b, R["ident_f"].b), writes=(pst2.b,),
                tag=f"{tag}tb{tau}.{b4}")
            P.step("act", (lambda e, b4=b4, pst2=pst2: e.activation(
                R["gatesT"][0:8, b4 * 128:(b4 + 1) * 128], pst2[0:8, 0:128],
                AF.Copy)),
                reads=(pst2.b,), writes=(R["gatesT"].b,), tag=f"{tag}gT{tau}.{b4}")
        for e_ in range(8):
            psr = B[1 + e_ % 2]
            P.step("pe", (lambda e, e_=e_, psr=psr: e.matmul(
                psr[:, 0:512], R["sel8"][:, e_ * 128:(e_ + 1) * 128],
                R["gatesT"][0:8, 0:512], start=True, stop=True)),
                reads=(R["gatesT"].b, R["sel8"].b), writes=(psr.b,),
                tag=f"{tag}rep{tau}.{e_}")
            gv = R["geb"][:, e_ * 512:(e_ + 1) * 512]
            P.step("act", (lambda e, gv=gv, psr=psr: e.activation(
                gv, psr[:, 0:512], AF.Copy)),
                reads=(psr.b,), writes=(R["geb"].bufs[e_],), tag=f"{tag}repe{tau}.{e_}")
        for e_ in range(8):
            for ht in range(8):
                w1c = R["w1c"]
                src = io["w1"][e_, :, ht * 128:(ht + 1) * 128].rearrange(
                    "(g p) c -> p g c", p=128)
                P.dma("pool", (lambda e, w1c=w1c, src=src: e.dma_start(
                    out=w1c[:].rearrange("p (g c) -> p g c", c=128), in_=src)),
                    R.sem_w, writes=(w1c.b,), tag=f"{tag}w1{tau}.{e_}.{ht}")
                psh = B[3 + ht % 2]
                def mmh(e, w1c=w1c, x3bf=x3bf, psh=psh):
                    for dg in range(16):
                        inst = e.matmul(psh[:, 0:512],
                                        w1c[:, dg * 128:(dg + 1) * 128],
                                        x3bf[:, dg * 512:(dg + 1) * 512],
                                        start=(dg == 0), stop=(dg == 15))
                    return inst
                P.step("pe", mmh, reads=(w1c.b, x3bf.b), writes=(psh.b,),
                       tag=f"{tag}h{tau}.{e_}.{ht}")
                hbt = R["hbt0" if ht % 2 == 0 else "hbt1"]
                P.step("act", (lambda e, hbt=hbt, psh=psh, e_=e_, ht=ht:
                               e.activation(hbt[:], psh[:, 0:512], AF.Gelu,
                                            bias=R["b1s"][:, e_ * 8 + ht:e_ * 8 + ht + 1])),
                       reads=(psh.b, R["b1s"].b), writes=(hbt.b,),
                       tag=f"{tag}ge{tau}.{e_}.{ht}")
                hidx = e_ * 8 + ht
                hv = R["harena"][:, hidx * 512:(hidx + 1) * 512]
                P.step("dve",
                       (lambda e, hbt=hbt, hv=hv, e_=e_: e.tensor_tensor(
                           hv, hbt[:], R["geb"][:, e_ * 512:(e_ + 1) * 512],
                           op=ALU.mult)),
                       reads=(hbt.b, R["geb"].bufs[e_]),
                       writes=(R["harena"].bufs[hidx],), tag=f"{tag}hg{tau}.{hidx}")
        for st in range(16):
            swc = R["w1c"]
            src = io["sw1"][:, st * 128:(st + 1) * 128].rearrange(
                "(g p) c -> p g c", p=128)
            P.dma("pool", (lambda e, swc=swc, src=src: e.dma_start(
                out=swc[:].rearrange("p (g c) -> p g c", c=128), in_=src)),
                R.sem_w, writes=(swc.b,), tag=f"{tag}sw1{tau}.{st}")
            psh = B[3 + st % 2]
            def mms(e, swc=swc, x3bf=x3bf, psh=psh):
                for dg in range(16):
                    inst = e.matmul(psh[:, 0:512],
                                    swc[:, dg * 128:(dg + 1) * 128],
                                    x3bf[:, dg * 512:(dg + 1) * 512],
                                    start=(dg == 0), stop=(dg == 15))
                return inst
            P.step("pe", mms, reads=(swc.b, x3bf.b), writes=(psh.b,),
                   tag=f"{tag}sh{tau}.{st}")
            shv = R["sharena"][:, st * 512:(st + 1) * 512]
            P.step("act", (lambda e, shv=shv, psh=psh, st=st: e.activation(
                shv, psh[:, 0:512], AF.Gelu, bias=R["sb1s"][:, st:st + 1])),
                reads=(psh.b, R["sb1s"].b), writes=(R["sharena"].bufs[st],),
                tag=f"{tag}she{tau}.{st}")
        for dt in range(16):
            w2c = R["w2c"]
            src = io["w2"][:, :, dt * 128:(dt + 1) * 128].rearrange(
                "E (g p) c -> p (E g) c", p=128)
            P.dma("pool", (lambda e, w2c=w2c, src=src: e.dma_start(
                out=w2c[:].rearrange("p (a c) -> p a c", c=128), in_=src)),
                R.sem_w, writes=(w2c.b,), tag=f"{tag}w2{tau}.{dt}")
            sw2c = R["sw2c"]
            src2 = io["sw2"][:, dt * 128:(dt + 1) * 128].rearrange(
                "(g p) c -> p g c", p=128)
            P.dma("pool", (lambda e, sw2c=sw2c, src2=src2: e.dma_start(
                out=sw2c[:].rearrange("p (g c) -> p g c", c=128), in_=src2)),
                R.sem_w, writes=(sw2c.b,), tag=f"{tag}sw2{tau}.{dt}")
            pso = B[5 + dt % 2]
            def mmo(e, w2c=w2c, sw2c=sw2c, dt=dt, pso=pso):
                for k in range(64):
                    e.matmul(pso[:, 0:512], w2c[:, k * 128:(k + 1) * 128],
                             R["harena"][:, k * 512:(k + 1) * 512],
                             start=(k == 0), stop=False)
                e.matmul(pso[:, 0:512], R["b2s"][:, dt * 128:(dt + 1) * 128],
                         R["gatesT"][0:8, 0:512], start=False, stop=False)
                for g in range(16):
                    e.matmul(pso[:, 0:512], sw2c[:, g * 128:(g + 1) * 128],
                             R["sharena"][:, g * 512:(g + 1) * 512],
                             start=False, stop=False)
                inst = e.matmul(pso[:, 0:512],
                                R["sb2s"][0:1, dt * 128:(dt + 1) * 128],
                                R["ones512"][:], start=False, stop=True)
                return inst
            P.step("pe", mmo,
                   reads=(w2c.b, sw2c.b, R["b2s"].b, R["gatesT"].b,
                          R["sb2s"].b, R["ones512"].b)
                   + tuple(R["harena"].bufs) + tuple(R["sharena"].bufs),
                   writes=(pso.b,), tag=f"{tag}o{tau}.{dt}")
            tf = R["tf0"]
            P.step("act", (lambda e, tf=tf, pso=pso: e.activation(
                tf[:], pso[:, 0:512], AF.Copy)),
                reads=(pso.b,), writes=(tf.b,), tag=f"{tag}te{tau}.{dt}")
            xf = R["xf0" if dt % 2 == 0 else "xf1"]
            P.dma("sp", (lambda e, xf=xf, dt=dt, tcols=tcols: e.dma_start(
                out=xf[:], in_=xin[dt * 128:(dt + 1) * 128, tcols])),
                R.sem_x, reads=tb, writes=(xf.b,), tag=f"{tag}xo{tau}.{dt}")
            rt = R["rt"][:, dt * 512:(dt + 1) * 512]
            P.step("dve", (lambda e, tf=tf, xf=xf, rt=rt: e.tensor_tensor(
                rt, tf[:], xf[:], op=ALU.add)),
                reads=(tf.b, xf.b), writes=(R["rt"].bufs[dt],),
                tag=f"{tag}r{tau}.{dt}")
        rt_views = [R["rt"][:, dg * 512:(dg + 1) * 512] for dg in range(16)]
        xout_blk = [xout[dg * 128:(dg + 1) * 128, tcols] for dg in range(16)]
        xo_bufs = [tuple(xout_bufs[tau * 4 + i] for i in range(4))
                   for dg in range(16)]
        banks = {"s1": B[0], "s2": B[1], "c0": B[2], "c1": B[3]}
        emit_ln_store(P, R, rt_views, R["rt"].bufs, xout_blk, xo_bufs, banks,
                      tag=f"{tag}ln{tau}.")


def alloc_moe_res(ctx, nc, R):
    R["x1bf"] = SBuf(ctx, nc, "x1bf", [128, 8192], BF16)
    R["harena"] = SBuf(ctx, nc, "harena", [128, 64 * 512], BF16, ntiles=64)
    R["sharena"] = SBuf(ctx, nc, "sharena", [128, 16 * 512], BF16, ntiles=16)
    R["geb"] = SBuf(ctx, nc, "geb", [128, 8 * 512], BF16, ntiles=8)
    R["rt"] = SBuf(ctx, nc, "rt", [128, 16 * 512], F32, ntiles=16)
    R["w1c"] = SBuf(ctx, nc, "w1c", [128, 2048], BF16)
    R["w2c"] = SBuf(ctx, nc, "w2c", [128, 8192], BF16)
    R["sw2c"] = SBuf(ctx, nc, "sw2c", [128, 2048], BF16)
    R["gwf"] = SBuf(ctx, nc, "gwf", [128, 128], F32)
    R["lgs"] = SBuf(ctx, nc, "lgs", [8, 512], F32)
    R["lgt"] = SBuf(ctx, nc, "lgt", [128, 8], F32)
    R["gatesT"] = SBuf(ctx, nc, "gatesT", [8, 512], BF16)
    for nm in ["m1", "m2", "g1"]:
        R[nm] = SBuf(ctx, nc, nm, [128, 1], F32)
    for nm in ["lg2", "eq1", "eq2"]:
        R[nm] = SBuf(ctx, nc, nm, [128, 8], F32)
    R["gts"] = SBuf(ctx, nc, "gts", [128, 8], F32)
    R["onecol"] = SBuf(ctx, nc, "onecol", [128, 1], F32)
    R["hbt0"] = SBuf(ctx, nc, "hbt0", [128, 512], BF16)
    R["hbt1"] = SBuf(ctx, nc, "hbt1", [128, 512], BF16)
    for nm in ["tf0", "xf0", "xf1"]:
        R[nm] = SBuf(ctx, nc, nm, [128, 512], F32)
    R["b1s"] = SBuf(ctx, nc, "b1s", [128, 64], F32)
    R["sb1s"] = SBuf(ctx, nc, "sb1s", [128, 16], F32)
    R["b2s"] = SBuf(ctx, nc, "b2s", [8, 2048], BF16)
    R["sb2s"] = SBuf(ctx, nc, "sb2s", [1, 2048], BF16)
    R["ones512"] = SBuf(ctx, nc, "ones512", [1, 512], BF16)
    R["ones_row_bf"] = SBuf(ctx, nc, "ones_row_bf", [1, 128], BF16)
    R["sel8"] = SBuf(ctx, nc, "sel8_s", [8, 1024], BF16)
    R["ident_f"] = SBuf(ctx, nc, "ident_f", [128, 128], F32)
    R["ident_bf"] = SBuf(ctx, nc, "ident_bf", [128, 128], BF16)
    for nm in ["ln_rb0", "ln_rq0"]:
        R[nm] = SBuf(ctx, nc, nm, [128, 512], BF16)
    for nm in ["ln_xn0", "ln_c0s", "ln_c1s"]:
        R[nm] = SBuf(ctx, nc, nm, [128, 512], F32)
    for nm in ["ln_mean", "ln_ex2", "ln_var", "ln_c0"]:
        R[nm] = SBuf(ctx, nc, nm, [1, 512], F32)


def build_moe_program(xin_name="s3", xout_name="s4"):
    nc = bass.Bass("TRN2", target_bir_lowering=False, debug=False)
    xin = nc.dram_tensor(xin_name, [D, T], F32, kind="ExternalInput").ap()
    gate_w = nc.dram_tensor("L1_ffn_gate_w", [2048, 8], F32, kind="ExternalInput").ap()
    w1 = nc.dram_tensor("L1_ffn_w1", [8, 2048, 1024], F32, kind="ExternalInput").ap()
    b1 = nc.dram_tensor("moe_b1", [128, 64], F32, kind="ExternalInput").ap()
    w2 = nc.dram_tensor("L1_ffn_w2", [8, 1024, 2048], F32, kind="ExternalInput").ap()
    b2 = nc.dram_tensor("L1_ffn_b2", [8, 2048], F32, kind="ExternalInput").ap()
    sw1 = nc.dram_tensor("L1_ffn_shared_w1", [2048, 2048], F32, kind="ExternalInput").ap()
    sb1 = nc.dram_tensor("moe_sb1", [128, 16], F32, kind="ExternalInput").ap()
    sw2 = nc.dram_tensor("L1_ffn_shared_w2", [2048, 2048], F32, kind="ExternalInput").ap()
    sb2 = nc.dram_tensor("moe_sb2", [1, 2048], F32, kind="ExternalInput").ap()
    onesc = nc.dram_tensor("ones_col", [128, 1], F32, kind="ExternalInput").ap()
    onesr = nc.dram_tensor("ones_row", [1, 128], F32, kind="ExternalInput").ap()
    ones5 = nc.dram_tensor("ones_5", [1, 512], F32, kind="ExternalInput").ap()
    cvec = nc.dram_tensor("cvec", [128, 8], F32, kind="ExternalInput").ap()
    ident = nc.dram_tensor("ident", [128, 128], F32, kind="ExternalInput").ap()
    sel8_d = nc.dram_tensor("sel8", [8, 1024], F32, kind="ExternalInput").ap()
    xout = nc.dram_tensor(xout_name, [D, T], F32, kind="ExternalOutput").ap()

    ctx = ExitStack()
    P = Prog()
    R = Res()
    for s in ("w", "x", "st", "c"):
        R["sem_" + s] = P.new_dma_sem(s)
    alloc_moe_res(ctx, nc, R)
    B = [PSum(ctx, nc, f"bank{i}", [128, 512], F32) for i in range(8)]
    R["ones_col_bf"] = SBuf(ctx, nc, "ones_col_bf", [128, 1], BF16)
    R["ones_row_f"] = SBuf(ctx, nc, "ones_row_f", [1, 128], F32)
    R["cvec"] = SBuf(ctx, nc, "cvec_s", [128, 8], F32)
    for nm, src_, q in (("ones_col_bf", onesc, "pool"), ("ones_row_f", onesr, "sp"),
                        ("ones_row_bf", onesr, "pool"), ("cvec", cvec, "sp"),

                        ("b1s", b1, "sp"), ("sb1s", sb1, "sp"),
                        ("b2s", b2, "pool"), ("sb2s", sb2, "pool"),
                        ("ones512", ones5, "pool"), ("ident_f", ident, "sp"),
                        ("ident_bf", ident, "pool"), ("onecol", onesc, "sp"),
                        ("sel8", sel8_d, "pool")):
        P.dma(q, (lambda e, nm=nm, src_=src_: e.dma_start(out=R[nm][:], in_=src_)),
              R.sem_c, writes=(R[nm].b,), tag="c_" + nm)

    P.dma("sp", (lambda e: e.dma_start(
        out=R["gwf"][:].rearrange("p (g c) -> p g c", c=8),
        in_=gate_w.rearrange("(g p) c -> p g c", p=128))),
        R.sem_c, writes=(R["gwf"].b,), tag="c_gwf")
    io = {"w1": w1, "w2": w2, "sw1": sw1, "sw2": sw2}
    xin_b = dram_bufs(16, "xin")
    xout_b = dram_bufs(16, "xout")
    emit_moe(P, R, B, nc, xin, xin_b, io, xout, xout_b)
    build_engine_sections(nc, P, ctx)
    ctx.close()
    return nc



TA = 512
NCA = TA // 128


def _rw_consts_A(P, ctx2, nc, R, dram):
    R["maa_w1"] = SBuf(ctx2, nc, "maa_w1s", [128, 16 * 160], BF16)
    R["decay_w1"] = SBuf(ctx2, nc, "decay_w1s", [128, 16 * 64], BF16)
    R["decay_w2"] = SBuf(ctx2, nc, "decay_w2s", [64, 2048], BF16)
    R["maa_w2a"] = SBuf(ctx2, nc, "maa_w2as", [96, 2048], BF16)
    R["maa_w2b"] = SBuf(ctx2, nc, "maa_w2bs", [64, 2048], BF16)
    for nm in ["maax", "maar", "maak", "maav", "maaw", "maag", "tdvec", "bg"]:
        R[nm] = SBuf(ctx2, nc, nm + "_s", [128, 16], F32)
    for nm, src_ in (("decay_w2", dram["decay_w2"]), ("maa_w2a", dram["maa_w2a"]),
                     ("maa_w2b", dram["maa_w2b"])):
        P.dma("pool", (lambda e, nm=nm, src_=src_, R=R: e.dma_start(
            out=R[nm][:], in_=src_)), R.sem_c, writes=(R[nm].b,), tag="rcA" + nm)
    P.dma("pool", (lambda e, R=R: e.dma_start(
        out=R["maa_w1"][:].rearrange("p (g c) -> p g c", c=160),
        in_=dram["maa_w1"].rearrange("(g p) c -> p g c", p=128))),
        R.sem_c, writes=(R["maa_w1"].b,), tag="rcAw1")
    P.dma("pool", (lambda e, R=R: e.dma_start(
        out=R["decay_w1"][:].rearrange("p (g c) -> p g c", c=64),
        in_=dram["decay_w1"].rearrange("(g p) c -> p g c", p=128))),
        R.sem_c, writes=(R["decay_w1"].b,), tag="rcAdw1")
    for i, nm in enumerate(["maax", "maar", "maak", "maav", "maaw", "maag",
                            "tdvec", "bg"]):
        P.dma("sp", (lambda e, nm=nm, i=i, R=R: e.dma_start(
            out=R[nm][:], in_=dram["vecs"][:, i * 16:(i + 1) * 16])),
            R.sem_c, writes=(R[nm].b,), tag="rcA" + nm)


def emit_rwkv_slot(P, nc, B, semn, dram, io, xin, xin_bufs, xout, xout_bufs,
                   eLca, eLca_buf, tag="w"):
    xin_r = xin.rearrange("(g p) c -> p g c", p=128)
    mixb = {f: dram_bufs(4, f"mx{f}") for f in range(4)}
    elab = dram_bufs(4, "elab")
    ekfb = dram_bufs(4, "ekfb")
    qdb = dram_bufs(8, "qdb")
    kdb = dram_bufs(8, "kdb")
    gb_ = dram_bufs(8, "gb")
    vb_ = dram_bufs(16, "vb")
    ygb = dram_bufs(4, "ygb")

    with ExitStack() as ctx2:
        _NAME_PREFIX[0] = "rwA_"
        RA = Res()
        RA.update(semn)
        _rw_consts_A(P, ctx2, nc, RA, dram)
        RA["x513"] = SBuf(ctx2, nc, "x513", [128, 16 * (TA + 1)], F32)
        RA["mixa"] = SBuf(ctx2, nc, "mixa", [128, 16 * TA], BF16, ntiles=16)
        RA["eLt"] = SBuf(ctx2, nc, "eLt", [128, 16 * TA], BF16, ntiles=16)
        RA["ekt"] = SBuf(ctx2, nc, "ekt", [128, 16 * TA], BF16, ntiles=16)
        RA["t5a"] = SBuf(ctx2, nc, "t5a", [96, TA], BF16)
        RA["t5b"] = SBuf(ctx2, nc, "t5b", [64, TA], BF16)
        RA["tdb"] = SBuf(ctx2, nc, "tdb", [64, TA], BF16)
        for nm in ["ptile", "satile", "etile", "untile", "epntile",
                   "uf0", "uf1", "tf0", "tf1", "dx0", "dx1"]:
            RA[nm] = SBuf(ctx2, nc, nm, [128, TA], F32)
        for tau in range(4):
            tcols = slice(tau * TA, (tau + 1) * TA)
            tb = tuple(xin_bufs[tau * 4 + i] for i in range(4))
            x513 = RA["x513"]
            x5v = x513[:].rearrange("p (g c) -> p g c", c=TA + 1)
            if tau == 0:
                P.step("dve", (lambda e, x5v=x5v: e.memset(x5v[:, :, 0:1], 0.0)),
                       writes=(x513.b,), tag=f"{tag}x0")
                P.dma("sp", (lambda e, x5v=x5v: e.dma_start(
                    out=x5v[:, :, 1:TA + 1], in_=xin_r[:, :, 0:TA])), RA.sem_x,
                    reads=tb, writes=(x513.b,), tag=f"{tag}xl{tau}")
            else:
                src = xin_r[:, :, tau * TA - 1:tau * TA + TA]
                P.dma("sp", (lambda e, x5v=x5v, src=src: e.dma_start(
                    out=x5v, in_=src)), RA.sem_x,
                    reads=tb + (xin_bufs[tau * 4 - 1],), writes=(x513.b,),
                    tag=f"{tag}xl{tau}")
            def xv(g):
                return x5v[:, g, 1:TA + 1]
            mixa = RA["mixa"]
            dxt = [RA["dx0"], RA["dx1"]]
            for g in range(16):
                dd = dxt[g % 2]
                P.step("dve", (lambda e, g=g, dd=dd: e.tensor_tensor(
                    dd[:], x5v[:, g, 0:TA], xv(g), op=ALU.subtract)),
                    reads=(x513.b,), writes=(dd.b,), tag=f"{tag}dx{tau}.{g}")
                P.step("dve", (lambda e, g=g, dd=dd: e.scalar_tensor_tensor(
                    mixa[:, g * TA:(g + 1) * TA], dd[:],
                    RA["maax"][:, g:g + 1], xv(g), op0=ALU.mult, op1=ALU.add)),
                    reads=(dd.b, RA["maax"].b, x513.b),
                    writes=(mixa.bufs[g],), tag=f"{tag}xx{tau}.{g}")
            psA6, psB7 = B[6], B[7]
            def t5mm(e):
                for g in range(16):
                    e.matmul(psA6[0:96, 0:TA],
                             RA["maa_w1"][:, g * 160:g * 160 + 96],
                             mixa[:, g * TA:(g + 1) * TA],
                             start=(g == 0), stop=(g == 15))
                for g in range(16):
                    inst = e.matmul(psB7[0:64, 0:TA],
                                    RA["maa_w1"][:, g * 160 + 96:g * 160 + 160],
                                    mixa[:, g * TA:(g + 1) * TA],
                                    start=(g == 0), stop=(g == 15))
                return inst
            P.step("pe", t5mm, reads=tuple(mixa.bufs) + (RA["maa_w1"].b,),
                   writes=(psA6.b, psB7.b), tag=f"{tag}t5{tau}")
            t5a, t5b = RA["t5a"], RA["t5b"]
            P.step("act", (lambda e: e.activation(t5a[:], psA6[0:96, 0:TA],
                                                  AF.Tanh)),
                   reads=(psA6.b,), writes=(t5a.b,), tag=f"{tag}t5a{tau}")
            P.step("act", (lambda e: e.activation(t5b[:], psB7[0:64, 0:TA],
                                                  AF.Tanh)),
                   reads=(psB7.b,), writes=(t5b.b,), tag=f"{tag}t5b{tau}")

            def emit_mix(f, maa, mtag):
                if f < 3:
                    t5src, t5base = t5a, f * 32
                    w2src, rbase = RA["maa_w2a"], f * 32
                else:
                    t5src, t5base = t5b, (f - 3) * 32
                    w2src, rbase = RA["maa_w2b"], (f - 3) * 32
                for g in range(16):
                    psm = B[6 + g % 2]
                    P.step("pe", (lambda e, g=g, psm=psm, t5src=t5src,
                                  t5base=t5base, w2src=w2src, rbase=rbase:
                                  e.matmul(psm[:, 0:TA],
                                           w2src[rbase:rbase + 32,
                                                 g * 128:(g + 1) * 128],
                                           t5src[t5base:t5base + 32, :],
                                           start=True, stop=True)),
                           reads=(w2src.b, t5src.b), writes=(psm.b,),
                           tag=f"{tag}{mtag}m{tau}.{g}")
                    uf = RA["uf0" if g % 2 == 0 else "uf1"]
                    P.step("act", (lambda e, uf=uf, psm=psm, g=g, maa=maa:
                                   e.activation(uf[:], psm[:, 0:TA],
                                                AF.Identity,
                                                bias=RA[maa][:, g:g + 1])),
                           reads=(psm.b, RA[maa].b), writes=(uf.b,),
                           tag=f"{tag}{mtag}u{tau}.{g}")
                    dd = dxt[g % 2]
                    P.step("dve", (lambda e, g=g, dd=dd: e.tensor_tensor(
                        dd[:], x5v[:, g, 0:TA], xv(g), op=ALU.subtract)),
                        reads=(x513.b,), writes=(dd.b,),
                        tag=f"{tag}{mtag}d{tau}.{g}")
                    tf = RA["tf0" if g % 2 == 0 else "tf1"]
                    P.step("dve", (lambda e, uf=uf, tf=tf, dd=dd:
                                   e.tensor_tensor(tf[:], uf[:], dd[:],
                                                   op=ALU.mult)),
                           reads=(uf.b, dd.b), writes=(tf.b,),
                           tag=f"{tag}{mtag}t{tau}.{g}")
                    P.step("dve", (lambda e, tf=tf, g=g: e.tensor_tensor(
                        mixa[:, g * TA:(g + 1) * TA], tf[:], xv(g),
                        op=ALU.add)),
                        reads=(tf.b, x513.b), writes=(mixa.bufs[g],),
                        tag=f"{tag}{mtag}x{tau}.{g}")

            def store_mix(f, tau):
                dst = dram["mix_s"][f][:, tau * TA:(tau + 1) * TA].rearrange(
                    "(g p) c -> p g c", p=128)
                P.dma("act", (lambda e, dst=dst: e.dma_start(
                    out=dst, in_=RA["mixa"][:].rearrange("p (g c) -> p g c",
                                                        c=TA))),
                    RA.sem_st, reads=tuple(RA["mixa"].bufs),
                    writes=(mixb[f][tau],), tag=f"{tag}ms{f}.{tau}")

            emit_mix(3, "maaw", "w")
            psy1 = B[0]
            def y1mm(e):
                for g in range(16):
                    inst = e.matmul(psy1[0:64, 0:TA],
                                    RA["decay_w1"][:, g * 64:(g + 1) * 64],
                                    mixa[:, g * TA:(g + 1) * TA],
                                    start=(g == 0), stop=(g == 15))
                return inst
            P.step("pe", y1mm, reads=tuple(mixa.bufs) + (RA["decay_w1"].b,),
                   writes=(psy1.b,), tag=f"{tag}y1{tau}")
            tdb = RA["tdb"]
            P.step("act", (lambda e: e.activation(tdb[:], psy1[0:64, 0:TA],
                                                  AF.Tanh)),
                   reads=(psy1.b,), writes=(tdb.b,), tag=f"{tag}td{tau}")
            for g in range(16):
                psw = B[1 + g % 2]
                P.step("pe", (lambda e, g=g, psw=psw: e.matmul(
                    psw[:, 0:TA], RA["decay_w2"][:, g * 128:(g + 1) * 128],
                    tdb[:], start=True, stop=True)),
                    reads=(tdb.b, RA["decay_w2"].b), writes=(psw.b,),
                    tag=f"{tag}wd{tau}.{g}")
                pt = RA["ptile"]
                P.step("act", (lambda e, psw=psw, g=g, pt=pt: e.activation(
                    pt[:], psw[:, 0:TA], AF.Exp, bias=RA["tdvec"][:, g:g + 1])),
                    reads=(psw.b, RA["tdvec"].b), writes=(pt.b,),
                    tag=f"{tag}p{tau}.{g}")
                sa = RA["satile"]
                for n in range(NCA):
                    P.step("dve", (lambda e, pt=pt, sa=sa, n=n:
                                   e.tensor_tensor_scan(
                        sa[:, n * 128:(n + 1) * 128],
                        pt[:, n * 128:(n + 1) * 128],
                        pt[:, n * 128:(n + 1) * 128], 0.0, ALU.add,
                        ALU.bypass)),
                        reads=(pt.b,), writes=(sa.b,),
                        tag=f"{tag}sc{tau}.{g}.{n}")
                P.step("act", (lambda e, sa=sa, g=g: e.activation(
                    RA["eLt"][:, g * TA:(g + 1) * TA], sa[:], AF.Exp,
                    scale=-1.0)),
                    reads=(sa.b,), writes=(RA["eLt"].bufs[g],),
                    tag=f"{tag}eL{tau}.{g}")
                et = RA["etile"]
                P.step("act", (lambda e, sa=sa, et=et: e.activation(
                    et[:], sa[:], AF.Exp)),
                    reads=(sa.b,), writes=(et.b,), tag=f"{tag}eS{tau}.{g}")
                epn = RA["epntile"]
                P.step("act", (lambda e, pt=pt, epn=epn: e.activation(
                    epn[:], pt[:], AF.Exp, scale=-1.0)),
                    reads=(pt.b,), writes=(epn.b,), tag=f"{tag}epn{tau}.{g}")
                un = RA["untile"]
                P.step("dve", (lambda e, epn=epn, un=un: e.tensor_single_scalar(
                    un[:], epn[:], -1.0, ALU.mult)),
                    reads=(epn.b,), writes=(un.b,), tag=f"{tag}un{tau}.{g}")
                P.step("dve", (lambda e, un=un, et=et, g=g:
                               e.scalar_tensor_tensor(
                    RA["ekt"][:, g * TA:(g + 1) * TA], un[:], 1.0, et[:],
                    op0=ALU.add, op1=ALU.mult)),
                    reads=(un.b, et.b), writes=(RA["ekt"].bufs[g],),
                    tag=f"{tag}ekf{tau}.{g}")
                P.step("act", (lambda e, sa=sa, g=g, tau=tau: e.activation(
                    eLca[:, g * 16 + tau * NCA:g * 16 + tau * NCA + NCA],
                    sa[:].rearrange("p (n c) -> p n c", c=128)[:, :, 127:128],
                    AF.Exp, scale=-1.0)),
                    reads=(sa.b,), writes=(eLca_buf,),
                    tag=f"{tag}eLc{tau}.{g}")
            for arena, scratch, bl in (("eLt", "ela_s", elab),
                                       ("ekt", "ekf_s", ekfb)):
                dst = dram[scratch][:, tau * TA:(tau + 1) * TA].rearrange(
                    "(g p) c -> p g c", p=128)
                P.dma("sp", (lambda e, dst=dst, arena=arena: e.dma_start(
                    out=dst, in_=RA[arena][:].rearrange("p (g c) -> p g c",
                                                       c=TA))),
                    RA.sem_st, reads=tuple(RA[arena].bufs),
                    writes=(bl[tau],), tag=f"{tag}es{arena}.{tau}")
            emit_mix(0, "maar", "r"); store_mix(0, tau)
            emit_mix(1, "maak", "k"); store_mix(1, tau)
            emit_mix(2, "maav", "v"); store_mix(2, tau)
            emit_mix(4, "maag", "g"); store_mix(3, tau)
        P.join_barrier()

    with ExitStack() as ctx2:
        _NAME_PREFIX[0] = "rwP_"
        RP = Res()
        RP.update(semn)
        RP["wbig"] = SBuf(ctx2, nc, "wbig", [128, 8 * 2048], BF16, ntiles=8)
        RP["mixt0"] = SBuf(ctx2, nc, "mixt0", [128, 16 * TA], BF16)
        RP["mixt1"] = SBuf(ctx2, nc, "mixt1", [128, 16 * TA], BF16)
        RP["sct0"] = SBuf(ctx2, nc, "sct0", [128, 16 * TA], BF16)
        RP["sct1"] = SBuf(ctx2, nc, "sct1", [128, 16 * TA], BF16)
        RP["oar0"] = SBuf(ctx2, nc, "oar0", [128, 8 * TA], BF16, ntiles=8)
        RP["oar1"] = SBuf(ctx2, nc, "oar1", [128, 8 * TA], BF16, ntiles=8)
        RP["bg"] = SBuf(ctx2, nc, "bg_s2", [128, 16], F32)
        P.dma("sp", (lambda e, RP=RP: e.dma_start(
            out=RP["bg"][:], in_=dram["vecs"][:, 112:128])),
            RP.sem_c, writes=(RP["bg"].b,), tag="rcPbg")

        def wload(g2, slot, wdram):
            src = wdram[:, g2 * 128:(g2 + 1) * 128].rearrange(
                "(g p) c -> p g c", p=128)
            P.dma("pool", (lambda e, slot=slot, src=src: e.dma_start(
                out=RP["wbig"][:, slot * 2048:(slot + 1) * 2048].rearrange(
                    "p (g c) -> p g c", c=128), in_=src)),
                RP.sem_w, writes=(RP["wbig"].bufs[slot],),
                tag=f"{tag}wl{g2}", stream=f"wbig{slot}")

        def proj_pass(pname, wdram, mixf, evac, scload=None):
            for half in range(2):
                for i8 in range(8):
                    wload(half * 8 + i8, i8, wdram)
                for tau in range(4):
                    mt = RP["mixt0" if tau % 2 == 0 else "mixt1"]
                    src = dram["mix_s"][mixf][:, tau * TA:(tau + 1) * TA].rearrange(
                        "(g p) c -> p g c", p=128)
                    P.dma("sp", (lambda e, mt=mt, src=src: e.dma_start(
                        out=mt[:].rearrange("p (g c) -> p g c", c=TA),
                        in_=src)), RP.sem_x, reads=(mixb[mixf][tau],),
                        writes=(mt.b,), tag=f"{tag}{pname}mx{half}.{tau}")
                    sct = None
                    if scload is not None:
                        sct = RP["sct0" if tau % 2 == 0 else "sct1"]
                        src2 = dram[scload[0]][:, tau * TA:(tau + 1) * TA].rearrange(
                            "(g p) c -> p g c", p=128)
                        P.dma("act", (lambda e, sct=sct, src2=src2: e.dma_start(
                            out=sct[:].rearrange("p (g c) -> p g c", c=TA),
                            in_=src2)), RP.sem_x, reads=(scload[1][tau],),
                            writes=(sct.b,), tag=f"{tag}{pname}sc{half}.{tau}")
                    oa = RP["oar0" if tau % 2 == 0 else "oar1"]
                    for i8 in range(8):
                        g2 = half * 8 + i8
                        ps = B[i8 % 2]
                        def mm(e, i8=i8, mt=mt, ps=ps):
                            for g in range(16):
                                inst = e.matmul(
                                    ps[:, 0:TA],
                                    RP["wbig"][:, i8 * 2048 + g * 128:
                                              i8 * 2048 + (g + 1) * 128],
                                    mt[:, g * TA:(g + 1) * TA],
                                    start=(g == 0), stop=(g == 15))
                            return inst
                        P.step("pe", mm, reads=(RP["wbig"].bufs[i8], mt.b),
                               writes=(ps.b,), tag=f"{tag}{pname}p{half}.{tau}.{i8}")
                        evac(g2, i8, tau, ps, oa, sct)
                    dst = dram[pname + "_s"][:, tau * TA:(tau + 1) * TA].rearrange(
                        "(g p) c -> p g c", p=128)[:, half * 8:(half + 1) * 8, :]
                    blist = {"qd": qdb, "kd": kdb, "g": gb_}[pname]
                    P.dma("act", (lambda e, oa=oa, dst=dst: e.dma_start(
                        out=dst, in_=oa[:].rearrange("p (g c) -> p g c",
                                                     c=TA))),
                        RP.sem_st, reads=tuple(oa.bufs),
                        writes=(blist[half * 4 + tau],),
                        tag=f"{tag}{pname}st{half}.{tau}")

        def ev_q(g2, i8, tau, ps, oa, sct):
            P.step("dve", (lambda e, i8=i8, g2=g2, ps=ps, oa=oa, sct=sct:
                           e.tensor_tensor(
                oa[:, i8 * TA:(i8 + 1) * TA],
                sct[:, g2 * TA:(g2 + 1) * TA] if False else
                sct[:].rearrange("p (g c) -> p g c", c=TA)[:, g2, :],
                ps[:, 0:TA], op=ALU.mult)),
                reads=(ps.b, sct.b), writes=(oa.bufs[i8],),
                tag=f"{tag}qd{tau}.{g2}")
        def ev_k(g2, i8, tau, ps, oa, sct):
            P.step("dve", (lambda e, i8=i8, g2=g2, ps=ps, oa=oa, sct=sct:
                           e.tensor_tensor(
                oa[:, i8 * TA:(i8 + 1) * TA],
                sct[:].rearrange("p (g c) -> p g c", c=TA)[:, g2, :],
                ps[:, 0:TA], op=ALU.mult)),
                reads=(ps.b, sct.b), writes=(oa.bufs[i8],),
                tag=f"{tag}kd{tau}.{g2}")
        def ev_g(g2, i8, tau, ps, oa, sct):
            P.step("act", (lambda e, i8=i8, g2=g2, ps=ps, oa=oa: e.activation(
                oa[:, i8 * TA:(i8 + 1) * TA], ps[:, 0:TA],
                AF.Silu, bias=RP["bg"][:, g2:g2 + 1])),
                reads=(ps.b, RP["bg"].b), writes=(oa.bufs[i8],),
                tag=f"{tag}ge{tau}.{g2}")

        proj_pass("qd", io["wq"], 0, ev_q, scload=("ela_s", elab))
        proj_pass("kd", io["wk"], 1, ev_k, scload=("ekf_s", ekfb))
        proj_pass("g", io["wg"], 3, ev_g)
        for dc in range(4):
            wvslot = RP["wbig"]
            src = io["wv"][:, dc * 512:(dc + 1) * 512].rearrange(
                "(g p) c -> p g c", p=128)
            P.dma("pool", (lambda e, dc=dc, src=src: e.dma_start(
                out=wvslot[:, (dc % 2) * 8192:(dc % 2) * 8192 + 8192].rearrange(
                    "p (g c) -> p g c", c=512), in_=src)),
                RP.sem_w, writes=tuple(wvslot.bufs[(dc % 2) * 4:(dc % 2) * 4 + 4]),
                tag=f"{tag}vw{dc}", stream=f"wbigv{dc % 2}")
            for tau in range(4):
                mt = RP["mixt0" if tau % 2 == 0 else "mixt1"]
                src2 = dram["mix_s"][2][:, tau * TA:(tau + 1) * TA].rearrange(
                    "(g p) c -> p g c", p=128)
                P.dma("sp", (lambda e, mt=mt, src2=src2: e.dma_start(
                    out=mt[:].rearrange("p (g c) -> p g c", c=TA), in_=src2)),
                    RP.sem_x, reads=(mixb[2][tau],), writes=(mt.b,),
                    tag=f"{tag}vmx{dc}.{tau}")
                oa = RP["oar0" if tau % 2 == 0 else "oar1"]
                for tk in range(NCA):
                    ps = B[2 + tk % 2]
                    def mmv(e, mt=mt, tk=tk, dc=dc, ps=ps):
                        for g in range(16):
                            inst = e.matmul(
                                ps[:, 0:512],
                                mt[:, g * TA + tk * 128:g * TA + (tk + 1) * 128],
                                RP["wbig"][:, (dc % 2) * 8192 + g * 512:
                                          (dc % 2) * 8192 + (g + 1) * 512],
                                start=(g == 0), stop=(g == 15))
                        return inst
                    P.step("pe", mmv,
                           reads=tuple(RP["wbig"].bufs[(dc % 2) * 4:(dc % 2) * 4 + 4])
                           + (mt.b,), writes=(ps.b,),
                           tag=f"{tag}v{dc}.{tau}.{tk}")
                    P.step("act", (lambda e, ps=ps, tk=tk, oa=oa:
                                   e.activation(
                        oa[:, tk * 512:(tk + 1) * 512], ps[:, 0:512],
                        AF.Copy)),
                        reads=(ps.b,), writes=(oa.bufs[tk],),
                        tag=f"{tag}ve{dc}.{tau}.{tk}")
                dst = dram["v_s"][tau * TA:(tau + 1) * TA,
                                  dc * 512:(dc + 1) * 512].rearrange(
                    "(tk p) c -> p tk c", p=128)
                P.dma("act", (lambda e, oa=oa, dst=dst: e.dma_start(
                    out=dst, in_=oa[:, 0:NCA * 512].rearrange(
                        "p (tk c) -> p tk c", c=512))),
                    RP.sem_st, reads=tuple(oa.bufs[0:NCA]),
                    writes=(vb_[dc * 4 + tau],), tag=f"{tag}vst{dc}.{tau}")
        P.join_barrier()

    with ExitStack() as ctx2:
        _NAME_PREFIX[0] = "rwB_"
        RB = Res()
        RB.update(semn)
        RB["qdT"] = SBuf(ctx2, nc, "qdT", [128, 16 * TA], BF16, ntiles=16)
        RB["kdT"] = SBuf(ctx2, nc, "kdT", [128, 16 * TA], BF16, ntiles=16)
        RB["garena"] = SBuf(ctx2, nc, "garena", [128, 16 * TA], BF16, ntiles=16)
        RB["ygarena"] = SBuf(ctx2, nc, "ygarena", [128, 16 * TA], BF16, ntiles=16)
        RB["varena"] = SBuf(ctx2, nc, "varena", [128, NCA * 2048], BF16,
                           ntiles=NCA * 4)
        RB["S_f32"] = SBuf(ctx2, nc, "S_f32", [128, 16 * 64], F32, ntiles=16)
        RB["S_bf"] = SBuf(ctx2, nc, "S_bf", [128, 16 * 64], BF16, ntiles=16)
        RB["Sodd"] = SBuf(ctx2, nc, "Sodd", [64, 16 * 64], F32)
        RB["SoddHi"] = SBuf(ctx2, nc, "SoddHi", [128, 16 * 64], F32)
        for nm in ["amT0", "amT1"]:
            RB[nm] = SBuf(ctx2, nc, nm, [128, 128], BF16)
        for nm in ["kt0", "kt1"]:
            RB[nm] = SBuf(ctx2, nc, nm, [128, 64], BF16)
        for nm in ["ke0", "ke1"]:
            RB[nm] = SBuf(ctx2, nc, nm, [128, 128], BF16)
        for nm in ["ysb0", "ysb1"]:
            RB[nm] = SBuf(ctx2, nc, nm, [128, 128], BF16)
        RB["ident64"] = SBuf(ctx2, nc, "ident64s", [128, 64], BF16)
        RB["ident_bf"] = SBuf(ctx2, nc, "identbfs", [128, 128], BF16)
        RB["maskA"] = SBuf(ctx2, nc, "maskA", [128, 128], BF16)
        for nm, src_ in (("ident64", dram["ident64"]), ("ident_bf", dram["ident"]),
                         ("maskA", dram["maskA"])):
            P.dma("pool", (lambda e, nm=nm, src_=src_, RB=RB: e.dma_start(
                out=RB[nm][:], in_=src_)), RB.sem_c, writes=(RB[nm].b,),
                tag="rcB" + nm)
        P.step("dve", (lambda e: e.memset(RB["S_f32"][:], 0.0)),
               writes=tuple(RB["S_f32"].bufs), tag=f"{tag}S0")
        P.step("dve", (lambda e: e.memset(RB["S_bf"][:], 0.0)),
               writes=tuple(RB["S_bf"].bufs), tag=f"{tag}Sb0")
        for tau in range(4):
            for arena, scr, bl in (("qdT", "qd_s", qdb), ("kdT", "kd_s", kdb),
                                   ("garena", "g_s", gb_)):
                for half in range(2):
                    src = dram[scr][:, tau * TA:(tau + 1) * TA].rearrange(
                        "(g p) c -> p g c", p=128)[:, half * 8:(half + 1) * 8, :]
                    q_ = "sp" if arena != "garena" else "act"
                    P.dma(q_, (lambda e, src=src, arena=arena, half=half:
                               e.dma_start(
                        out=RB[arena][:, half * 8 * TA:(half + 1) * 8 * TA].rearrange(
                            "p (g c) -> p g c", c=TA), in_=src)),
                        RB.sem_x, reads=(bl[half * 4 + tau],),
                        writes=tuple(RB[arena].bufs[half * 8:(half + 1) * 8]),
                        tag=f"{tag}B{arena}{tau}.{half}",
                        stream=f"{arena}{half}")
            for dc in range(4):
                src = dram["v_s"][tau * TA:(tau + 1) * TA,
                                  dc * 512:(dc + 1) * 512].rearrange(
                    "(tk p) c -> p tk c", p=128)
                P.dma("act", (lambda e, src=src, dc=dc: e.dma_start(
                    out=RB["varena"][:].rearrange("p (tk c) -> p tk c",
                                                 c=2048)[:, :, dc * 512:(dc + 1) * 512],
                    in_=src)), RB.sem_x, reads=(vb_[dc * 4 + tau],),
                    writes=tuple(RB["varena"].bufs[tk * 4 + dc]
                                 for tk in range(NCA)),
                    tag=f"{tag}Bv{tau}.{dc}", stream=f"varena{dc}")
            for n in range(NCA):
                gc = tau * NCA + n
                for hp in range(16):
                    psAT = B[0]
                    amTs = [RB["amT0"], RB["amT1"]]
                    ktoks = [RB["kt0"], RB["kt1"]]
                    for sub in range(2):
                        rb = sub * 64
                        P.step("pe", (lambda e, hp=hp, n=n, rb=rb, psAT=psAT:
                                      e.matmul(psAT[:, 0:128],
                                               RB["kdT"][rb:rb + 64,
                                                        hp * TA + n * 128:hp * TA + (n + 1) * 128],
                                               RB["qdT"][rb:rb + 64,
                                                        hp * TA + n * 128:hp * TA + (n + 1) * 128],
                                               start=True, stop=True)),
                               reads=(RB["kdT"].bufs[hp], RB["qdT"].bufs[hp]),
                               writes=(psAT.b,), tag=f"{tag}A{gc}.{hp}.{sub}")
                        amT = amTs[sub]
                        P.step("dve", (lambda e, amT=amT, psAT=psAT:
                                       e.tensor_tensor(
                            amT[:], psAT[:, 0:128], RB["maskA"][:],
                            op=ALU.mult)),
                            reads=(psAT.b, RB["maskA"].b), writes=(amT.b,),
                            tag=f"{tag}Am{gc}.{hp}.{sub}")
                        ke = RB["ke0" if sub == 0 else "ke1"]
                        P.step("dve", (lambda e, hp=hp, n=n, rb=rb, ke=ke,
                                        gc=gc: e.tensor_scalar(
                            ke[rb:rb + 64, :],
                            RB["kdT"][rb:rb + 64,
                                     hp * TA + n * 128:hp * TA + (n + 1) * 128],
                            eLca[rb:rb + 64, hp * 16 + gc:hp * 16 + gc + 1],
                            None, ALU.mult)),
                            reads=(RB["kdT"].bufs[hp], eLca_buf),
                            writes=(ke.b,), tag=f"{tag}ke{gc}.{hp}.{sub}")
                        psK = B[3 + sub]
                        P.step("pe", (lambda e, ke=ke, rb=rb, psK=psK:
                                      e.transpose(
                            psK[:, 0:32].bitcast(BF16), ke[rb:rb + 64, :],
                            RB["ident64"][rb:rb + 64, :])),
                            reads=(ke.b, RB["ident64"].b), writes=(psK.b,),
                            tag=f"{tag}kt{gc}.{hp}.{sub}")
                        P.step("act", (lambda e, sub=sub, psK=psK,
                                       ktoks=ktoks: e.activation(
                            ktoks[sub][:], psK[:, 0:32].bitcast(BF16),
                            AF.Copy)),
                            reads=(psK.b,), writes=(ktoks[sub].b,),
                            tag=f"{tag}kte{gc}.{hp}.{sub}")
                    psY = B[1 + hp % 2]
                    def mmy2(e, hp=hp, n=n, amTs=amTs, psY=psY):
                        inst = None
                        for sub in range(2):
                            rb = sub * 64
                            inst = e.matmul(psY[:, rb:rb + 64],
                                            amTs[sub][:],
                                            RB["varena"][:, n * 2048 + hp * 128 + rb:
                                                        n * 2048 + hp * 128 + rb + 64],
                                            start=(sub == 0), stop=False)
                            inst = e.matmul(psY[:, rb:rb + 64],
                                            RB["qdT"][rb:rb + 64,
                                                     hp * TA + n * 128:hp * TA + (n + 1) * 128],
                                            RB["S_bf"][rb:rb + 64,
                                                      hp * 64:(hp + 1) * 64],
                                            start=False, stop=(sub == 1))
                        return inst
                    vread = (RB["varena"].bufs[n * 4 + hp // 4],)
                    P.step("pe", mmy2,
                           reads=(amTs[0].b, amTs[1].b, RB["S_bf"].bufs[hp],
                                  RB["qdT"].bufs[hp]) + vread,
                           writes=(psY.b,), tag=f"{tag}Y{gc}.{hp}")
                    ysb = RB["ysb0" if hp % 2 == 0 else "ysb1"]
                    P.step("dve", (lambda e, psY=psY, ysb=ysb: e.tensor_copy(
                        ysb[:], psY[:, 0:128])),
                        reads=(psY.b,), writes=(ysb.b,), tag=f"{tag}ye{gc}.{hp}")
                    psT = B[6 + hp % 2]
                    P.step("pe", (lambda e, ysb=ysb, psT=psT: e.transpose(
                        psT[:, 0:64].bitcast(BF16), ysb[:], RB["ident_bf"][:])),
                        reads=(ysb.b, RB["ident_bf"].b), writes=(psT.b,),
                        tag=f"{tag}yt{gc}.{hp}")
                    P.step("dve", (lambda e, psT=psT, hp=hp, n=n:
                                   e.tensor_tensor(
                        RB["ygarena"][:, hp * TA + n * 128:hp * TA + (n + 1) * 128],
                        RB["garena"][:, hp * TA + n * 128:hp * TA + (n + 1) * 128],
                        psT[:, 0:64].bitcast(BF16), op=ALU.mult)),
                        reads=(psT.b, RB["garena"].bufs[hp]),
                        writes=(RB["ygarena"].bufs[hp],),
                        tag=f"{tag}yg{gc}.{hp}")
                    psS = B[5]
                    P.step("pe", (lambda e, ktoks=ktoks, hp=hp, n=n, psS=psS:
                                  e.matmul(psS[0:64, 0:64], ktoks[0][:],
                                           RB["varena"][:, n * 2048 + hp * 128:
                                                       n * 2048 + hp * 128 + 64],
                                           start=True, stop=True)),
                           reads=(ktoks[0].b,) + vread, writes=(psS.b,),
                           tag=f"{tag}SaA{gc}.{hp}")
                    P.step("dve", (lambda e, hp=hp, gc=gc, psS=psS:
                                   e.scalar_tensor_tensor(
                        RB["S_f32"][0:64, hp * 64:(hp + 1) * 64],
                        RB["S_f32"][0:64, hp * 64:(hp + 1) * 64],
                        eLca[0:64, hp * 16 + gc:hp * 16 + gc + 1],
                        psS[0:64, 0:64], op0=ALU.mult, op1=ALU.add)),
                        reads=(eLca_buf, psS.b, RB["S_f32"].bufs[hp]),
                        writes=(RB["S_f32"].bufs[hp],), tag=f"{tag}SuA{gc}.{hp}")
                    P.step("pe", (lambda e, ktoks=ktoks, hp=hp, n=n, psS=psS:
                                  e.matmul(psS[0:64, 64:128], ktoks[1][:],
                                           RB["varena"][:, n * 2048 + hp * 128 + 64:
                                                       n * 2048 + hp * 128 + 128],
                                           start=False, stop=True)),
                           reads=(ktoks[1].b,) + vread, writes=(psS.b,),
                           tag=f"{tag}SaB{gc}.{hp}")
                    P.step("dve", (lambda e, hp=hp, psS=psS: e.tensor_copy(
                        RB["Sodd"][0:64, hp * 64:(hp + 1) * 64],
                        psS[0:64, 64:128])),
                        reads=(psS.b,), writes=(RB["Sodd"].b,),
                        tag=f"{tag}Se{gc}.{hp}")
                P.dma("sp", (lambda e: e.dma_start(
                    out=RB["SoddHi"][64:128, :], in_=RB["Sodd"][0:64, :])),
                    RB.sem_x, reads=(RB["Sodd"].b,), writes=(RB["SoddHi"].b,),
                    tag=f"{tag}Smv{gc}")
                for hp in range(16):
                    P.step("dve", (lambda e, hp=hp, gc=gc:
                                   e.scalar_tensor_tensor(
                        RB["S_f32"][64:128, hp * 64:(hp + 1) * 64],
                        RB["S_f32"][64:128, hp * 64:(hp + 1) * 64],
                        eLca[64:128, hp * 16 + gc:hp * 16 + gc + 1],
                        RB["SoddHi"][64:128, hp * 64:(hp + 1) * 64],
                        op0=ALU.mult, op1=ALU.add)),
                        reads=(eLca_buf, RB["SoddHi"].b, RB["S_f32"].bufs[hp]),
                        writes=(RB["S_f32"].bufs[hp],), tag=f"{tag}SuB{gc}.{hp}")
                    P.step("act", (lambda e, hp=hp: e.copy(
                        RB["S_bf"][:, hp * 64:(hp + 1) * 64],
                        RB["S_f32"][:, hp * 64:(hp + 1) * 64])),
                        reads=(RB["S_f32"].bufs[hp],),
                        writes=(RB["S_bf"].bufs[hp],), tag=f"{tag}Sc{gc}.{hp}")
            dst = dram["yg_s"][:, tau * TA:(tau + 1) * TA].rearrange(
                "(g p) c -> p g c", p=128)
            P.dma("act", (lambda e, dst=dst: e.dma_start(
                out=dst, in_=RB["ygarena"][:].rearrange("p (g c) -> p g c",
                                                       c=TA))),
                RB.sem_st, reads=tuple(RB["ygarena"].bufs),
                writes=(ygb[tau],), tag=f"{tag}ygs{tau}")
        P.join_barrier()

    with ExitStack() as ctx2:
        _NAME_PREFIX[0] = "rwC_"
        RC = Res()
        RC.update(semn)
        RC["ygt"] = SBuf(ctx2, nc, "ygt", [128, 16 * TA], BF16, ntiles=16)
        RC["wst0"] = SBuf(ctx2, nc, "wst0", [128, 2048], BF16)
        RC["wst1"] = SBuf(ctx2, nc, "wst1", [128, 2048], BF16)
        RC["rt"] = SBuf(ctx2, nc, "rt", [128, 16 * 512], F32, ntiles=16)
        for nm in ["tf0", "tf1", "xf0", "xf1"]:
            RC[nm] = SBuf(ctx2, nc, nm, [128, 512], F32)
        for nm in ["ln_rb0", "ln_rq0"]:
            RC[nm] = SBuf(ctx2, nc, nm, [128, 512], BF16)
        for nm in ["ln_xn0", "ln_c0s", "ln_c1s"]:
            RC[nm] = SBuf(ctx2, nc, nm, [128, 512], F32)
        for nm in ["ln_mean", "ln_ex2", "ln_var", "ln_c0"]:
            RC[nm] = SBuf(ctx2, nc, nm, [1, 512], F32)
        RC["ones_col_bf"] = SBuf(ctx2, nc, "onescbf", [128, 1], BF16)
        RC["ones_row_f"] = SBuf(ctx2, nc, "onesrf", [1, 128], F32)
        RC["cvec"] = SBuf(ctx2, nc, "cvecs", [128, 8], F32)
        P.dma("pool", (lambda e, RC=RC: e.dma_start(
            out=RC["ones_col_bf"][:], in_=dram["onesc"])), RC.sem_c,
            writes=(RC["ones_col_bf"].b,), tag="rcCoc")
        P.dma("sp", (lambda e, RC=RC: e.dma_start(
            out=RC["ones_row_f"][:], in_=dram["onesr"])), RC.sem_c,
            writes=(RC["ones_row_f"].b,), tag="rcCor")
        P.dma("sp", (lambda e, RC=RC: e.dma_start(
            out=RC["cvec"][:], in_=dram["cvec"])), RC.sem_c,
            writes=(RC["cvec"].b,), tag="rcCcv")
        for tau in range(4):
            tcols = slice(tau * TA, (tau + 1) * TA)
            src = dram["yg_s"][:, tcols].rearrange("(g p) c -> p g c", p=128)
            P.dma("sp", (lambda e, src=src: e.dma_start(
                out=RC["ygt"][:].rearrange("p (g c) -> p g c", c=TA),
                in_=src)), RC.sem_x, reads=(ygb[tau],),
                writes=tuple(RC["ygt"].bufs), tag=f"{tag}Cyg{tau}")
            for g2 in range(16):
                wc = RC["wst0" if g2 % 2 == 0 else "wst1"]
                srcw = io["wo"][:, g2 * 128:(g2 + 1) * 128].rearrange(
                    "(g p) c -> p g c", p=128)
                P.dma("pool",
                      (lambda e, wc=wc, srcw=srcw: e.dma_start(
                    out=wc[:].rearrange("p (g c) -> p g c", c=128),
                    in_=srcw)), RC.sem_w, writes=(wc.b,),
                    tag=f"{tag}Cw{tau}.{g2}")
                pso = B[6 + g2 % 2]
                def mmo(e, wc=wc, pso=pso):
                    for g in range(16):
                        inst = e.matmul(pso[:, 0:TA],
                                        wc[:, g * 128:(g + 1) * 128],
                                        RC["ygt"][:, g * TA:(g + 1) * TA],
                                        start=(g == 0), stop=(g == 15))
                    return inst
                P.step("pe", mmo, reads=(wc.b,) + tuple(RC["ygt"].bufs),
                       writes=(pso.b,), tag=f"{tag}Co{tau}.{g2}")
                tf = RC["tf0" if g2 % 2 == 0 else "tf1"]
                P.step("act", (lambda e, tf=tf, pso=pso: e.activation(
                    tf[:], pso[:, 0:TA], AF.Copy)),
                    reads=(pso.b,), writes=(tf.b,), tag=f"{tag}Coe{tau}.{g2}")
                xf = RC["xf0" if g2 % 2 == 0 else "xf1"]
                P.dma("sp", (lambda e, xf=xf, g2=g2, tcols=tcols: e.dma_start(
                    out=xf[:], in_=xin[g2 * 128:(g2 + 1) * 128, tcols])),
                    RC.sem_x, reads=tuple(xin_bufs[tau * 4 + i]
                                         for i in range(4)),
                    writes=(xf.b,), tag=f"{tag}Cx{tau}.{g2}")
                rt = RC["rt"][:, g2 * 512:(g2 + 1) * 512]
                P.step("dve", (lambda e, tf=tf, xf=xf, rt=rt:
                                e.tensor_tensor(rt, tf[:], xf[:],
                                                op=ALU.add)),
                       reads=(tf.b, xf.b), writes=(RC["rt"].bufs[g2],),
                       tag=f"{tag}Cr{tau}.{g2}")
            rt_views = [RC["rt"][:, dg * 512:(dg + 1) * 512] for dg in range(16)]
            xout_blk = [xout[dg * 128:(dg + 1) * 128, tcols]
                        for dg in range(16)]
            xo_bufs = [tuple(xout_bufs[tau * 4 + i] for i in range(4))
                       for dg in range(16)]
            banks = {"s1": B[2], "s2": B[3], "c0": B[4], "c1": B[5]}
            emit_ln_store(P, RC, rt_views, RC["rt"].bufs, xout_blk, xo_bufs,
                          banks, tag=f"{tag}ln{tau}.", width=512)
        P.join_barrier()


def build_rwkv_program(xin_name="s0", xout_name="s1", debug=False):
    nc = bass.Bass("TRN2", target_bir_lowering=False, debug=False)
    xin = nc.dram_tensor(xin_name, [D, T], F32, kind="ExternalInput").ap()
    names = ["Wq", "Wk", "Wv", "Wg", "Wo"]
    wd = {n: nc.dram_tensor("rwkv_" + n + "T", [D, D], F32,
                            kind="ExternalInput").ap() for n in names}
    dram = {}
    dram["maa_w1"] = nc.dram_tensor("L0_attn_maa_w1", [2048, 160], F32, kind="ExternalInput").ap()
    dram["maa_w2a"] = nc.dram_tensor("rwkv_maa_w2a", [96, 2048], F32, kind="ExternalInput").ap()
    dram["maa_w2b"] = nc.dram_tensor("rwkv_maa_w2b", [64, 2048], F32, kind="ExternalInput").ap()
    dram["ident64"] = nc.dram_tensor("rwkv_ident64", [128, 64], F32, kind="ExternalInput").ap()
    dram["decay_w1"] = nc.dram_tensor("L0_attn_decay_w1", [2048, 64], F32, kind="ExternalInput").ap()
    dram["decay_w2"] = nc.dram_tensor("L0_attn_decay_w2", [64, 2048], F32, kind="ExternalInput").ap()
    dram["vecs"] = nc.dram_tensor("rwkv_vecs", [128, 128], F32, kind="ExternalInput").ap()
    dram["maskA"] = nc.dram_tensor("rwkv_maskA", [128, 128], F32, kind="ExternalInput").ap()
    dram["ident"] = nc.dram_tensor("ident", [128, 128], F32, kind="ExternalInput").ap()
    dram["onesc"] = nc.dram_tensor("ones_col", [128, 1], F32, kind="ExternalInput").ap()
    dram["onesr"] = nc.dram_tensor("ones_row", [1, 128], F32, kind="ExternalInput").ap()
    dram["cvec"] = nc.dram_tensor("cvec", [128, 8], F32, kind="ExternalInput").ap()
    xout = nc.dram_tensor(xout_name, [D, T], F32, kind="ExternalOutput").ap()
    kd_ = "ExternalOutput" if debug else "Internal"
    mix_s = [nc.dram_tensor(f"mix{f}_s", [D, T], BF16, kind=kd_).ap()
             for f in range(4)]
    dram["mix_s"] = mix_s
    for nm in ["ela_s", "ekf_s", "qd_s", "kd_s", "g_s", "yg_s"]:
        dram[nm] = nc.dram_tensor(nm, [D, T], BF16, kind=kd_).ap()
    dram["v_s"] = nc.dram_tensor("v_s", [T, D], BF16, kind=kd_).ap()

    ctx = ExitStack()
    P = Prog()
    semn = {}
    for s in ("w", "x", "st", "c"):
        semn["sem_" + s] = P.new_dma_sem(s)
    B = [PSum(ctx, nc, f"bank{i}", [128, 512], F32) for i in range(8)]
    eLca_sb = SBuf(ctx, nc, "eLca_g", [128, 256], F32)
    io = {"wq": wd["Wq"], "wk": wd["Wk"], "wv": wd["Wv"], "wg": wd["Wg"],
          "wo": wd["Wo"]}
    xin_b = dram_bufs(16, "xin")
    xout_b = dram_bufs(16, "xout")
    emit_rwkv_slot(P, nc, B, semn, dram, io, xin, xin_b, xout, xout_b,
                   eLca_sb, eLca_sb.b)
    _NAME_PREFIX[0] = ""
    build_engine_sections(nc, P, ctx)
    ctx.close()
    return nc



def build_entry_program():
    nc = bass.Bass("TRN2", target_bir_lowering=False, debug=False)
    x = nc.dram_tensor("x_nat", [T, D], F32, kind="ExternalInput").ap()
    ident_f32 = nc.dram_tensor("ident", [128, 128], F32, kind="ExternalInput").ap()
    out = nc.dram_tensor("s0", [D, T], F32, kind="ExternalOutput").ap()
    ctx = ExitStack()
    P = Prog()
    R = {}
    for nm in ["e_nat0", "e_nat1", "e_col0", "e_col1"]:
        R[nm] = SBuf(ctx, nc, nm, [128, 2048], F32)
    R["psA"] = PSum(ctx, nc, "psA", [128, 512], F32)
    R["psB"] = PSum(ctx, nc, "psB", [128, 512], F32)
    R["ident_f32"] = SBuf(ctx, nc, "ident_f32_s", [128, 128], F32)
    s_ld = P.new_dma_sem("ld")
    s_st = P.new_dma_sem("st")
    s_c = P.new_dma_sem("c")
    P.dma("sp", lambda e: e.dma_start(out=R["ident_f32"][:], in_=ident_f32),
          s_c, writes=(R["ident_f32"].b,), tag="ldid")
    emit_nat_to_fm(P, R, x, out, "e_", s_ld, s_st)
    build_engine_sections(nc, P, ctx)
    ctx.close()
    return nc


def build_exit_program():
    nc = bass.Bass("TRN2", target_bir_lowering=False, debug=False)
    xin = nc.dram_tensor("s4", [D, T], F32, kind="ExternalInput").ap()
    ident_f32 = nc.dram_tensor("ident", [128, 128], F32, kind="ExternalInput").ap()
    out = nc.dram_tensor("out_nat", [T, D], F32, kind="ExternalOutput").ap()
    ctx = ExitStack()
    P = Prog()
    R = {}
    for nm in ["x_nat0", "x_nat1", "x_col0", "x_col1"]:
        R[nm] = SBuf(ctx, nc, nm, [128, 2048], F32)
    R["psA"] = PSum(ctx, nc, "psA", [128, 512], F32)
    R["psB"] = PSum(ctx, nc, "psB", [128, 512], F32)
    R["ident_f32"] = SBuf(ctx, nc, "ident_f32_s", [128, 128], F32)
    s_ld = P.new_dma_sem("ld")
    s_st = P.new_dma_sem("st")
    s_c = P.new_dma_sem("c")
    P.dma("sp", lambda e: e.dma_start(out=R["ident_f32"][:], in_=ident_f32),
          s_c, writes=(R["ident_f32"].b,), tag="ldid")
    emit_fm_to_nat(P, R, xin, out, "x_", s_ld, s_st)
    build_engine_sections(nc, P, ctx)
    ctx.close()
    return nc



_PROG_CACHE = {}


def _get_prog(name, builder):
    if name not in _PROG_CACHE:
        _PROG_CACHE[name] = builder()
    return _PROG_CACHE[name]


def _host_consts():
    c = {}
    c["ident"] = np.eye(128, dtype=np.float32)
    c["ones_col"] = np.ones((128, 1), np.float32)
    c["ones_row"] = np.ones((1, 128), np.float32)
    c["ones_5"] = np.ones((1, 512), np.float32)
    cvec = np.zeros((128, 8), np.float32)
    cvec[:, 0] = 1e-5
    cvec[:, 1] = 1.1920929e-7
    cvec[:, 2] = -8.0
    c["cvec"] = cvec
    return c


def _prep_rwkv(inp):
    m = {}
    for n in ["Wq", "Wk", "Wv", "Wg", "Wo"]:
        m["rwkv_" + n + "T"] = np.ascontiguousarray(
            inp["L0_attn_" + n].T).astype(np.float32)
    m["L0_attn_maa_w1"] = inp["L0_attn_maa_w1"]
    w2 = inp["L0_attn_maa_w2"]
    m["rwkv_maa_w2a"] = np.ascontiguousarray(
        np.concatenate([w2[0], w2[1], w2[2]], axis=0))
    m["rwkv_maa_w2b"] = np.ascontiguousarray(
        np.concatenate([w2[3], w2[4]], axis=0))
    m["rwkv_ident64"] = np.tile(np.eye(64, dtype=np.float32), (2, 1))
    m["L0_attn_decay_w1"] = inp["L0_attn_decay_w1"]
    m["L0_attn_decay_w2"] = inp["L0_attn_decay_w2"]
    vecs = np.zeros((128, 128), np.float32)
    order = [inp["L0_attn_maa_x"].reshape(-1), inp["L0_attn_maa_r"].reshape(-1),
             inp["L0_attn_maa_k"].reshape(-1), inp["L0_attn_maa_v"].reshape(-1),
             inp["L0_attn_maa_w"].reshape(-1), inp["L0_attn_maa_g"].reshape(-1),
             inp["L0_attn_time_decay"].reshape(-1), inp["L0_attn_bg"].reshape(-1)]
    for i, a in enumerate(order):
        vecs[:, i * 16:(i + 1) * 16] = np.asarray(a, np.float32).reshape(16, 128).T
    m["rwkv_vecs"] = vecs
    m["rwkv_maskA"] = (np.arange(128)[:, None] <=
                       np.arange(128)[None, :]).astype(np.float32)
    return m


def _prep_attn(inp):
    m = {}
    m["wqT"] = np.ascontiguousarray(inp["L1_attn_Wq"].T)
    m["wkT"] = np.ascontiguousarray(inp["L1_attn_Wk"].T)
    m["wvT"] = np.ascontiguousarray(inp["L1_attn_Wv"].T)
    m["woT"] = np.ascontiguousarray(inp["L1_attn_Wo"].T)
    biases = np.zeros((128, 64), np.float32)
    biases[:, 0:16] = np.asarray(inp["L1_attn_bq"]).reshape(16, 128).T
    biases[:, 16:32] = np.asarray(inp["L1_attn_bk"]).reshape(16, 128).T
    biases[:, 32:48] = np.asarray(inp["L1_attn_bo"]).reshape(16, 128).T
    m["attn_biases"] = biases
    inv_freq = 1.0 / 10000.0 ** (np.arange(64, dtype=np.float32) / 64)
    angles = np.arange(T, dtype=np.float32)[:, None] * inv_freq[None, :]
    cos_p = np.cos(angles).astype(np.float32)
    sin_p = np.sin(angles).astype(np.float32)
    ctab = np.zeros((128, T), np.float32)
    stab = np.zeros((128, T), np.float32)
    for cch in range(128):
        pair = cch // 2
        ctab[cch] = cos_p[:, pair]
        stab[cch] = (-sin_p[:, pair]) if cch % 2 == 0 else sin_p[:, pair]
    m["rope_ctab"] = ctab
    m["rope_stab"] = stab
    swap = np.zeros((128, 128), np.float32)
    for i in range(128):
        swap[i, i ^ 1] = 1.0
    m["rope_swap"] = swap
    masks = np.zeros((128, 2048), np.float32)
    for s_ in range(4):
        masks[:, s_ * 512:(s_ + 1) * 512] = (
            np.arange(128)[:, None] + 128 * s_ <= np.arange(512)[None, :])
    m["causal_masks"] = masks
    m["bv_row"] = np.asarray(inp["L1_attn_bv"]).reshape(1, 2048)
    return m


def _prep_moe(inp):
    m = {}
    m["L1_ffn_gate_w"] = inp["L1_ffn_gate_w"]
    m["L1_ffn_w1"] = inp["L1_ffn_w1"]
    m["L1_ffn_w2"] = inp["L1_ffn_w2"]
    m["L1_ffn_b2"] = inp["L1_ffn_b2"]
    m["L1_ffn_shared_w1"] = inp["L1_ffn_shared_w1"]
    m["L1_ffn_shared_w2"] = inp["L1_ffn_shared_w2"]
    b1 = np.asarray(inp["L1_ffn_b1"])
    b1s = np.zeros((128, 64), np.float32)
    for e in range(8):
        for ht in range(8):
            b1s[:, e * 8 + ht] = b1[e, ht * 128:(ht + 1) * 128]
    m["moe_b1"] = b1s
    m["moe_sb1"] = np.asarray(inp["L1_ffn_shared_b1"]).reshape(16, 128).T.copy()
    m["moe_sb2"] = np.asarray(inp["L1_ffn_shared_b2"]).reshape(1, 2048)
    m["sel8"] = np.concatenate(
        [np.eye(8, dtype=np.float32)[:, e:e + 1] @ np.ones((1, 128), np.float32)
         for e in range(8)], axis=1)
    return m


def kernel(**inputs):
    inputs = {k: np.asarray(v) for k, v in inputs.items()}
    x_full = inputs["L0_attn_hidden_states"]
    consts = _host_consts()
    rw = _prep_rwkv(inputs)
    at = _prep_attn(inputs)
    mo = _prep_moe(inputs)
    ffn_in = {"L0_ffn_w_gate_up": inputs["L0_ffn_w_gate_up"],
              "L0_ffn_w_proj": inputs["L0_ffn_w_proj"]}

    cores = list(range(NCORES))
    nc_e = _get_prog("entry", build_entry_program)
    maps = [{"x_nat": np.ascontiguousarray(x_full[c]), "ident": consts["ident"]}
            for c in cores]
    res = run_bass_kernel_spmd(nc_e, maps, core_ids=cores)
    s0 = [r["s0"] for r in res.results]
    nc_r = _get_prog("rwkv", build_rwkv_program)
    maps = [dict(rw, s0=s0[c], ident=consts["ident"],
                 ones_col=consts["ones_col"], ones_row=consts["ones_row"],
                 cvec=consts["cvec"]) for c in cores]
    res = run_bass_kernel_spmd(nc_r, maps, core_ids=cores)
    s1 = [r["s1"] for r in res.results]
    nc_f = _get_prog("ffn", build_ffn_program)
    maps = [dict(ffn_in, s1=s1[c], ones_col=consts["ones_col"],
                 ones_row=consts["ones_row"], cvec=consts["cvec"])
            for c in cores]
    res = run_bass_kernel_spmd(nc_f, maps, core_ids=cores)
    s2 = [r["s2"] for r in res.results]
    nc_a = _get_prog("attn", build_attn_program)
    maps = [dict(at, s2=s2[c], ones_col=consts["ones_col"],
                 ones_row=consts["ones_row"], cvec=consts["cvec"])
            for c in cores]
    res = run_bass_kernel_spmd(nc_a, maps, core_ids=cores)
    s3 = [r["s3"] for r in res.results]
    nc_m = _get_prog("moe", build_moe_program)
    maps = [dict(mo, s3=s3[c], ones_col=consts["ones_col"],
                 ones_row=consts["ones_row"], ones_5=consts["ones_5"],
                 cvec=consts["cvec"], ident=consts["ident"]) for c in cores]
    res = run_bass_kernel_spmd(nc_m, maps, core_ids=cores)
    s4 = [r["s4"] for r in res.results]
    nc_x = _get_prog("exit", build_exit_program)
    maps = [{"s4": s4[c], "ident": consts["ident"]} for c in cores]
    res = run_bass_kernel_spmd(nc_x, maps, core_ids=cores)
    out = np.stack([r["out_nat"] for r in res.results], axis=0)
    return out.astype(np.float32)



def build_fused_program(debug=False):
    nc = bass.Bass("TRN2", target_bir_lowering=False, debug=False)
    x_nat = nc.dram_tensor("x_nat", [T, D], F32, kind="ExternalInput").ap()
    ident = nc.dram_tensor("ident", [128, 128], F32, kind="ExternalInput").ap()
    onesc = nc.dram_tensor("ones_col", [128, 1], F32, kind="ExternalInput").ap()
    onesr = nc.dram_tensor("ones_row", [1, 128], F32, kind="ExternalInput").ap()
    ones5 = nc.dram_tensor("ones_5", [1, 512], F32, kind="ExternalInput").ap()
    cvec = nc.dram_tensor("cvec", [128, 8], F32, kind="ExternalInput").ap()
    wd = {n: nc.dram_tensor("rwkv_" + n + "T", [D, D], F32,
                            kind="ExternalInput").ap()
          for n in ["Wq", "Wk", "Wv", "Wg", "Wo"]}
    maa_w1 = nc.dram_tensor("L0_attn_maa_w1", [2048, 160], F32, kind="ExternalInput").ap()
    maa_w2a_d = nc.dram_tensor("rwkv_maa_w2a", [96, 2048], F32, kind="ExternalInput").ap()
    maa_w2b_d = nc.dram_tensor("rwkv_maa_w2b", [64, 2048], F32, kind="ExternalInput").ap()
    ident64_d = nc.dram_tensor("rwkv_ident64", [128, 64], F32, kind="ExternalInput").ap()
    decay_w1 = nc.dram_tensor("L0_attn_decay_w1", [2048, 64], F32, kind="ExternalInput").ap()
    decay_w2 = nc.dram_tensor("L0_attn_decay_w2", [64, 2048], F32, kind="ExternalInput").ap()
    vecs = nc.dram_tensor("rwkv_vecs", [128, 128], F32, kind="ExternalInput").ap()
    maskA_d = nc.dram_tensor("rwkv_maskA", [128, 128], F32, kind="ExternalInput").ap()
    wgu = nc.dram_tensor("L0_ffn_w_gate_up", [16384, 2048], F32, kind="ExternalInput").ap()
    wproj = nc.dram_tensor("L0_ffn_w_proj", [2048, 8192], F32, kind="ExternalInput").ap()
    wq_a = nc.dram_tensor("wqT", [D, D], F32, kind="ExternalInput").ap()
    wk_a = nc.dram_tensor("wkT", [D, D], F32, kind="ExternalInput").ap()
    wv_a = nc.dram_tensor("wvT", [D, D], F32, kind="ExternalInput").ap()
    wo_a = nc.dram_tensor("woT", [D, D], F32, kind="ExternalInput").ap()
    biases = nc.dram_tensor("attn_biases", [128, 64], F32, kind="ExternalInput").ap()
    cos_d = nc.dram_tensor("rope_ctab", [128, T], F32, kind="ExternalInput").ap()
    sin_d = nc.dram_tensor("rope_stab", [128, T], F32, kind="ExternalInput").ap()
    swap_d = nc.dram_tensor("rope_swap", [128, 128], F32, kind="ExternalInput").ap()
    masks_d = nc.dram_tensor("causal_masks", [128, 2048], F32, kind="ExternalInput").ap()
    bvrow = nc.dram_tensor("bv_row", [1, 2048], F32, kind="ExternalInput").ap()
    gate_w = nc.dram_tensor("L1_ffn_gate_w", [2048, 8], F32, kind="ExternalInput").ap()
    w1 = nc.dram_tensor("L1_ffn_w1", [8, 2048, 1024], F32, kind="ExternalInput").ap()
    b1 = nc.dram_tensor("moe_b1", [128, 64], F32, kind="ExternalInput").ap()
    w2 = nc.dram_tensor("L1_ffn_w2", [8, 1024, 2048], F32, kind="ExternalInput").ap()
    b2 = nc.dram_tensor("L1_ffn_b2", [8, 2048], F32, kind="ExternalInput").ap()
    sw1 = nc.dram_tensor("L1_ffn_shared_w1", [2048, 2048], F32, kind="ExternalInput").ap()
    sb1 = nc.dram_tensor("moe_sb1", [128, 16], F32, kind="ExternalInput").ap()
    sw2 = nc.dram_tensor("L1_ffn_shared_w2", [2048, 2048], F32, kind="ExternalInput").ap()
    sb2 = nc.dram_tensor("moe_sb2", [1, 2048], F32, kind="ExternalInput").ap()
    sel8_d = nc.dram_tensor("sel8", [8, 1024], F32, kind="ExternalInput").ap()
    out_nat = nc.dram_tensor("out_nat", [T, D], F32, kind="ExternalOutput").ap()
    kd_ = "ExternalOutput" if debug else "Internal"
    s0 = nc.dram_tensor("s0", [D, T], F32, kind=kd_).ap()
    s1 = nc.dram_tensor("s1", [D, T], F32, kind=kd_).ap()
    s2 = nc.dram_tensor("s2", [D, T], F32, kind=kd_).ap()
    s3 = nc.dram_tensor("s3", [D, T], F32, kind=kd_).ap()
    s4 = nc.dram_tensor("s4", [D, T], F32, kind=kd_).ap()
    wgu_hat = nc.dram_tensor("wgu_hat", [16384, 2048], BF16).ap()
    wproj_hat = nc.dram_tensor("wproj_hat", [2048, 8192], BF16).ap()
    wguT = nc.dram_tensor("wguT", [512, 128, 512], BF16).ap()
    wprojT = nc.dram_tensor("wprojT", [256, 128, 512], BF16).ap()
    qs_s = nc.dram_tensor("qs_s", [D, T], BF16).ap()
    ks_s = nc.dram_tensor("ks_s", [D, T], BF16).ap()
    vs_s = nc.dram_tensor("vs_s", [T, D], BF16).ap()

    ctx = ExitStack()
    P = Prog()
    semn = {}
    for s in ("w", "x", "st", "c"):
        semn["sem_" + s] = P.new_dma_sem(s)
    B = [PSum(ctx, nc, f"bank{i}", [128, 512], F32) for i in range(8)]
    s0b = dram_bufs(16, "s0")
    s1b = dram_bufs(16, "s1")
    s2b = dram_bufs(16, "s2")
    s3b = dram_bufs(16, "s3")
    s4b = dram_bufs(16, "s4")

    def load_common(R):
        R.update(semn)
        R["ones_col_bf"] = SBuf(ctx2, nc, "ones_col_bf", [128, 1], BF16)
        R["ones_row_f"] = SBuf(ctx2, nc, "ones_row_f", [1, 128], F32)
        R["cvec"] = SBuf(ctx2, nc, "cvec_s", [128, 8], F32)
        P.dma("pool", lambda e: e.dma_start(out=R["ones_col_bf"][:], in_=onesc),
              R.sem_c, writes=(R["ones_col_bf"].b,), tag="c1")
        P.dma("sp", lambda e: e.dma_start(out=R["ones_row_f"][:], in_=onesr),
              R.sem_c, writes=(R["ones_row_f"].b,), tag="c2")
        P.dma("sp", lambda e: e.dma_start(out=R["cvec"][:], in_=cvec),
              R.sem_c, writes=(R["cvec"].b,), tag="c3")

    with ExitStack() as ctx2:
        _NAME_PREFIX[0] = "en_"
        R = Res()
        R.update(semn)
        for nm in ["e_nat0", "e_nat1", "e_col0", "e_col1"]:
            R[nm] = SBuf(ctx2, nc, nm, [128, 2048], F32)
        R["psA"] = B[0]
        R["psB"] = B[1]
        R["ident_f32"] = SBuf(ctx2, nc, "ident_f32_s", [128, 128], F32)
        P.dma("sp", (lambda e, R=R: e.dma_start(out=R["ident_f32"][:],
                                                in_=ident)),
              R.sem_c, writes=(R["ident_f32"].b,), tag="ldid")
        emit_nat_to_fm_b(P, R, x_nat, s0, s0b, "e_", R.sem_x, R.sem_st)
        P.join_barrier()
    rw_dram = {"maa_w1": maa_w1, "maa_w2a": maa_w2a_d, "maa_w2b": maa_w2b_d,
               "ident64": ident64_d, "decay_w1": decay_w1,
               "decay_w2": decay_w2, "vecs": vecs, "maskA": maskA_d,
               "ident": ident, "onesc": onesc, "onesr": onesr, "cvec": cvec}
    rw_dram["mix_s"] = [nc.dram_tensor(f"mix{f}_s", [D, T], BF16).ap()
                        for f in range(4)]
    for nm in ["ela_s", "ekf_s", "qd_s", "kd_s", "g_s", "yg_s"]:
        rw_dram[nm] = nc.dram_tensor(nm, [D, T], BF16).ap()
    rw_dram["v_s"] = nc.dram_tensor("v_s", [T, D], BF16).ap()
    eLca_sb = SBuf(ctx, nc, "eLca_g", [128, 256], F32)
    io_rw = {"wq": wd["Wq"], "wk": wd["Wk"], "wv": wd["Wv"], "wg": wd["Wg"],
             "wo": wd["Wo"]}
    emit_rwkv_slot(P, nc, B, semn, rw_dram, io_rw, s0, s0b, s1, s1b,
                   eLca_sb, eLca_sb.b)
    with ExitStack() as ctx2:
        _NAME_PREFIX[0] = "ff_"
        R = Res()
        alloc_ffn_res(ctx2, nc, R)
        load_common(R)
        wgu_b = dram_bufs(128, "wgu")
        wproj_b = dram_bufs(64, "wproj")
        wguh_b = dram_bufs(128, "wguh")
        wproh_b = dram_bufs(64, "wproh")
        wguT_b = dram_bufs(512, "wguT")
        wprojT_b = dram_bufs(256, "wprojT")
        emit_quant(P, R, wgu, wgu_b, wgu_hat, wguh_b, 128, 2048, "qg")
        emit_quant(P, R, wproj, wproj_b, wproj_hat, wproh_b, 16, 8192, "qp")
        emit_gu_transform(P, R, wgu_hat, wguh_b, wguT, wguT_b, "tg")
        emit_wp_transform(P, R, wproj_hat, wproh_b, wprojT, wprojT_b, "tp")
        emit_ffn(P, R, B, s1, s1b, wguT, wguT_b, wprojT, wprojT_b,
                 s2, s2b, tag="f")
        P.join_barrier()
    with ExitStack() as ctx2:
        _NAME_PREFIX[0] = "at_"
        R = Res()
        alloc_attn_res(ctx2, nc, R)
        load_common(R)
        R["ctab"] = SBuf(ctx2, nc, "ctab", [128, T], BF16)
        R["stab"] = SBuf(ctx2, nc, "stab", [128, T], BF16)
        R["swap_bf"] = SBuf(ctx2, nc, "swap_bf", [128, 128], BF16)
        R["masks"] = SBuf(ctx2, nc, "masks", [128, 2048], BF16)
        R["bq"] = SBuf(ctx2, nc, "bq", [128, 16], F32)
        R["bk"] = SBuf(ctx2, nc, "bk", [128, 16], F32)
        R["bo"] = SBuf(ctx2, nc, "bo", [128, 16], F32)
        R["bv"] = SBuf(ctx2, nc, "bv", [1, 2048], F32)
        for nm, src_ in (("ctab", cos_d), ("stab", sin_d),
                         ("swap_bf", swap_d), ("masks", masks_d)):
            P.dma("pool", (lambda e, nm=nm, src_=src_, R=R: e.dma_start(
                out=R[nm][:], in_=src_)), R.sem_c, writes=(R[nm].b,),
                tag="ac_" + nm)
        P.dma("sp", (lambda e, R=R: e.dma_start(out=R["bv"][:], in_=bvrow)),
              R.sem_c, writes=(R["bv"].b,), tag="ac_bv")
        for i, nm in enumerate(("bq", "bk", "bo")):
            P.dma("sp", (lambda e, nm=nm, i=i, R=R: e.dma_start(
                out=R[nm][:], in_=biases[:, i * 16:(i + 1) * 16])),
                R.sem_c, writes=(R[nm].b,), tag="ac_" + nm)
        io = {"wq": wq_a, "wk": wk_a, "wv": wv_a, "wo": wo_a,
              "qs_s": qs_s, "ks_s": ks_s, "vs_s": vs_s,
              "qs_bufs": dram_bufs(64, "qsx"), "ks_bufs": dram_bufs(64, "ksx"),
              "vs_bufs": dram_bufs(16, "vsx")}
        emit_attn(P, R, B, nc, s2, s2b, io, s3, s3b)
        P.join_barrier()
    with ExitStack() as ctx2:
        _NAME_PREFIX[0] = "mo_"
        R = Res()
        alloc_moe_res(ctx2, nc, R)
        load_common(R)
        for nm, src_, q in (("ones_row_bf", onesr, "pool"),
                            ("b1s", b1, "sp"), ("sb1s", sb1, "sp"),
                            ("b2s", b2, "pool"), ("sb2s", sb2, "pool"),
                            ("ones512", ones5, "pool"), ("ident_f", ident, "sp"),
                            ("ident_bf", ident, "pool"), ("onecol", onesc, "sp"),
                            ("sel8", sel8_d, "pool")):
            P.dma(q, (lambda e, nm=nm, src_=src_, R=R: e.dma_start(
                out=R[nm][:], in_=src_)), R.sem_c, writes=(R[nm].b,),
                tag="mc_" + nm)
        P.dma("sp", (lambda e, R=R: e.dma_start(
            out=R["gwf"][:].rearrange("p (g c) -> p g c", c=8),
            in_=gate_w.rearrange("(g p) c -> p g c", p=128))),
            R.sem_c, writes=(R["gwf"].b,), tag="mc_gwf")
        io = {"w1": w1, "w2": w2, "sw1": sw1, "sw2": sw2}
        emit_moe(P, R, B, nc, s3, s3b, io, s4, s4b)
        P.join_barrier()
    with ExitStack() as ctx2:
        _NAME_PREFIX[0] = "ex_"
        R = Res()
        R.update(semn)
        for nm in ["x_nat0", "x_nat1", "x_col0", "x_col1"]:
            R[nm] = SBuf(ctx2, nc, nm, [128, 2048], F32)
        R["psA"] = B[0]
        R["psB"] = B[1]
        R["ident_f32"] = SBuf(ctx2, nc, "ident_f32_s", [128, 128], F32)
        P.dma("sp", (lambda e, R=R: e.dma_start(out=R["ident_f32"][:],
                                                in_=ident)),
              R.sem_c, writes=(R["ident_f32"].b,), tag="xldid")
        emit_fm_to_nat_b(P, R, s4, s4b, out_nat, "x_", R.sem_x, R.sem_st)
    _NAME_PREFIX[0] = ""
    build_engine_sections(nc, P, ctx)
    ctx.close()
    return nc


def emit_nat_to_fm_b(P, R, src, dst, dst_bufs, pfx, sem_ld, sem_st):
    nat = [R[pfx + "nat0"], R[pfx + "nat1"]]
    col = [R[pfx + "col0"], R[pfx + "col1"]]
    ps = [R["psA"], R["psB"]]
    ident = R["ident_f32"]
    nT = T // 128
    for t in range(nT):
        nb = nat[t % 2]
        cb = col[t % 2]
        P.dma("sp", (lambda e, nb=nb, t=t: e.dma_start(
            out=nb[:], in_=src[t * 128:(t + 1) * 128, :])), sem_ld,
            writes=(nb.b,), tag=f"{pfx}ld{t}")
        for dg in range(D // 128):
            p = ps[dg % 2]
            P.step("pe", (lambda e, nb=nb, p=p, dg=dg: e.transpose(
                p[:, 0:128], nb[:, dg * 128:(dg + 1) * 128], ident[:])),
                reads=(nb.b, ident.b), writes=(p.b,), tag=f"{pfx}tr{t}.{dg}")
            P.step("dve", (lambda e, cb=cb, p=p, dg=dg: e.tensor_copy(
                cb[:, dg * 128:(dg + 1) * 128], p[:, 0:128])),
                reads=(p.b,), writes=(cb.b,), tag=f"{pfx}ev{t}.{dg}")
        dst_view = dst[:, t * 128:(t + 1) * 128].rearrange(
            "(g p) c -> p g c", p=128)
        P.dma("act", (lambda e, cb=cb, dst_view=dst_view: e.dma_start(
            out=dst_view, in_=cb[:].rearrange("p (g c) -> p g c", c=128))),
            sem_st, reads=(cb.b,), writes=(dst_bufs[t],), tag=f"{pfx}st{t}")


def emit_fm_to_nat_b(P, R, src, src_bufs, dst, pfx, sem_ld, sem_st):
    nat = [R[pfx + "nat0"], R[pfx + "nat1"]]
    col = [R[pfx + "col0"], R[pfx + "col1"]]
    ps = [R["psA"], R["psB"]]
    ident = R["ident_f32"]
    nT = T // 128
    for t in range(nT):
        cb = col[t % 2]
        nb = nat[t % 2]
        src_view = src[:, t * 128:(t + 1) * 128].rearrange(
            "(g p) c -> p g c", p=128)
        P.dma("sp", (lambda e, cb=cb, src_view=src_view: e.dma_start(
            out=cb[:].rearrange("p (g c) -> p g c", c=128), in_=src_view)),
            sem_ld, reads=(src_bufs[t],), writes=(cb.b,), tag=f"{pfx}ld{t}")
        for dg in range(D // 128):
            p = ps[dg % 2]
            P.step("pe", (lambda e, cb=cb, p=p, dg=dg: e.transpose(
                p[:, 0:128], cb[:, dg * 128:(dg + 1) * 128], ident[:])),
                reads=(cb.b, ident.b), writes=(p.b,), tag=f"{pfx}tr{t}.{dg}")
            P.step("dve", (lambda e, nb=nb, p=p, dg=dg: e.tensor_copy(
                nb[:, dg * 128:(dg + 1) * 128], p[:, 0:128])),
                reads=(p.b,), writes=(nb.b,), tag=f"{pfx}ev{t}.{dg}")
        P.dma("act", (lambda e, nb=nb, t=t: e.dma_start(
            out=dst[t * 128:(t + 1) * 128, :], in_=nb[:])),
            sem_st, reads=(nb.b,), writes=(), tag=f"{pfx}st{t}")


def kernel_fused(**inputs):
    inputs = {k: np.asarray(v) for k, v in inputs.items()}
    x_full = inputs["L0_attn_hidden_states"]
    consts = _host_consts()
    m = {}
    m.update(_prep_rwkv(inputs))
    m.update(_prep_attn(inputs))
    m.update(_prep_moe(inputs))
    m["L0_ffn_w_gate_up"] = inputs["L0_ffn_w_gate_up"]
    m["L0_ffn_w_proj"] = inputs["L0_ffn_w_proj"]
    m.update(consts)
    cores = list(range(NCORES))
    nc = _get_prog("fused", build_fused_program)
    maps = [dict(m, x_nat=np.ascontiguousarray(x_full[c])) for c in cores]
    res = run_bass_kernel_spmd(nc, maps, core_ids=cores)
    out = np.stack([r["out_nat"] for r in res.results], axis=0)
    return out.astype(np.float32)


kernel = kernel_fused
